# version 01
# baseline (speedup 1.0000x reference)
import contextlib
import numpy as np
import ml_dtypes
import concourse.bass as bass
import concourse.mybir as mybir
from concourse.bass_utils import run_bass_kernel_spmd

F32 = mybir.dt.float32
BF16 = mybir.dt.bfloat16
AF = mybir.ActivationFunctionType
ALU = mybir.AluOpType
AX = mybir.AxisListType

T = 4096
D = 1024
NT = T // 128
ND = D // 128
NC_CHUNK = 512
NCH = T // NC_CHUNK
EPS = 1e-5

BF = ml_dtypes.bfloat16


class Prog:

    CENG = ("pe", "act", "dve", "pool")
    DENG = ("sp", "act", "pool")

    def __init__(self, nc, es):
        self.nc = nc
        self.es = es
        self.eng = dict(pe=nc.tensor, act=nc.scalar, dve=nc.vector,
                        pool=nc.gpsimd, sp=nc.sync)
        self.ops = {e: [] for e in self.eng}
        self.cnt = {e: 0 for e in self.eng}
        self.dcnt = {e: 0 for e in self.eng}
        self.csem = {e: es.enter_context(nc.semaphore(f"c_{e}")) for e in self.CENG}
        self.NDSEM = 80
        self.dpool = [es.enter_context(nc.semaphore(f"dk_{i}")) for i in range(self.NDSEM)]
        self.dtotal = [0] * self.NDSEM
        self.dlastq = [None] * self.NDSEM
        self.dkey = {}
        self.dlru = list(range(self.NDSEM))
        self.maxwait = {e: {} for e in self.eng}
        self.writer = {}
        self.readers = {}
        self.carry = {}

    def _sem(self, semkey):
        kind, e = semkey
        return self.csem[e] if kind == "c" else self.dpool[e]

    def _edge(self, eng, waits, token):
        if token is None:
            return
        semkey, val = token
        if semkey[0] == "c" and semkey[1] == "pe" and eng == "pe":
            return
        got = self.maxwait[eng].get(semkey, 0)
        if val <= got:
            return
        if semkey in waits:
            waits[semkey] = max(waits[semkey], val)
        else:
            waits[semkey] = val

    def emit(self, eng, fn, reads=(), writes=(), dma=False, tag=None):
        assert not dma or eng in self.DENG, f"engine {eng} cannot DMA"
        waits = {}
        for r in reads:
            self._edge(eng, waits, self.writer.get(r))
        for w in writes:
            self._edge(eng, waits, self.writer.get(w))
            for tok in self.readers.get(w, ()):
                self._edge(eng, waits, tok)
            for tok in self.carry.pop(w, ()):
                self._edge(eng, waits, tok)
        if dma:
            assert len(writes) == 1, f"dma needs exactly one write key, got {writes}"
            key = writes[0]
            if key not in self.dkey:
                idx = self.dlru.pop(0)
                for k2, (i2, b2) in list(self.dkey.items()):
                    if i2 == idx:
                        del self.dkey[k2]
                if self.dtotal[idx] > 0:
                    self._edge(eng, waits, (("d", idx), self.dtotal[idx]))
                self.dkey[key] = (idx, self.dtotal[idx])
            idx, base = self.dkey[key]
            self.dlru.remove(idx) if idx in self.dlru else None
            self.dlru.append(idx)
            self.dtotal[idx] += 16
            self.dlastq[idx] = eng
            self.dcnt[eng] += 16
            semkey = ("d", idx)
            inc_sem, inc_val, newval = self.dpool[idx], 16, self.dtotal[idx]
        else:
            self.cnt[eng] += 1
            semkey = ("c", eng)
            inc_sem, inc_val, newval = self.csem[eng], 1, self.cnt[eng]
        token = (semkey, newval)
        for sk, v in waits.items():
            self.maxwait[eng][sk] = max(self.maxwait[eng].get(sk, 0), v)
        wl = sorted(waits.items())

        def closure(eobj, wl=wl, fn=fn, inc_sem=inc_sem, inc_val=inc_val):
            for sk, v in wl:
                eobj.wait_ge(self._sem(sk), v)
            inst = fn(eobj)
            inst.then_inc(inc_sem, inc_val)
        self.ops[eng].append(closure)
        for r in reads:
            self.readers.setdefault(r, []).append(token)
        for w in writes:
            self.writer[w] = token
            self.readers[w] = []
        return token

    def barrier(self):
        snapshot = [(("c", e), self.cnt[e]) for e in self.CENG if self.cnt[e] > 0]
        snapshot += [(("d", i), self.dtotal[i]) for i in range(self.NDSEM) if self.dtotal[i] > 0]
        for e in self.eng:
            waits = {}
            for sk, v in snapshot:
                if sk == ("c", e):
                    continue
                got = self.maxwait[e].get(sk, 0)
                if v > got:
                    waits[sk] = v
                    self.maxwait[e][sk] = v
            if not waits:
                continue
            wl = sorted(waits.items())
            if e == "pe":
                def closure(eobj, wl=wl):
                    for sk, v in wl:
                        eobj.wait_ge(self._sem(sk), v)
                    eobj.nop()
            else:
                def closure(eobj, wl=wl):
                    for sk, v in wl:
                        eobj.wait_ge(self._sem(sk), v)
            self.ops[e].append(closure)

    def finalize(self):
        for i in range(self.NDSEM):
            if self.dtotal[i] > 0:
                e = self.dlastq[i] or "sp"
                got = self.maxwait[e].get(("d", i), 0)
                if got >= self.dtotal[i]:
                    continue
                def closure(eobj, i=i, v=self.dtotal[i]):
                    eobj.wait_ge(self.dpool[i], v)
                self.ops[e].append(closure)
        nc = self.nc
        with nc.Block() as block:
            @block.tensor
            def _(eobj):
                for f in self.ops["pe"]:
                    f(eobj)
            @block.scalar
            def _(eobj):
                for f in self.ops["act"]:
                    f(eobj)
            @block.vector
            def _(eobj):
                for f in self.ops["dve"]:
                    f(eobj)
            @block.gpsimd
            def _(eobj):
                for f in self.ops["pool"]:
                    f(eobj)
            @block.sync
            def _(eobj):
                for f in self.ops["sp"]:
                    f(eobj)

    def dma(self, eng, dst, src, reads=(), writes=(), slow=False):
        assert writes, "dma requires a destination key"
        return self.emit(eng, lambda E: E.dma_start(dst, src, allow_slow_non_contiguous=slow),
                         reads=reads, writes=writes, dma=True)

    def mm(self, out, lhsT, rhs, start, stop, reads=(), writes=()):
        return self.emit("pe", lambda E: E.matmul(out, lhsT, rhs, start=start, stop=stop),
                         reads=reads, writes=writes)

    def trans(self, out, in_, ident, reads=(), writes=()):
        return self.emit("pe", lambda E: E.transpose(out, in_, ident),
                         reads=reads, writes=writes)

    def act(self, out, in_, func, bias=None, scale=None, reads=(), writes=()):
        kw = {}
        if bias is not None:
            kw["bias"] = bias
        if scale is not None:
            kw["scale"] = scale
        return self.emit("act", lambda E: E.activation(out, in_, func, **kw),
                         reads=reads, writes=writes)

    def tt(self, eng, out, a, b, op, reads=(), writes=()):
        return self.emit(eng, lambda E: E.tensor_tensor(out, a, b, op),
                         reads=reads, writes=writes)

    def ts(self, eng, out, a, s1, s2, op0, op1=None, reads=(), writes=()):
        if op1 is None:
            return self.emit(eng, lambda E: E.tensor_scalar(out, a, s1, None, op0),
                             reads=reads, writes=writes)
        return self.emit(eng, lambda E: E.tensor_scalar(out, a, s1, s2, op0, op1),
                         reads=reads, writes=writes)

    def copy(self, eng, out, in_, reads=(), writes=()):
        return self.emit(eng, lambda E: E.tensor_copy(out, in_), reads=reads, writes=writes)

    def memset(self, eng, ap, val, reads=(), writes=()):
        return self.emit(eng, lambda E: E.memset(ap, val), reads=reads, writes=writes)



def host_constants():
    c = {}
    c["ident"] = np.eye(128, dtype=np.float32)
    c["ones_row"] = np.ones((1, 512), dtype=np.float32)
    c["ones_row_bf"] = np.ones((1, 512), dtype=BF)
    c["ones_col"] = np.ones((128, 1), dtype=np.float32)
    c["ones_col_bf"] = np.ones((128, 1), dtype=BF)
    lam = 0.01 * 0.01
    p = np.arange(128)[:, None]
    z = np.arange(8192)[None, :]
    c["decay"] = np.exp(-lam * np.abs(p - (z - 3968))).astype(np.float32)
    return c


def rotary_tables():
    DIM_HEAD, WINDOW = 64, 512
    n = T
    half = np.arange(0, DIM_HEAD, 2, dtype=np.float64)
    inv_freq = 1.0 / 10000.0 ** (half / DIM_HEAD)
    t = np.arange(n, dtype=np.float64)
    freqs = t[:, None] * inv_freq[None, :]
    freqs = np.concatenate([freqs, freqs], axis=-1)
    base = (half + 0.4 * DIM_HEAD) / (1.4 * DIM_HEAD)
    power = (t - n // 2) / float(WINDOW // 2)
    scale = base[None, :] ** power[:, None]
    scale = np.concatenate([scale, scale], axis=-1)
    cos, sin = np.cos(freqs), np.sin(freqs)
    out = {
        "rot_cq": (cos * scale).T.astype(np.float32),
        "rot_sq": (sin * scale).T.astype(np.float32),
        "rot_ck": (cos / scale).T.astype(np.float32),
        "rot_sk": (sin / scale).T.astype(np.float32),
    }
    R = np.zeros((64, 64), dtype=np.float32)
    for m in range(32):
        R[m, m + 32] = -1.0
    for m in range(32, 64):
        R[m, m - 32] = 1.0
    out["rotmatT"] = R.T.astype(BF)
    p = np.arange(128)[:, None]
    cc = np.arange(128)[None, :]
    out["maskA"] = np.where(cc >= p, 0.0, -1.0e30).astype(np.float32)
    out["maskB"] = np.where(cc <= p, 0.0, -1.0e30).astype(np.float32)
    return out



class Arena:
    def __init__(self, P, name, cols, dt):
        self.P = P
        self.t = P.es.enter_context(P.nc.sbuf_tensor(name, [128, cols], dt))
        self.cols = cols
        self.free = [(0, cols, [])]
        self.name = name
        self.live = {}

    def alloc(self, tag, n, keys=None):
        keys = list(keys) if keys else [tag]
        for i, (off, ln, toks) in enumerate(self.free):
            if ln >= n:
                self.free[i] = (off + n, ln - n, list(toks))
                self.live[tag] = (off, n, keys)
                for k in keys:
                    self.P.carry.setdefault(k, []).extend(toks)
                return self.t[:, off:off + n]
        raise RuntimeError(f"arena {self.name} OOM for {tag} ({n} cols); "
                           f"free={[(o, l) for o, l, _ in self.free]}")

    def release(self, tag):
        off, n, keys = self.live.pop(tag)
        toks = []
        for k in keys:
            w = self.P.writer.pop(k, None)
            if w is not None:
                toks.append(w)
            toks.extend(self.P.readers.pop(k, ()))
            self.P.carry.pop(k, None)
        self.free.append([off, n, toks])
        self.free = [list(x) for x in self.free]
        self.free.sort(key=lambda x: x[0])
        merged = []
        for off, ln, tk in self.free:
            if merged and merged[-1][0] + merged[-1][1] == off:
                merged[-1] = [merged[-1][0], merged[-1][1] + ln, merged[-1][2] + tk]
            else:
                merged.append([off, ln, list(tk)])
        self.free = merged


def ln_stats(P, C, src_tiles, n_feat):
    nc = P.nc
    pn0, pn1 = C["pN0"], C["pN1"]
    ssum = pn0[0:1, 0:512]
    ssq = pn1[0:1, 0:512]
    ones_c = C["ones_col"]
    ones_cb = C["ones_col_bf"]
    sqt = C["lnsq"]
    first = True
    for i, (ap, key) in enumerate(src_tiles):
        p = ap.shape[0]
        oc = ones_cb if ap.dtype == BF16 else ones_c
        ock = "ones_col_bf" if ap.dtype == BF16 else "ones_col"
        P.mm(ssum, oc[0:p, 0:1], ap, start=first, stop=(i == len(src_tiles) - 1),
             reads=(key, ock), writes=("pN0",))
        P.act(sqt[0:p, :], ap, AF.Square, reads=(key,), writes=("lnsq",))
        P.mm(ssq, ones_c[0:p, 0:1], sqt[0:p, :], start=first, stop=(i == len(src_tiles) - 1),
             reads=("lnsq", "ones_col"), writes=("pN1",))
        first = False
    st = C["stat"]
    m = st[0:1, 0:512]; e = st[0:1, 512:1024]
    c2 = st[0:1, 1024:1536]; a = st[0:1, 1536:2048]
    inv = 1.0 / n_feat
    P.ts("dve", m, ssum, inv, None, ALU.mult, reads=("pN0",), writes=("st_m",))
    P.ts("dve", e, ssq, inv, None, ALU.mult, reads=("pN1",), writes=("st_e",))
    P.tt("dve", c2, m, m, ALU.mult, reads=("st_m",), writes=("st_c",))
    P.tt("dve", e, e, c2, ALU.subtract, reads=("st_e", "st_c"), writes=("st_e",))
    P.ts("dve", e, e, EPS, None, ALU.add, reads=("st_e",), writes=("st_e",))
    P.emit("dve", lambda E: E.reciprocal(c2, e), reads=("st_e",), writes=("st_c",))
    P.act(a, c2, AF.Sqrt, reads=("st_c",), writes=("st_a",))
    b = c2
    P.tt("dve", b, m, a, ALU.mult, reads=("st_m", "st_a"), writes=("st_b",))
    P.ts("dve", b, b, -1.0, None, ALU.mult, reads=("st_b",), writes=("st_b",))
    return a, b


def ln_apply(P, C, a_row, b_row, src_tiles, dst_tiles):
    ps = C["pS0"]
    ab = ps[:, 0:512]
    bb = ps[:, 512:1024]
    orow = C["ones_row"]
    P.mm(ab, orow[0:1, 0:128], a_row, start=True, stop=True, reads=("st_a", "ones_row"), writes=("pS0",))
    P.mm(bb, orow[0:1, 0:128], b_row, start=True, stop=True, reads=("st_b", "ones_row"), writes=("pS0",))
    for (sap, skey), (dap, dkey) in zip(src_tiles, dst_tiles):
        p = sap.shape[0]
        tmpk = ("lnt",)
        tmp = C["lnsq"]
        P.tt("dve", tmp[0:p, :], sap, ab[0:p, :], ALU.mult, reads=(skey, "pS0"), writes=("lnsq",))
        P.tt("dve", dap, tmp[0:p, :], bb[0:p, :], ALU.add, reads=("lnsq", "pS0"), writes=(dkey,))




ARB_COLS = 71680
ARF_COLS = 12288

def prog_setup(nc, es, const_dram):
    P = Prog(nc, es)
    A = {"ARB": Arena(P, "ARB", ARB_COLS, BF16), "ARF": Arena(P, "ARF", ARF_COLS, F32)}
    C = {}
    for name, shape, dt in [
        ("ident", [128, 128], F32), ("ones_row", [1, 512], F32),
        ("ones_row_bf", [1, 512], BF16),
        ("ones_col", [128, 1], F32), ("ones_col_bf", [128, 1], BF16),
    ]:
        t = es.enter_context(nc.sbuf_tensor("c_" + name, shape, dt))
        P.dma("sp", t[:], const_dram[name], writes=(name,))
        C[name] = t
    C["pS0"] = es.enter_context(nc.psum_tensor("pS0", [128, 1024], F32))
    C["pS1"] = es.enter_context(nc.psum_tensor("pS1", [128, 1024], F32))
    C["pN0"] = es.enter_context(nc.psum_tensor("pN0", [65, 1024], F32))
    C["pN1"] = es.enter_context(nc.psum_tensor("pN1", [65, 1024], F32))
    C["lnsq"] = es.enter_context(nc.sbuf_tensor("lnsq", [128, 512], F32))
    C["stat"] = es.enter_context(nc.sbuf_tensor("statrows", [1, 2048], F32))
    return P, A, C


def stream_rin_chunk(P, A, C, rin, rc, tc):
    tiles = []
    for dm in range(ND):
        v = rc[:, dm * 512:(dm + 1) * 512]
        P.dma("sp", v, rin[dm * 128:(dm + 1) * 128, tc * 512:(tc + 1) * 512],
              writes=(f"RC{dm}",))
        tiles.append((v, f"RC{dm}"))
    return tiles


def compute_xn(P, A, C, rin, xn_view, xn_key):
    rc = A["ARF"].alloc("RCx", ND * 512, keys=[f"RC{i}" for i in range(ND)])
    for tc in range(NCH):
        tiles = stream_rin_chunk(P, A, C, rin, rc, tc)
        a_row, b_row = ln_stats(P, C, tiles, D)
        dst = [(xn_view[:, dm * T + tc * 512: dm * T + (tc + 1) * 512], xn_key)
               for dm in range(ND)]
        ln_apply(P, C, a_row, b_row, tiles, dst)
    A["ARF"].release("RCx")


def sandwich_update(P, A, C, zc_tiles, rin, rout, tc, rc):
    rtiles = stream_rin_chunk(P, A, C, rin, rc, tc)
    a_row, b_row = ln_stats(P, C, zc_tiles, D)
    ps = C["pS0"]
    ab, bb = ps[:, 0:512], ps[:, 512:1024]
    orow = C["ones_row"]
    P.mm(ab, orow[0:1, 0:128], a_row, start=True, stop=True, reads=("st_a", "ones_row"), writes=("pS0",))
    P.mm(bb, orow[0:1, 0:128], b_row, start=True, stop=True, reads=("st_b", "ones_row"), writes=("pS0",))
    tmp = C["lnsq"]
    for dm in range(ND):
        zap, zkey = zc_tiles[dm]
        rap, rkey = rtiles[dm]
        P.tt("dve", tmp[:], zap, ab, ALU.mult, reads=(zkey, "pS0"), writes=("lnsq",))
        P.tt("dve", tmp[:], tmp[:], bb, ALU.add, reads=("lnsq", "pS0"), writes=("lnsq",))
        P.tt("dve", rap, tmp[:], rap, ALU.add, reads=("lnsq", rkey), writes=(rkey,))
        P.dma("pool", rout[dm * 128:(dm + 1) * 128, tc * 512:(tc + 1) * 512], rap,
              reads=(rkey,), writes=("rout",))


def load_cast_bf(P, A, C, w_dram, rows, cols, tag):
    nb = rows // 128
    stg = [A["ARF"].alloc(tag + f"_s{i}", cols) for i in range(2)]
    view = A["ARB"].alloc(tag, nb * cols)
    for r in range(nb):
        s = stg[r % 2]
        sk = tag + f"_s{r % 2}"
        P.dma("sp", s, w_dram[r * 128:(r + 1) * 128, :], writes=(sk,))
        P.copy("pool", view[:, r * cols:(r + 1) * cols], s, reads=(sk,), writes=(tag,))
    A["ARF"].release(tag + "_s0")
    A["ARF"].release(tag + "_s1")
    return view



def emit_stage_a(P, A, C, dram, rin, rout, scr):
    nc = P.nc
    NH = 16

    xn = A["ARB"].alloc("XN", ND * T)
    compute_xn(P, A, C, rin, xn, "XN")

    wsumT = A["ARB"].alloc("WsT", ND * D)
    stg0 = A["ARF"].alloc("wk_s", D)
    stg1 = A["ARF"].alloc("wv_s", D)
    for r in range(ND):
        P.dma("sp", stg0, dram["L0_attn_Wk"][r * 128:(r + 1) * 128, :], writes=("wk_s",))
        P.dma("sp", stg1, dram["L0_attn_Wv"][r * 128:(r + 1) * 128, :], writes=("wv_s",))
        P.tt("dve", stg0, stg0, stg1, ALU.add, reads=("wk_s", "wv_s"), writes=("wk_s",))
        for of in range(ND):
            pt = C["pS0"][:, 0:128]
            P.trans(pt, stg0[:, of * 128:(of + 1) * 128], C["ident"][:],
                    reads=("wk_s", "ident"), writes=("pS0",))
            P.act(wsumT[:, of * D + r * 128:of * D + (r + 1) * 128], pt, AF.Copy,
                  reads=("pS0",), writes=("WsT",))
    A["ARF"].release("wk_s")
    A["ARF"].release("wv_s")
    wdn = load_cast_bf(P, A, C, dram["L0_attn_W_down_kv"], D, 256, "Wdn")
    wcomb = A["ARB"].alloc("Wcomb", ND * 256)
    for i in range(ND):
        out = C["pS1"][:, 0:256]
        for m in range(ND):
            P.mm(out, wsumT[:, m * D + i * 128:m * D + (i + 1) * 128],
                 wdn[:, m * 256:(m + 1) * 256],
                 start=(m == 0), stop=(m == ND - 1), reads=("WsT", "Wdn"), writes=("pS1",))
        P.act(wcomb[:, i * 256:(i + 1) * 256], out, AF.Copy, reads=("pS1",), writes=("Wcomb",))
    A["ARB"].release("WsT")
    A["ARB"].release("Wdn")
    P.barrier()

    lat = A["ARB"].alloc("LAT", 2 * T)
    for lt in range(2):
        for tc in range(NCH):
            ps, pk = (C["pS0"], "pS0") if (lt * NCH + tc) % 2 == 0 else (C["pS1"], "pS1")
            out = ps[:, 0:512]
            for dm in range(ND):
                P.mm(out, wcomb[:, dm * 256 + lt * 128:dm * 256 + (lt + 1) * 128],
                     xn[:, dm * T + tc * 512:dm * T + (tc + 1) * 512],
                     start=(dm == 0), stop=(dm == ND - 1), reads=("Wcomb", "XN"), writes=(pk,))
            P.act(lat[:, lt * T + tc * 512:lt * T + (tc + 1) * 512], out, AF.Copy,
                  reads=(pk,), writes=("LAT",))
    A["ARB"].release("Wcomb")
    P.barrier()

    wq = load_cast_bf(P, A, C, dram["L0_attn_Wq"], D, D, "Wq")
    qstg = [A["ARB"].alloc(f"QS{i}", 512) for i in range(2)]
    for of in range(ND):
        for tc in range(NCH):
            i = of * NCH + tc
            ps, pk = (C["pS0"], "pS0") if i % 2 == 0 else (C["pS1"], "pS1")
            out = ps[:, 0:512]
            for dm in range(ND):
                P.mm(out, wq[:, dm * D + of * 128:dm * D + of * 128 + 128],
                     xn[:, dm * T + tc * 512:dm * T + (tc + 1) * 512],
                     start=(dm == 0), stop=(dm == ND - 1), reads=("Wq", "XN"), writes=(pk,))
            qs, qk = qstg[i % 2], f"QS{i % 2}"
            P.act(qs, out, AF.Copy, reads=(pk,), writes=(qk,))
            P.dma("pool", scr["Qs"][of * 128:(of + 1) * 128, tc * 512:(tc + 1) * 512],
                  qs, reads=(qk,), writes=("Qs",))
    A["ARB"].release("Wq")
    A["ARB"].release("XN")
    A["ARB"].release("QS0")
    A["ARB"].release("QS1")
    P.barrier()

    wupk = load_cast_bf(P, A, C, dram["L0_attn_W_up_k"], 256, D, "Wupk")
    wupv = load_cast_bf(P, A, C, dram["L0_attn_W_up_v"], 256, D, "Wupv")
    krec = A["ARB"].alloc("KRE", T)
    vaug = A["ARB"].alloc("VAU", NT * 65)
    qh = [A["ARB"].alloc(f"QH{i}", T) for i in range(2)]
    ohb = A["ARB"].alloc("OHB", 1024)
    pb = [A["ARB"].alloc(f"PB{i}", 1024) for i in range(2)]
    dky = A["ARF"].alloc("DKY", 8192)
    P.dma("sp", dky, dram["decay"], writes=("DKY",))
    pm = [A["ARF"].alloc(f"PM{i}", 1024) for i in range(2)]
    ncomb = A["ARF"].alloc("NCOMB", 1024, keys=["NC1"])
    nrr = A["ARF"].alloc("NRR", 1024, keys=["NRR"])
    pN0 = C["pN0"]
    for h in range(NH):
        qcur, qck = qh[h % 2], f"QH{h % 2}"
        P.dma("sp", qcur[0:64, :], scr["Qs"][h * 64:(h + 1) * 64, :], reads=("Qs",), writes=(qck,))
        for tc in range(NCH):
            out = C["pS0"][0:64, 0:512]
            for lt in range(2):
                P.mm(out, wupk[:, lt * D + h * 64:lt * D + h * 64 + 64],
                     lat[:, lt * T + tc * 512:lt * T + (tc + 1) * 512],
                     start=(lt == 0), stop=(lt == 1), reads=("Wupk", "LAT"), writes=("pS0",))
            P.act(krec[0:64, tc * 512:(tc + 1) * 512], out, AF.Copy, reads=("pS0",), writes=("KRE",))
        P.memset("pool", vaug[:], 1.0, writes=("VAU",))
        for tt in range(NT):
            out = C["pS1"][0:128, 0:64]
            for lt in range(2):
                P.mm(out, lat[:, lt * T + tt * 128:lt * T + (tt + 1) * 128],
                     wupv[:, lt * D + h * 64:lt * D + h * 64 + 64],
                     start=(lt == 0), stop=(lt == 1), reads=("Wupv", "LAT"), writes=("pS1",))
            P.act(vaug[:, tt * 65:tt * 65 + 64], out, AF.Copy, reads=("pS1",), writes=("VAU",))
        for qcb in range(4):
            for kt in range(NT):
                sps = "pS0" if kt % 2 == 0 else "pS1"
                S = C[sps]
                pmb, pmk = pm[kt % 2], f"PM{kt % 2}"
                pP, pPk = pb[kt % 2], f"PB{kt % 2}"
                for half in range(2):
                    P.mm(S[:, half * 512:(half + 1) * 512],
                         krec[0:64, kt * 128:(kt + 1) * 128],
                         qcur[0:64, qcb * 1024 + half * 512:qcb * 1024 + (half + 1) * 512],
                         start=True, stop=True, reads=("KRE", qck), writes=(sps,))
                doff = qcb * 1024 - 128 * kt + 3968
                P.tt("dve", pmb, S[:], dky[:, doff:doff + 1024], ALU.mult,
                     reads=(sps, "DKY"), writes=(pmk,))
                P.act(pP, pmb, AF.Exp, scale=1.0 / 32.0, reads=(pmk,), writes=(pPk,))
                for half in range(2):
                    sl = slice(half * 512, (half + 1) * 512)
                    P.mm(pN0[:, sl], vaug[:, kt * 65:kt * 65 + 65], pP[:, sl],
                         start=(kt == 0), stop=(kt == NT - 1), reads=("VAU", pPk), writes=("pN0",))
            t1 = ncomb[0:65, 0:1024]
            P.copy("dve", t1, pN0[:], reads=("pN0",), writes=("NC1",))
            rr = nrr[0:1, 0:1024]
            P.emit("dve", lambda E, rr=rr, t1=t1: E.reciprocal(rr, t1[64:65, :]),
                   reads=("NC1",), writes=("NRR",))
            bc = C["pS0"][0:64, 0:1024]
            for half in range(2):
                P.mm(bc[:, half * 512:(half + 1) * 512], C["ones_row"][0:1, 0:64],
                     rr[:, half * 512:(half + 1) * 512], start=True, stop=True,
                     reads=("NRR", "ones_row"), writes=("pS0",))
            P.tt("dve", ohb[0:64, :], t1[0:64, :], bc, ALU.mult, reads=("NC1", "pS0"), writes=("OHB",))
            P.dma("pool", scr["AOs"][h * 64:(h + 1) * 64, qcb * 1024:(qcb + 1) * 1024],
                  ohb[0:64, :], reads=("OHB",), writes=("AOs",))
    for nm in ["KRE", "VAU", "QH0", "QH1", "OHB", "PB0", "PB1", "LAT",
               "Wupk", "Wupv"]:
        A["ARB"].release(nm)
    A["ARF"].release("NCOMB")
    A["ARF"].release("NRR")
    A["ARF"].release("DKY")
    A["ARF"].release("PM0")
    A["ARF"].release("PM1")
    P.barrier()

    wo = load_cast_bf(P, A, C, dram["L0_attn_Wo"], D, D, "Wo")
    aostg = A["ARB"].alloc("AOC", ND * 512, keys=[f"AOC{i}" for i in range(ND)])
    zc = A["ARF"].alloc("ZC", ND * 512, keys=[f"ZC{i}" for i in range(ND)])
    rc = A["ARF"].alloc("RCs", ND * 512, keys=[f"RC{i}" for i in range(ND)])
    for tc in range(NCH):
        for dm in range(ND):
            P.dma("sp", aostg[:, dm * 512:(dm + 1) * 512],
                  scr["AOs"][dm * 128:(dm + 1) * 128, tc * 512:(tc + 1) * 512],
                  reads=("AOs",), writes=(f"AOC{dm}",))
        ztiles = []
        for of in range(ND):
            ps, pk = (C["pS1"], "pS1")
            out = ps[:, 0:512]
            for dm in range(ND):
                P.mm(out, wo[:, dm * D + of * 128:dm * D + of * 128 + 128],
                     aostg[:, dm * 512:(dm + 1) * 512],
                     start=(dm == 0), stop=(dm == ND - 1), reads=("Wo", f"AOC{dm}"), writes=(pk,))
            zv = zc[:, of * 512:(of + 1) * 512]
            P.act(zv, out, AF.Copy, reads=(pk,), writes=(f"ZC{of}",))
            ztiles.append((zv, f"ZC{of}"))
        sandwich_update(P, A, C, ztiles, rin, rout, tc, rc)
    A["ARB"].release("Wo")
    A["ARB"].release("AOC")
    A["ARF"].release("ZC")
    A["ARF"].release("RCs")
    P.barrier()


def xs_views(P, A, xi, xe, dm, cols):
    c0, ln = cols
    if dm < 4:
        return xi[:, dm * T + c0:dm * T + c0 + ln], "XI"
    if c0 == 0:
        return xe[:, (dm - 4) * 512:(dm - 4) * 512 + ln], "XE"
    return xi[:, dm * T + c0 - 1:dm * T + c0 - 1 + ln], "XI"


def build_xe(P, A, C, xi, xe):
    for dm in range(4, 8):
        sl = xe[:, (dm - 4) * 512:(dm - 3) * 512]
        P.memset("dve", sl[:, 0:1], 0.0, writes=("XE",))
        P.copy("dve", sl[:, 1:512], xi[:, dm * T:dm * T + 511], reads=("XI",), writes=("XE",))


def emit_stage_b(P, A, C, dram, rin, rout, scr):
    nc = P.nc
    NH, DH, WIN = 8, 64, 512
    HID = 2730
    NJ = 22
    for nm in ["maskA", "maskB"]:
        t = P.es.enter_context(nc.sbuf_tensor("b_" + nm, [128, 128], F32))
        P.dma("sp", t[:], dram[nm], writes=(nm,))
        C[nm] = t
    rott = {}
    t = P.es.enter_context(nc.sbuf_tensor("b_rotmatT", [64, 64], BF16))
    P.dma("sp", t[:], dram["rotmatT"], writes=("rotmatT",))
    rott["rotmatT"] = t

    for l in range(2):
        stg = A["ARF"].alloc("w1s", 5460)
        stgb = A["ARB"].alloc("w1sb", 5460)
        for r in range(ND):
            P.dma("sp", stg, dram["L0_ffn_w1"][l][r * 128:(r + 1) * 128, :], writes=("w1s",))
            P.copy("pool", stgb, stg, reads=("w1s",), writes=("w1sb",))
            P.dma("pool", scr["w1bf"][l][r * 128:(r + 1) * 128, :], stgb,
                  reads=("w1sb",), writes=("w1bf",))
        A["ARF"].release("w1s")
        A["ARB"].release("w1sb")
        stg = A["ARF"].alloc("w2s", 1024)
        stgb = A["ARB"].alloc("w2sb", 1024)
        for r in range(NJ):
            p = min(128, HID - r * 128)
            P.dma("sp", stg[0:p, :], dram["L0_ffn_w2"][l][r * 128:r * 128 + p, :], writes=("w2s",))
            P.copy("pool", stgb[0:p, :], stg[0:p, :], reads=("w2s",), writes=("w2sb",))
            P.dma("pool", scr["w2bf"][l][r * 128:r * 128 + p, :], stgb[0:p, :],
                  reads=("w2sb",), writes=("w2bf",))
        A["ARF"].release("w2s")
        A["ARB"].release("w2sb")

    xi = A["ARB"].alloc("XI", ND * T)
    compute_xn(P, A, C, rin, xi, "XI")
    xe = A["ARB"].alloc("XE", 4 * 512)
    wst = A["ARF"].alloc("WST", 64)
    pmb = A["ARF"].alloc("PMB", 640)
    pef = A["ARF"].alloc("PEF", 640)
    sqb = A["ARF"].alloc("SQB", 512)
    nrm = A["ARF"].alloc("NRM", 512)
    ncomb = A["ARF"].alloc("NCB", 128, keys=["NC1"])
    nrr = A["ARF"].alloc("NRR", 128, keys=["NRR"])
    qsc = P.es.enter_context(nc.sbuf_tensor("b_qsc", [64, 2], F32))
    ksc = P.es.enter_context(nc.sbuf_tensor("b_ksc", [64, 2], F32))
    for l in range(2):
        P.dma("sp", qsc[:, l:l + 1], dram["L0_ffn_q_scale"][l].unsqueeze(1), writes=("qsc",), slow=True)
        P.dma("sp", ksc[:, l:l + 1], dram["L0_ffn_k_scale"][l].unsqueeze(1), writes=("ksc",), slow=True)

    pS0, pS1, pN0, pN1 = C["pS0"], C["pS1"], C["pN0"], C["pN1"]
    for l in range(2):
        build_xe(P, A, C, xi, xe)
        for nm in ["rot_cq", "rot_sq", "rot_ck", "rot_sk"]:
            v = A["ARB"].alloc(nm, T)
            P.dma("sp", v[0:64, :], dram[nm], writes=(nm,))
            rott[nm] = v
        whb = [A["ARB"].alloc(f"WHB{i}", 24 * 64) for i in range(2)]
        qraw = A["ARB"].alloc("QRAW", 512)
        kraw = A["ARB"].alloc("KRAW", 512)
        qfin = A["ARB"].alloc("QFIN", T)
        kfin = A["ARB"].alloc("KFIN", T)
        qtmp = A["ARB"].alloc("QTMP", 512)
        vaug = A["ARB"].alloc("VAU", NT * 65)
        ptb = A["ARB"].alloc("PTB", 2 * 128, keys=["PTB0", "PTB1"])
        ohb = A["ARB"].alloc("OHB", 128)
        for h in range(NH):
            wcur, wck = whb[h % 2], f"WHB{h % 2}"
            for gi, colbase in enumerate((h * 64, 512 + h * 64, 1024 + h * 64)):
                for dm in range(ND):
                    P.dma("sp", wst, dram["L0_ffn_wqkv"][l][dm * 128:(dm + 1) * 128,
                          colbase:colbase + 64], writes=("WST",))
                    P.copy("pool", wcur[:, (gi * 8 + dm) * 64:(gi * 8 + dm + 1) * 64],
                           wst, reads=("WST",), writes=(wck,))
            P.memset("pool", vaug[:], 1.0, writes=("VAU",))
            for tt in range(NT):
                out = pS1[0:128, 0:64]
                for dm in range(ND):
                    xv, xk = xs_views(P, A, xi, xe, dm, (tt * 128, 128))
                    P.mm(out, xv, wcur[:, (16 + dm) * 64:(16 + dm + 1) * 64],
                         start=(dm == 0), stop=(dm == ND - 1), reads=(xk, wck), writes=("pS1",))
                P.act(vaug[:, tt * 65:tt * 65 + 64], out, AF.Copy, reads=("pS1",), writes=("VAU",))
            for gi, raw, rk, fin, fk, sc, sck, cn, sn in (
                    (0, qraw, "QRAW", qfin, "QFIN", qsc, "qsc", "rot_cq", "rot_sq"),
                    (1, kraw, "KRAW", kfin, "KFIN", ksc, "ksc", "rot_ck", "rot_sk")):
                for tc in range(NCH):
                    sl = slice(tc * 512, (tc + 1) * 512)
                    out = pN1[0:64, 0:512]
                    for dm in range(ND):
                        xv, xk = xs_views(P, A, xi, xe, dm, (tc * 512, 512))
                        P.mm(out, wcur[:, (gi * 8 + dm) * 64:(gi * 8 + dm + 1) * 64], xv,
                             start=(dm == 0), stop=(dm == ND - 1), reads=(wck, xk), writes=("pN1",))
                    rsl = raw[0:64, 0:512]
                    P.act(rsl, out, AF.Copy, reads=("pN1",), writes=(rk,))
                    P.act(sqb[0:64, :], rsl, AF.Square, reads=(rk,), writes=("SQB",))
                    P.mm(pN1[0:1, 0:512], C["ones_col"][0:64, 0:1], sqb[0:64, :],
                         start=True, stop=True, reads=("SQB", "ones_col"), writes=("pN1",))
                    r0 = nrm[0:1, 0:512]
                    P.act(r0, pN1[0:1, 0:512], AF.Sqrt, reads=("pN1",), writes=("NRM",))
                    P.ts("dve", r0, r0, 1e-12, None, ALU.max, reads=("NRM",), writes=("NRM",))
                    P.emit("dve", lambda E, r0=r0: E.reciprocal(r0, r0), reads=("NRM",), writes=("NRM",))
                    bc = pN1[0:64, 0:512]
                    P.mm(bc, C["ones_row"][0:1, 0:64], r0, start=True, stop=True,
                         reads=("NRM", "ones_row"), writes=("pN1",))
                    P.tt("dve", qtmp[0:64, :], rsl, bc, ALU.mult, reads=(rk, "pN1"), writes=("QTMP",))
                    P.ts("dve", qtmp[0:64, :], qtmp[0:64, :], sc[:, l:l + 1], None, ALU.mult,
                         reads=("QTMP", sck), writes=("QTMP",))
                    rot = pN1[0:64, 0:512]
                    P.mm(rot, rott["rotmatT"][:], qtmp[0:64, :], start=True, stop=True,
                         reads=("QTMP", "rotmatT"), writes=("pN1",))
                    P.tt("dve", sqb[0:64, :], qtmp[0:64, :], rott[cn][0:64, sl], ALU.mult,
                         reads=("QTMP", cn), writes=("SQB",))
                    P.tt("dve", pmb[0:64, 0:512], rot, rott[sn][0:64, sl], ALU.mult,
                         reads=("pN1", sn), writes=("PMB",))
                    P.tt("dve", fin[0:64, sl], sqb[0:64, :], pmb[0:64, 0:512], ALU.add,
                         reads=("SQB", "PMB"), writes=(fk,))
            for w in range(8):
                for it in range(4):
                    jlo = it if w > 0 else max(it, 4)
                    jhi = it + 4
                    njc = jhi - jlo + 1
                    irng = slice(w * 512 + it * 128, w * 512 + (it + 1) * 128)
                    for jc in range(jlo, jhi + 1):
                        jsl = slice((jc - jlo) * 128, (jc - jlo + 1) * 128)
                        gtok = 512 * (w - 1) + 128 * jc
                        P.mm(pS0[:, jsl], qfin[0:64, irng],
                             kfin[0:64, gtok:gtok + 128],
                             start=True, stop=True, reads=("QFIN", "KFIN"), writes=("pS0",))
                        if jc == it:
                            P.tt("dve", pmb[:, jsl], pS0[:, jsl], C["maskA"][:], ALU.add,
                                 reads=("pS0", "maskA"), writes=("PMB",))
                        elif jc == jhi:
                            P.tt("dve", pmb[:, jsl], pS0[:, jsl], C["maskB"][:], ALU.add,
                                 reads=("pS0", "maskB"), writes=("PMB",))
                        else:
                            P.copy("dve", pmb[:, jsl], pS0[:, jsl], reads=("pS0",), writes=("PMB",))
                    wid = njc * 128
                    mrow = nrm[0:128, 0:1]
                    P.emit("dve", lambda E, mrow=mrow, w_=wid: E.reduce_max(
                        mrow, pmb[:, 0:w_], axis=AX.X), reads=("PMB",), writes=("NRM",))
                    P.ts("dve", mrow, mrow, -8.0, None, ALU.mult, reads=("NRM",), writes=("NRM",))
                    P.act(pef[:, 0:wid], pmb[:, 0:wid], AF.Exp, bias=mrow, scale=8.0,
                          reads=("PMB", "NRM"), writes=("PEF",))
                    for jc in range(jlo, jhi + 1):
                        jsl = slice((jc - jlo) * 128, (jc - jlo + 1) * 128)
                        pt = pS1[:, 0:128]
                        P.trans(pt, pef[:, jsl], C["ident"][:], reads=("PEF", "ident"), writes=("pS1",))
                        slot = (jc % 2)
                        P.act(ptb[:, slot * 128:(slot + 1) * 128], pt, AF.Copy,
                              reads=("pS1",), writes=(f"PTB{slot}",))
                        tglob = 4 * (w - 1) + jc
                        P.mm(pN0[0:65, 0:128], vaug[:, tglob * 65:tglob * 65 + 65],
                             ptb[:, slot * 128:(slot + 1) * 128],
                             start=(jc == jlo), stop=(jc == jhi),
                             reads=("VAU", f"PTB{slot}"), writes=("pN0",))
                    t1 = ncomb[0:65, 0:128]
                    P.copy("dve", t1, pN0[0:65, 0:128], reads=("pN0",), writes=("NC1",))
                    rr = nrr[0:1, 0:128]
                    P.emit("dve", lambda E, rr=rr, t1=t1: E.reciprocal(rr, t1[64:65, :]),
                           reads=("NC1",), writes=("NRR",))
                    bc2 = pN1[0:64, 0:128]
                    P.mm(bc2, C["ones_row"][0:1, 0:64], rr, start=True, stop=True,
                         reads=("NRR", "ones_row"), writes=("pN1",))
                    P.tt("dve", ohb[0:64, :], t1[0:64, :], bc2, ALU.mult,
                         reads=("NC1", "pN1"), writes=("OHB",))
                    P.dma("pool", scr["AOs"][h * 64:(h + 1) * 64,
                          w * 512 + it * 128:w * 512 + (it + 1) * 128],
                          ohb[0:64, :], reads=("OHB",), writes=("AOs",))
        for nm in ["WHB0", "WHB1", "QRAW", "KRAW", "QFIN", "KFIN", "QTMP",
                   "VAU", "PTB", "OHB", "rot_cq", "rot_sq", "rot_ck", "rot_sk"]:
            A["ARB"].release(nm)
        P.barrier()
        wo = load_cast_bf(P, A, C, dram["L0_ffn_wo"][l], 512, D, "Wo")
        aostg = A["ARB"].alloc("AOC", 4 * 512, keys=[f"AOC{i}" for i in range(4)])
        for tc in range(NCH):
            for dm in range(4):
                P.dma("sp", aostg[:, dm * 512:(dm + 1) * 512],
                      scr["AOs"][dm * 128:(dm + 1) * 128, tc * 512:(tc + 1) * 512],
                      reads=("AOs",), writes=(f"AOC{dm}",))
            for of in range(ND):
                out = pS1[:, 0:512]
                for dm in range(4):
                    P.mm(out, wo[:, dm * D + of * 128:dm * D + of * 128 + 128],
                         aostg[:, dm * 512:(dm + 1) * 512],
                         start=(dm == 0), stop=(dm == 3), reads=("Wo", f"AOC{dm}"), writes=("pS1",))
                xsl = xi[:, of * T + tc * 512:of * T + (tc + 1) * 512]
                P.tt("dve", xsl, xsl, out, ALU.add, reads=("XI", "pS1"), writes=("XI",))
        A["ARB"].release("Wo")
        A["ARB"].release("AOC")
        P.barrier()
        build_xe(P, A, C, xi, xe)
        lgb = P.es.enter_context(nc.sbuf_tensor(f"b_lgb{l}", [128, 16], F32))
        P.dma("sp", lgb[:, 0:8], dram["L0_ffn_ln_g"][l].rearrange("(a p) -> p a", p=128),
              writes=("lgb",), slow=True)
        P.dma("sp", lgb[:, 8:16], dram["L0_ffn_ln_b"][l].rearrange("(a p) -> p a", p=128),
              writes=("lgb",), slow=True)
        b1a = A["ARF"].alloc("B1A", 22)
        b1g = A["ARF"].alloc("B1G", 22)
        P.dma("sp", b1a[:, 0:21], dram["L0_ffn_b1"][l][0:2688].rearrange("(a p) -> p a", p=128),
              writes=("B1A",), slow=True)
        P.dma("sp", b1a[0:42, 21:22], dram["L0_ffn_b1"][l][2688:2730].unsqueeze(1),
              writes=("B1A",), slow=True)
        P.dma("sp", b1g[:, 0:21], dram["L0_ffn_b1"][l][2730:5418].rearrange("(a p) -> p a", p=128),
              writes=("B1G",), slow=True)
        P.dma("sp", b1g[0:42, 21:22], dram["L0_ffn_b1"][l][5418:5460].unsqueeze(1),
              writes=("B1G",), slow=True)
        b2c = A["ARF"].alloc("B2C", 8)
        P.dma("sp", b2c[:], dram["L0_ffn_b2"][l].rearrange("(a p) -> p a", p=128),
              writes=("B2C",), slow=True)
        zb = A["ARB"].alloc("ZB", ND * 512)
        mch = A["ARB"].alloc("MCH", NJ * 512)
        w1r = [A["ARB"].alloc(f"W1R{i}", 16 * 128) for i in range(2)]
        w2r = [A["ARB"].alloc("W2R0", NJ * 128)]
        for tc in range(NCH):
            tiles = []
            for dm in range(ND):
                xv, xk = xs_views(P, A, xi, xe, dm, (tc * 512, 512))
                tiles.append((xv, xk))
            a_row, b_row = ln_stats(P, C, tiles, D)
            ab = pS0[:, 0:512]
            bb = pS0[:, 512:1024]
            P.mm(ab, C["ones_row"][0:1, 0:128], a_row, start=True, stop=True,
                 reads=("st_a", "ones_row"), writes=("pS0",))
            P.mm(bb, C["ones_row"][0:1, 0:128], b_row, start=True, stop=True,
                 reads=("st_b", "ones_row"), writes=("pS0",))
            for dm in range(ND):
                xv, xk = tiles[dm]
                t0 = C["lnsq"]
                P.tt("dve", t0[:], xv, ab, ALU.mult, reads=(xk, "pS0"), writes=("lnsq",))
                P.tt("dve", t0[:], t0[:], bb, ALU.add, reads=("lnsq", "pS0"), writes=("lnsq",))
                P.ts("dve", zb[:, dm * 512:(dm + 1) * 512], t0[:],
                     lgb[:, dm:dm + 1], lgb[:, 8 + dm:9 + dm], ALU.mult, ALU.add,
                     reads=("lnsq", "lgb"), writes=("ZB",))
            for j in range(NJ):
                wj = min(128, HID - j * 128)
                wr, wrk = w1r[j % 2], f"W1R{j % 2}"
                for half, cb in ((0, j * 128), (1, HID + j * 128)):
                    for dm in range(ND):
                        P.dma("sp", wr[:, (half * 8 + dm) * 128:(half * 8 + dm) * 128 + wj],
                              scr["w1bf"][l][dm * 128:(dm + 1) * 128, cb:cb + wj],
                              reads=("w1bf",), writes=(wrk,))
                pa = pS1[:, 0:512]
                for dm in range(ND):
                    P.mm(pa[0:wj, :] if wj < 128 else pa,
                         wr[:, dm * 128:dm * 128 + wj],
                         zb[:, dm * 512:(dm + 1) * 512],
                         start=(dm == 0), stop=(dm == ND - 1), reads=(wrk, "ZB"), writes=("pS1",))
                pg = pS1[0:wj, 512:1024]
                for dm in range(ND):
                    P.mm(pg, wr[:, (8 + dm) * 128:(8 + dm) * 128 + wj],
                         zb[:, dm * 512:(dm + 1) * 512],
                         start=(dm == 0), stop=(dm == ND - 1), reads=(wrk, "ZB"), writes=("pS1",))
                gact = pmb[0:wj, 0:512]
                P.act(gact, pg, AF.Gelu, bias=b1g[0:wj, j:j + 1], scale=1.0,
                      reads=("pS1", "B1G"), writes=("PMB",))
                P.emit("dve", lambda E, o=mch[0:wj, j * 512:j * 512 + 512],
                       a=(pa[0:wj, :] if wj < 128 else pa),
                       s=b1a[0:wj, j:j + 1], g=gact: E.scalar_tensor_tensor(
                           o, a, s, g, ALU.add, ALU.mult),
                       reads=("pS1", "B1A", "PMB"), writes=("MCH",))
            for of in range(ND):
                wr2, wrk2 = w2r[0], "W2R0"
                for j in range(NJ):
                    wj = min(128, HID - j * 128)
                    P.dma("sp", wr2[:, j * 128:j * 128 + 128][0:wj, :],
                          scr["w2bf"][l][j * 128:j * 128 + wj, of * 128:(of + 1) * 128],
                          reads=("w2bf",), writes=(wrk2,))
                out = pS0[:, 0:512] if of % 2 == 0 else pS0[:, 512:1024]
                ok = "pS0"
                for j in range(NJ):
                    wj = min(128, HID - j * 128)
                    P.mm(out, wr2[:, j * 128:j * 128 + 128][0:wj, :],
                         mch[0:wj, j * 512:(j + 1) * 512],
                         start=(j == 0), stop=(j == NJ - 1), reads=(wrk2, "MCH"), writes=(ok,))
                xsl = xi[:, of * T + tc * 512:of * T + (tc + 1) * 512]
                P.emit("dve", lambda E, o=xsl, a=out, s=b2c[:, of:of + 1], g=xsl:
                       E.scalar_tensor_tensor(o, a, s, g, ALU.add, ALU.add),
                       reads=(ok, "B2C", "XI"), writes=("XI",))
        A["ARF"].release("B1A")
        A["ARF"].release("B1G")
        A["ARF"].release("B2C")
        A["ARB"].release("ZB")
        A["ARB"].release("MCH")
        A["ARB"].release("W1R0")
        A["ARB"].release("W1R1")
        A["ARB"].release("W2R0")
        P.barrier()
    rc = A["ARF"].alloc("RCs", ND * 512, keys=[f"RC{i}" for i in range(ND)])
    for tc in range(NCH):
        ztiles = [(xi[:, dm * T + tc * 512:dm * T + (tc + 1) * 512], "XI") for dm in range(ND)]
        sandwich_update(P, A, C, ztiles, rin, rout, tc, rc)
    A["ARF"].release("RCs")
    for nm in ["XI", "XE"]:
        A["ARB"].release(nm)
    for nm in ["WST", "PMB", "PEF", "SQB", "NRM", "NCB", "NRR"]:
        A["ARF"].release(nm)
    P.barrier()

def emit_stage_c(P, A, C, dram, rin, rout, scr):
    nc = P.nc
    NH = 16
    gb = P.es.enter_context(nc.sbuf_tensor("c_gb", [128, NT], F32))
    P.dma("sp", gb[:], dram["L1_attn_pos_rev"], writes=("GB",))
    wb = P.es.enter_context(nc.sbuf_tensor("c_wb", [64, 1], F32))
    P.dma("sp", wb[:], dram["L1_attn_w_pos_rev"], writes=("WB",))
    bvr = P.es.enter_context(nc.sbuf_tensor("c_bvr", [1, D], F32))
    P.dma("sp", bvr[:], dram["L1_attn_b_v"].unsqueeze(0), writes=("BVR",))
    xn = A["ARB"].alloc("XN", ND * T)
    compute_xn(P, A, C, rin, xn, "XN")
    whb = A["ARB"].alloc("WHB", 40 * 64)
    bcolv = A["ARF"].alloc("BCOL", 4)
    wst = A["ARF"].alloc("WST", 64)
    qh = A["ARB"].alloc("WQH", T)
    kh = A["ARB"].alloc("WKH", T)
    qg = A["ARB"].alloc("GQH", T)
    kg = A["ARB"].alloc("GKH", T)
    vaug = A["ARB"].alloc("VAU", NT * 65)
    vaug64 = A["ARB"].alloc("VA64", 64 * 65)
    owh = A["ARB"].alloc("OWH", T)
    pwb = A["ARB"].alloc("PWB", 512)
    pb = [A["ARB"].alloc(f"PB{i}", 1024) for i in range(2)]
    ohb = A["ARB"].alloc("OHB", 1024)
    ncomb = A["ARF"].alloc("NCOMB", 1024, keys=["NC1"])
    nrr = A["ARF"].alloc("NRR", 1024, keys=["NRR"])
    pS0, pS1, pN0, pN1 = C["pS0"], C["pS1"], C["pN0"], C["pN1"]
    for h in range(NH):
        for gi, (wn, cb) in enumerate([("L1_attn_w_wqk", h * 64), ("L1_attn_w_wqk", D + h * 64),
                                       ("L1_attn_w_qk", h * 64), ("L1_attn_w_qk", D + h * 64),
                                       ("L1_attn_w_v", h * 64)]):
            for dm in range(ND):
                P.dma("sp", wst, dram[wn][dm * 128:(dm + 1) * 128, cb:cb + 64], writes=("WST",))
                P.copy("pool", whb[:, (gi * 8 + dm) * 64:(gi * 8 + dm + 1) * 64],
                       wst, reads=("WST",), writes=("WHB",))
        for gi, (bn, off) in enumerate([("L1_attn_b_wqk", h * 64), ("L1_attn_b_wqk", D + h * 64),
                                        ("L1_attn_b_qk", h * 64), ("L1_attn_b_qk", D + h * 64)]):
            P.dma("sp", bcolv[0:64, gi:gi + 1], dram[bn][off:off + 64].unsqueeze(1),
                  writes=("BCOL",), slow=True)
        for gi, dst, dk in ((0, qh, "WQH"), (1, kh, "WKH"), (2, qg, "GQH"), (3, kg, "GKH")):
            for tc in range(NCH):
                out = pN1[0:64, 0:512]
                for dm in range(ND):
                    P.mm(out, whb[:, (gi * 8 + dm) * 64:(gi * 8 + dm + 1) * 64],
                         xn[:, dm * T + tc * 512:dm * T + (tc + 1) * 512],
                         start=(dm == 0), stop=(dm == ND - 1), reads=("WHB", "XN"), writes=("pN1",))
                P.act(dst[0:64, tc * 512:(tc + 1) * 512], out, AF.Identity,
                      bias=bcolv[0:64, gi:gi + 1], reads=("pN1", "BCOL"), writes=(dk,))
        P.memset("pool", vaug[:], 1.0, writes=("VAU",))
        P.memset("pool", vaug64[:], 1.0, writes=("VA64",))
        for tt in range(NT):
            out = pS1[0:128, 0:64]
            for dm in range(ND):
                P.mm(out, xn[:, dm * T + tt * 128:dm * T + (tt + 1) * 128],
                     whb[:, (32 + dm) * 64:(32 + dm + 1) * 64],
                     start=(dm == 0), stop=False, reads=("XN", "WHB"), writes=("pS1",))
            P.mm(out, C["ones_row"][0:1, 0:128], bvr[0:1, h * 64:(h + 1) * 64],
                 start=False, stop=True, reads=("ones_row", "BVR"), writes=("pS1",))
            P.act(vaug[:, tt * 65:tt * 65 + 64], out, AF.Copy, reads=("pS1",), writes=("VAU",))
        for n in range(64):
            out = pS1[0:64, 64:128]
            for dm in range(ND):
                P.mm(out, xn[:, dm * T + n * 64:dm * T + (n + 1) * 64],
                     whb[:, (32 + dm) * 64:(32 + dm + 1) * 64],
                     start=(dm == 0), stop=False, reads=("XN", "WHB"), writes=("pS1",))
            P.mm(out, C["ones_row"][0:1, 0:64], bvr[0:1, h * 64:(h + 1) * 64],
                 start=False, stop=True, reads=("ones_row", "BVR"), writes=("pS1",))
            P.act(vaug64[0:64, n * 65:n * 65 + 64], out, AF.Copy, reads=("pS1",), writes=("VA64",))
        for nb in range(8):
            sp = pS0[0:64, 0:512]
            for wloc in range(8):
                n = nb * 8 + wloc
                P.mm(sp[:, wloc * 64:(wloc + 1) * 64],
                     kh[0:64, n * 64:(n + 1) * 64], qh[0:64, n * 64:(n + 1) * 64],
                     start=True, stop=True, reads=("WKH", "WQH"), writes=("pS0",))
            P.act(pwb[0:64, :], sp, AF.Exp, bias=wb[:], scale=0.125,
                  reads=("pS0", "WB"), writes=("PWB",))
            nps = pN0[0:65, 0:512]
            for wloc in range(8):
                n = nb * 8 + wloc
                P.mm(nps[:, wloc * 64:(wloc + 1) * 64],
                     vaug64[0:64, n * 65:n * 65 + 65],
                     pwb[0:64, wloc * 64:(wloc + 1) * 64],
                     start=True, stop=True, reads=("VA64", "PWB"), writes=("pN0",))
            t1 = ncomb[0:65, 0:512]
            P.copy("dve", t1, nps, reads=("pN0",), writes=("NC1",))
            rr = nrr[0:1, 0:512]
            P.emit("dve", lambda E, rr=rr, t1=t1: E.reciprocal(rr, t1[64:65, :]),
                   reads=("NC1",), writes=("NRR",))
            bc = pN1[0:64, 0:512]
            P.mm(bc, C["ones_row"][0:1, 0:64], rr, start=True, stop=True,
                 reads=("NRR", "ones_row"), writes=("pN1",))
            P.tt("dve", owh[0:64, nb * 512:(nb + 1) * 512], t1[0:64, :], bc, ALU.mult,
                 reads=("NC1", "pN1"), writes=("OWH",))
        for qcb in range(4):
            for kt in range(NT):
                sps = "pS0" if kt % 2 == 0 else "pS1"
                S = C[sps]
                pP, pPk = pb[kt % 2], f"PB{kt % 2}"
                for half in range(2):
                    P.mm(S[:, half * 512:(half + 1) * 512],
                         kg[0:64, kt * 128:(kt + 1) * 128],
                         qg[0:64, qcb * 1024 + half * 512:qcb * 1024 + (half + 1) * 512],
                         start=True, stop=True, reads=("GKH", "GQH"), writes=(sps,))
                P.act(pP, S[:], AF.Exp, bias=gb[:, kt:kt + 1], scale=1.0,
                      reads=(sps, "GB"), writes=(pPk,))
                for half in range(2):
                    sl = slice(half * 512, (half + 1) * 512)
                    P.mm(pN0[:, sl], vaug[:, kt * 65:kt * 65 + 65], pP[:, sl],
                         start=(kt == 0), stop=(kt == NT - 1), reads=("VAU", pPk), writes=("pN0",))
            t1 = ncomb[0:65, 0:1024]
            P.copy("dve", t1, pN0[:], reads=("pN0",), writes=("NC1",))
            rr = nrr[0:1, 0:1024]
            P.emit("dve", lambda E, rr=rr, t1=t1: E.reciprocal(rr, t1[64:65, :]),
                   reads=("NC1",), writes=("NRR",))
            bc = pS0[0:64, 0:1024]
            for half in range(2):
                P.mm(bc[:, half * 512:(half + 1) * 512], C["ones_row"][0:1, 0:64],
                     rr[:, half * 512:(half + 1) * 512], start=True, stop=True,
                     reads=("NRR", "ones_row"), writes=("pS0",))
            P.tt("dve", ohb[0:64, :], t1[0:64, :], bc, ALU.mult, reads=("NC1", "pS0"), writes=("OHB",))
            qsl = slice(qcb * 1024, (qcb + 1) * 1024)
            P.tt("dve", ohb[0:64, :], ohb[0:64, :], owh[0:64, qsl], ALU.add,
                 reads=("OHB", "OWH"), writes=("OHB",))
            P.dma("pool", scr["AOs"][h * 64:(h + 1) * 64, qsl], ohb[0:64, :],
                  reads=("OHB",), writes=("AOs",))
    for nm in ["WHB", "WQH", "WKH", "GQH", "GKH", "VAU", "VA64", "OWH", "PWB",
               "PB0", "PB1", "OHB"]:
        A["ARB"].release(nm)
    A["ARF"].release("NCOMB")
    A["ARF"].release("NRR")
    A["ARF"].release("BCOL")
    A["ARF"].release("WST")
    A["ARB"].release("XN")
    P.barrier()
    wout = load_cast_bf(P, A, C, dram["L1_attn_w_out"], D, D, "WoutC")
    boc = A["ARF"].alloc("BOC", 8)
    P.dma("sp", boc[:], dram["L1_attn_b_out"].rearrange("(a p) -> p a", p=128),
          writes=("BOC",), slow=True)
    aostg = A["ARB"].alloc("AOC", ND * 512, keys=[f"AOC{i}" for i in range(ND)])
    zc = A["ARF"].alloc("ZC", ND * 512, keys=[f"ZC{i}" for i in range(ND)])
    rc = A["ARF"].alloc("RCs", ND * 512, keys=[f"RC{i}" for i in range(ND)])
    for tc in range(NCH):
        for dm in range(ND):
            P.dma("sp", aostg[:, dm * 512:(dm + 1) * 512],
                  scr["AOs"][dm * 128:(dm + 1) * 128, tc * 512:(tc + 1) * 512],
                  reads=("AOs",), writes=(f"AOC{dm}",))
        ztiles = []
        for of in range(ND):
            out = C["pS1"][:, 0:512]
            for dm in range(ND):
                P.mm(out, wout[:, dm * D + of * 128:dm * D + of * 128 + 128],
                     aostg[:, dm * 512:(dm + 1) * 512],
                     start=(dm == 0), stop=(dm == ND - 1), reads=("WoutC", f"AOC{dm}"),
                     writes=("pS1",))
            zv = zc[:, of * 512:(of + 1) * 512]
            P.act(zv, out, AF.Identity, bias=boc[:, of:of + 1], reads=("pS1", "BOC"),
                  writes=(f"ZC{of}",))
            ztiles.append((zv, f"ZC{of}"))
        sandwich_update(P, A, C, ztiles, rin, rout, tc, rc)
    A["ARB"].release("WoutC")
    A["ARB"].release("AOC")
    A["ARF"].release("ZC")
    A["ARF"].release("RCs")
    A["ARF"].release("BOC")
    P.barrier()

def emit_stage_d(P, A, C, dram, rin, rout, scr):
    nc = P.nc
    E, F = 8, 4096
    NF = F // 128
    cts = P.es.enter_context(nc.sbuf_tensor("d_cts", [128, 64], F32))
    P.dma("sp", cts[:], dram["L1_ffn_centT"],
          writes=("CTS",))
    gtc = P.es.enter_context(nc.sbuf_tensor("d_gtc", [128, 64], F32))
    P.dma("sp", gtc[:], dram["L1_ffn_gammaT"],
          writes=("GTC",))
    btc = P.es.enter_context(nc.sbuf_tensor("d_btc", [128, 64], F32))
    P.dma("sp", btc[:], dram["L1_ffn_betaT"],
          writes=("BTC",))
    for e in range(E):
        stg = A["ARF"].alloc("wts", 4096)
        stgb = A["ARB"].alloc("wtsb", 4096)
        for r in range(ND):
            P.dma("sp", stg, dram["L1_ffn_W1T"][e][r * 128:(r + 1) * 128, :], writes=("wts",))
            P.copy("pool", stgb, stg, reads=("wts",), writes=("wtsb",))
            P.dma("pool", scr["W1Tbf"][e][r * 128:(r + 1) * 128, :], stgb,
                  reads=("wtsb",), writes=("W1Tbf",))
        A["ARF"].release("wts")
        A["ARB"].release("wtsb")
        stg = A["ARF"].alloc("wts", 1024)
        stgb = A["ARB"].alloc("wtsb", 1024)
        for r in range(NF):
            P.dma("sp", stg, dram["L1_ffn_W2T"][e][r * 128:(r + 1) * 128, :], writes=("wts",))
            P.copy("pool", stgb, stg, reads=("wts",), writes=("wtsb",))
            P.dma("pool", scr["W2Tbf"][e][r * 128:(r + 1) * 128, :], stgb,
                  reads=("wtsb",), writes=("W2Tbf",))
        A["ARF"].release("wts")
        A["ARB"].release("wtsb")
    b1r = A["ARB"].alloc("B1R", 4096)
    b2r = A["ARB"].alloc("B2R", 1024)
    xhc = A["ARB"].alloc("XHC", ND * 512)
    fcb = A["ARB"].alloc("FCB", ND * 512)
    zaf = A["ARB"].alloc("ZAF", ND * 512)
    ych = A["ARB"].alloc("YCH", ND * 512)
    rlc = A["ARB"].alloc("RLC", NF * 512)
    mfm = A["ARB"].alloc("MFM", 512)
    mrw = A["ARB"].alloc("MRW", E * 512)
    w1r = [A["ARB"].alloc(f"W1R{i}", 8 * 128) for i in range(2)]
    w2r = [A["ARB"].alloc(f"W2R{i}", 32 * 128) for i in range(2)]
    alc = A["ARF"].alloc("ALC", 4)
    alt4 = A["ARF"].alloc("ALT4", 128)
    alr = A["ARF"].alloc("ALR", 512)
    s2r = A["ARF"].alloc("S2R", 512)
    mu2r = A["ARF"].alloc("MU2R", 512)
    mkb = A["ARF"].alloc("MKB", 512)
    mxv = A["ARF"].alloc("MXV", 1)
    mtm = A["ARF"].alloc("MTM", 8)
    rc = A["ARF"].alloc("RCs", ND * 512, keys=[f"RC{i}" for i in range(ND)])
    pS0, pS1, pN0, pN1 = C["pS0"], C["pS1"], C["pN0"], C["pN1"]
    for tc in range(NCH):
        fc = A["ARF"].alloc("FC", ND * 512, keys=[f"FC{i}" for i in range(ND)])
        rtiles = stream_rin_chunk(P, A, C, rin, rc, tc)
        a_row, b_row = ln_stats(P, C, rtiles, D)
        ab, bb = pS0[:, 0:512], pS0[:, 512:1024]
        P.mm(ab, C["ones_row"][0:1, 0:128], a_row, start=True, stop=True,
             reads=("st_a", "ones_row"), writes=("pS0",))
        P.mm(bb, C["ones_row"][0:1, 0:128], b_row, start=True, stop=True,
             reads=("st_b", "ones_row"), writes=("pS0",))
        ftiles = []
        for dm in range(ND):
            rap, rkey = rtiles[dm]
            fv = fc[:, dm * 512:(dm + 1) * 512]
            P.tt("dve", C["lnsq"][:], rap, ab, ALU.mult, reads=(rkey, "pS0"), writes=("lnsq",))
            P.tt("dve", fv, C["lnsq"][:], bb, ALU.add, reads=("lnsq", "pS0"), writes=(f"FC{dm}",))
            P.copy("dve", fcb[:, dm * 512:(dm + 1) * 512], fv, reads=(f"FC{dm}",), writes=("FCB",))
            ftiles.append((fv, f"FC{dm}"))
        for t4 in range(4):
            sc = pS1[0:128, 0:8]
            for dm in range(ND):
                P.mm(sc, fc[:, dm * 512 + t4 * 128:dm * 512 + (t4 + 1) * 128],
                     cts[:, dm * 8:(dm + 1) * 8],
                     start=(dm == 0), stop=(dm == ND - 1), reads=(f"FC{dm}", "CTS"),
                     writes=("pS1",))
            mv = mxv[0:128, 0:1]
            P.emit("dve", lambda E_, mv=mv, sc=sc: E_.reduce_max(mv, sc, axis=AX.X),
                   reads=("pS1",), writes=("MXV",))
            P.emit("dve", lambda E_, sc=sc, mv=mv: E_.tensor_scalar(
                mtm[0:128, 0:8], sc, mv, None, ALU.is_ge),
                reads=("pS1", "MXV"), writes=("MTM",))
            P.act(alc[0:128, t4:t4 + 1], mv, AF.Sigmoid, reads=("MXV",), writes=("ALC",))
            pt = pS0[0:8, 0:128]
            P.trans(pt, mtm[0:128, 0:8], C["ident"][:], reads=("MTM", "ident"), writes=("pS0",))
            P.copy("dve", mfm[0:8, t4 * 128:(t4 + 1) * 128], pt, reads=("pS0",), writes=("MFM",))
        pt = pS0[0:4, 128:256]
        P.trans(pt, alc[0:128, 0:4], C["ident"][:], reads=("ALC", "ident"), writes=("pS0",))
        P.copy("dve", alt4[0:4, 0:128], pt, reads=("pS0",), writes=("ALT4",))
        for t4 in range(4):
            P.dma("pool", alr[0:1, t4 * 128:(t4 + 1) * 128], alt4[t4:t4 + 1, 0:128],
                  reads=("ALT4",), writes=("ALR",))
        for e in range(E):
            P.dma("pool", mrw[0:1, e * 512:(e + 1) * 512], mfm[e:e + 1, :],
                  reads=("MFM",), writes=("MRW",))
        a2, b2 = ln_stats(P, C, ftiles, D)
        P.emit("dve", lambda E_: E_.reciprocal(s2r[0:1, :], a2), reads=("st_a",), writes=("S2R",))
        P.tt("dve", mu2r[0:1, :], b2, s2r[0:1, :], ALU.mult, reads=("st_b", "S2R"), writes=("MU2R",))
        P.ts("dve", mu2r[0:1, :], mu2r[0:1, :], -1.0, None, ALU.mult, reads=("MU2R",), writes=("MU2R",))
        ab2, bb2 = pS0[:, 0:512], pS0[:, 512:1024]
        P.mm(ab2, C["ones_row"][0:1, 0:128], a2, start=True, stop=True,
             reads=("st_a", "ones_row"), writes=("pS0",))
        P.mm(bb2, C["ones_row"][0:1, 0:128], b2, start=True, stop=True,
             reads=("st_b", "ones_row"), writes=("pS0",))
        for dm in range(ND):
            fv = fc[:, dm * 512:(dm + 1) * 512]
            P.tt("dve", C["lnsq"][:], fv, ab2, ALU.mult, reads=(f"FC{dm}", "pS0"), writes=("lnsq",))
            P.tt("dve", xhc[:, dm * 512:(dm + 1) * 512], C["lnsq"][:], bb2, ALU.add,
                 reads=("lnsq", "pS0"), writes=("XHC",))
        A["ARF"].release("FC")
        P.memset("pool", ych[:], 0.0, writes=("YCH",))
        for e in range(E):
            b1s = A["ARF"].alloc("B1S", 4096)
            P.dma("sp", b1s[0:1, :], dram["L1_ffn_b1"][e].unsqueeze(0), writes=("B1S",))
            P.copy("pool", b1r[0:1, :], b1s[0:1, :], reads=("B1S",), writes=("B1R",))
            P.dma("sp", b1s[0:1, 0:1024], dram["L1_ffn_b2"][e].unsqueeze(0), writes=("B1S",))
            P.copy("pool", b2r[0:1, :], b1s[0:1, 0:1024], reads=("B1S",), writes=("B2R",))
            A["ARF"].release("B1S")
            mb = pS1[:, 0:512]
            P.mm(mb, C["ones_row_bf"][0:1, 0:128], mrw[0:1, e * 512:(e + 1) * 512],
                 start=True, stop=True, reads=("ones_row_bf", "MRW"), writes=("pS1",))
            P.copy("dve", mkb[:], mb, reads=("pS1",), writes=("MKB",))
            for dm in range(ND):
                P.emit("dve", lambda E_, o=C["lnsq"][:], i=xhc[:, dm * 512:(dm + 1) * 512],
                       g=gtc[:, dm * 8 + e:dm * 8 + e + 1], b=btc[:, dm * 8 + e:dm * 8 + e + 1]:
                       E_.tensor_scalar(o, i, g, b, ALU.mult, ALU.add),
                       reads=("XHC", "GTC", "BTC"), writes=("lnsq",))
                P.tt("dve", zaf[:, dm * 512:(dm + 1) * 512], C["lnsq"][:], mkb[:], ALU.mult,
                     reads=("lnsq", "MKB"), writes=("ZAF",))
            for ft in range(NF):
                wr, wrk = w1r[ft % 2], f"W1R{ft % 2}"
                for dm in range(ND):
                    P.dma("sp", wr[:, dm * 128:(dm + 1) * 128],
                          scr["W1Tbf"][e][dm * 128:(dm + 1) * 128, ft * 128:(ft + 1) * 128],
                          reads=("W1Tbf",), writes=(wrk,))
                hp = pS0[:, 0:512] if ft % 2 == 0 else pS0[:, 512:1024]
                for dm in range(ND):
                    P.mm(hp, wr[:, dm * 128:(dm + 1) * 128], zaf[:, dm * 512:(dm + 1) * 512],
                         start=(dm == 0), stop=False, reads=(wrk, "ZAF"), writes=("pS0",))
                P.mm(hp, b1r[0:1, ft * 128:(ft + 1) * 128], mrw[0:1, e * 512:(e + 1) * 512],
                     start=False, stop=True, reads=("B1R", "MRW"), writes=("pS0",))
                P.act(rlc[:, ft * 512:(ft + 1) * 512], hp, AF.Relu, reads=("pS0",), writes=("RLC",))
            for of in range(ND):
                wr2, wrk2 = w2r[of % 2], f"W2R{of % 2}"
                for ft in range(NF):
                    P.dma("sp", wr2[:, ft * 128:(ft + 1) * 128],
                          scr["W2Tbf"][e][ft * 128:(ft + 1) * 128, of * 128:(of + 1) * 128],
                          reads=("W2Tbf",), writes=(wrk2,))
                yp = pS1[:, 0:512] if of % 2 == 0 else pS1[:, 512:1024]
                for ft in range(NF):
                    P.mm(yp, wr2[:, ft * 128:(ft + 1) * 128], rlc[:, ft * 512:(ft + 1) * 512],
                         start=(ft == 0), stop=False, reads=(wrk2, "RLC"), writes=("pS1",))
                P.mm(yp, b2r[0:1, of * 128:(of + 1) * 128], mrw[0:1, e * 512:(e + 1) * 512],
                     start=False, stop=True, reads=("B2R", "MRW"), writes=("pS1",))
                ysl = ych[:, of * 512:(of + 1) * 512]
                P.tt("dve", ysl, ysl, yp, ALU.add, reads=("YCH", "pS1"), writes=("YCH",))
        zc = A["ARF"].alloc("ZC", ND * 512, keys=[f"ZC{i}" for i in range(ND)])
        s2b, mu2b = pS0[:, 0:512], pS0[:, 512:1024]
        alb = pS1[:, 0:512]
        P.mm(s2b, C["ones_row"][0:1, 0:128], s2r[0:1, :], start=True, stop=True,
             reads=("S2R", "ones_row"), writes=("pS0",))
        P.mm(mu2b, C["ones_row"][0:1, 0:128], mu2r[0:1, :], start=True, stop=True,
             reads=("MU2R", "ones_row"), writes=("pS0",))
        P.mm(alb, C["ones_row"][0:1, 0:128], alr[0:1, :], start=True, stop=True,
             reads=("ALR", "ones_row"), writes=("pS1",))
        ztiles = []
        for dm in range(ND):
            zv = zc[:, dm * 512:(dm + 1) * 512]
            P.tt("dve", C["lnsq"][:], xhc[:, dm * 512:(dm + 1) * 512], s2b, ALU.mult,
                 reads=("XHC", "pS0"), writes=("lnsq",))
            P.tt("dve", C["lnsq"][:], C["lnsq"][:], mu2b, ALU.add, reads=("lnsq", "pS0"),
                 writes=("lnsq",))
            P.tt("dve", zv, ych[:, dm * 512:(dm + 1) * 512], alb, ALU.mult,
                 reads=("YCH", "pS1"), writes=(f"ZC{dm}",))
            P.tt("dve", zv, zv, C["lnsq"][:], ALU.add, reads=(f"ZC{dm}", "lnsq"),
                 writes=(f"ZC{dm}",))
            ztiles.append((zv, f"ZC{dm}"))
        sandwich_update(P, A, C, ztiles, rin, rout, tc, rc)
        A["ARF"].release("ZC")
    P.barrier()


CONSTS = ["ident", "ones_row", "ones_row_bf", "ones_col", "ones_col_bf"]
A_CONSTS = ["decay"]
A_WEIGHTS = ["L0_attn_Wq", "L0_attn_Wk", "L0_attn_Wv", "L0_attn_Wo",
             "L0_attn_W_down_kv", "L0_attn_W_up_k", "L0_attn_W_up_v"]

WSHAPES = {
    "L0_attn_Wq": (D, D), "L0_attn_Wk": (D, D), "L0_attn_Wv": (D, D),
    "L0_attn_Wo": (D, D), "L0_attn_W_down_kv": (D, 256),
    "L0_attn_W_up_k": (256, D), "L0_attn_W_up_v": (256, D),
    "L0_ffn_wqkv": (2, D, 1536), "L0_ffn_q_scale": (2, 64), "L0_ffn_k_scale": (2, 64),
    "L0_ffn_wo": (2, 512, D), "L0_ffn_ln_g": (2, D), "L0_ffn_ln_b": (2, D),
    "L0_ffn_w1": (2, D, 5460), "L0_ffn_b1": (2, 5460),
    "L0_ffn_w2": (2, 2730, D), "L0_ffn_b2": (2, D),
    "L1_attn_w_v": (D, D), "L1_attn_b_v": (D,),
    "L1_attn_w_wqk": (D, 2048), "L1_attn_b_wqk": (2048,),
    "L1_attn_w_qk": (D, 2048), "L1_attn_b_qk": (2048,),
    "L1_attn_w_out": (D, D), "L1_attn_b_out": (D,),
    "L1_attn_w_pos_rev": (64, 1), "L1_attn_pos_rev": (128, 32),
    "L1_ffn_centT": (128, 64), "L1_ffn_gammaT": (128, 64), "L1_ffn_betaT": (128, 64),
    "L1_ffn_W1T": (8, D, 4096), "L1_ffn_b1": (8, 4096),
    "L1_ffn_W2T": (8, 4096, D), "L1_ffn_b2": (8, D),
}

CSHAPES = {
    "ident": (128, 128), "ones_row": (1, 512), "ones_row_bf": (1, 512),
    "ones_col": (128, 1), "ones_col_bf": (128, 1),
    "decay": (128, 8192),
    "rot_cq": (64, T), "rot_sq": (64, T), "rot_ck": (64, T), "rot_sk": (64, T),
    "rotmatT": (64, 64), "maskA": (128, 128), "maskB": (128, 128),
}
CBF16 = {"ones_row_bf", "ones_col_bf", "rot_cq", "rot_sq", "rot_ck", "rot_sk", "rotmatT"}


def build_stage_program(stage, with_out=False):
    nc = bass.Bass("TRN2", target_bir_lowering=False)
    es = contextlib.ExitStack()
    dram = {}
    needed_c = list(CONSTS)
    needed_w = []
    if "a" in stage:
        needed_c += A_CONSTS
        needed_w += A_WEIGHTS
    if "b" in stage:
        needed_c += ["rot_cq", "rot_sq", "rot_ck", "rot_sk", "rotmatT", "maskA", "maskB"]
        needed_w += [k for k in WSHAPES if k.startswith("L0_ffn")]
    if "c" in stage:
        needed_w += [k for k in WSHAPES if k.startswith("L1_attn")]
    if "d" in stage:
        needed_w += [k for k in WSHAPES if k.startswith("L1_ffn")]
    for k in needed_c:
        dram[k] = nc.dram_tensor("K_" + k, list(CSHAPES[k]),
                                 BF16 if k in CBF16 else F32, kind="ExternalInput").ap()
    for k in needed_w:
        dram[k] = nc.dram_tensor(k, list(WSHAPES[k]), F32, kind="ExternalInput").ap()
    rin = nc.dram_tensor("rin", [D, T], F32, kind="ExternalInput").ap()
    last = stage[-1]
    scr = {}
    scr["Qs"] = nc.dram_tensor("sQs", [D, T], BF16, kind="Internal").ap()
    scr["AOs"] = nc.dram_tensor("sAOs", [D, T], BF16, kind="Internal").ap()
    scr["XH"] = nc.dram_tensor("sXH", [D, T], BF16, kind="Internal").ap()
    scr["w1bf"] = nc.dram_tensor("sw1bf", [2, D, 5460], BF16, kind="Internal").ap()
    scr["w2bf"] = nc.dram_tensor("sw2bf", [2, 2730, D], BF16, kind="Internal").ap()
    scr["W1Tbf"] = nc.dram_tensor("sW1Tbf", [8, D, 4096], BF16, kind="Internal").ap()
    scr["W2Tbf"] = nc.dram_tensor("sW2Tbf", [8, 4096, D], BF16, kind="Internal").ap()
    rmids = {}
    for s in stage[:-1]:
        rmids[s] = nc.dram_tensor("rmid_" + s, [D, T], F32, kind="Internal").ap()
    rout = nc.dram_tensor("rout", [D, T], F32, kind="ExternalOutput").ap()

    P, A, C = prog_setup(nc, es, dram)
    cur = rin
    emitters = {"a": emit_stage_a, "b": emit_stage_b, "c": emit_stage_c, "d": emit_stage_d}
    for s in stage:
        dst = rout if s == last else rmids[s]
        emitters[s](P, A, C, dram, cur, dst, scr)
        cur = dst
        P.barrier()
    P.finalize()
    es.close()
    names = needed_c + needed_w
    return nc, names


_PROG_CACHE = {}

def _get_prog(stage):
    if stage not in _PROG_CACHE:
        _PROG_CACHE[stage] = build_stage_program(stage)
    return _PROG_CACHE[stage]


def host_inputs(inputs):
    h = dict(host_constants())
    h.update(rotary_tables())
    for k in CBF16:
        if h.get(k) is not None and h[k].dtype != BF:
            h[k] = h[k].astype(BF)
    w = {}
    for k in ["L0_attn_Wq", "L0_attn_Wk", "L0_attn_Wv", "L0_attn_Wo",
              "L0_attn_W_down_kv", "L0_attn_W_up_k", "L0_attn_W_up_v",
              "L0_ffn_wqkv", "L0_ffn_q_scale", "L0_ffn_k_scale", "L0_ffn_wo",
              "L0_ffn_ln_g", "L0_ffn_ln_b", "L0_ffn_w1", "L0_ffn_b1",
              "L0_ffn_w2", "L0_ffn_b2",
              "L1_attn_w_v", "L1_attn_b_v", "L1_attn_w_wqk", "L1_attn_b_wqk",
              "L1_attn_w_qk", "L1_attn_b_qk", "L1_attn_w_out", "L1_attn_b_out",
              "L1_ffn_b1", "L1_ffn_b2"]:
        w[k] = np.ascontiguousarray(inputs[k])
    w["L1_attn_w_pos_rev"] = np.ascontiguousarray(
        inputs["L1_attn_w_pos_embedding"][63::-1][:, None])
    pr = inputs["L1_attn_pos_embedding"][::-1]
    gbv = np.concatenate([pr[:1], pr])[:4096]
    w["L1_attn_pos_rev"] = np.ascontiguousarray(gbv.reshape(32, 128).T)
    def _col64(a):
        return np.ascontiguousarray(a.T.reshape(8, 128, 8).transpose(1, 0, 2).reshape(128, 64))
    w["L1_ffn_centT"] = _col64(inputs["L1_ffn_centroids"])
    w["L1_ffn_gammaT"] = _col64(inputs["L1_ffn_gamma"])
    w["L1_ffn_betaT"] = _col64(inputs["L1_ffn_beta"])
    w["L1_ffn_W1T"] = np.ascontiguousarray(inputs["L1_ffn_W1"].transpose(0, 2, 1))
    w["L1_ffn_W2T"] = np.ascontiguousarray(inputs["L1_ffn_W2"].transpose(0, 2, 1))
    h.update(w)
    return h


def run_stages(stages_list, inputs):
    hi = host_inputs(inputs)
    x = inputs["L0_attn_x"]
    B = x.shape[0]
    r_cur = [np.ascontiguousarray(x[b].T) for b in range(B)]
    for stage in stages_list:
        nc, names = _get_prog(stage)
        base = {("K_" + k if k in CSHAPES else k): hi[k] for k in names}
        in_maps = []
        for b in range(B):
            m = dict(base)
            m["rin"] = r_cur[b]
            in_maps.append(m)
        res = run_bass_kernel_spmd(nc, in_maps, core_ids=list(range(B)))
        r_cur = [np.asarray(res.results[b]["rout"]) for b in range(B)]
    return np.stack([r.T for r in r_cur], axis=0)


STAGES_MODE = ["a", "b", "c", "d"]

def kernel(**inputs):
    out = run_stages(STAGES_MODE, inputs)
    return out.astype(np.float32)


# version 06 (final)
# speedup vs baseline: 2.4956x; 2.4956x over previous
import contextlib
import numpy as np
import ml_dtypes
import concourse.bass as bass
import concourse.mybir as mybir
from concourse.bass_utils import run_bass_kernel_spmd

F32 = mybir.dt.float32
BF16 = mybir.dt.bfloat16
AF = mybir.ActivationFunctionType
ALU = mybir.AluOpType
AX = mybir.AxisListType

T = 4096
D = 1024
NT = T // 128
ND = D // 128
NC_CHUNK = 512
NCH = T // NC_CHUNK
EPS = 1e-5

BF = ml_dtypes.bfloat16


class Prog:

    CENG = ("pe", "act", "dve", "pool")
    DENG = ("sp", "act", "pool")

    def __init__(self, nc, es):
        self.nc = nc
        self.es = es
        self.eng = dict(pe=nc.tensor, act=nc.scalar, dve=nc.vector,
                        pool=nc.gpsimd, sp=nc.sync)
        self.ops = {e: [] for e in self.eng}
        self.cnt = {e: 0 for e in self.eng}
        self.dcnt = {e: 0 for e in self.eng}
        self.csem = {e: es.enter_context(nc.semaphore(f"c_{e}")) for e in self.CENG}
        self.NDSEM = 40
        self.dpool = [es.enter_context(nc.semaphore(f"dk_{i}")) for i in range(self.NDSEM)]
        self.dtotal = [0] * self.NDSEM
        self.dlastq = [None] * self.NDSEM
        self.dkey = {}
        self.dlru = list(range(self.NDSEM))
        self.swsems = []
        self.maxwait = {e: {} for e in self.eng}
        self.writer = {}
        self.readers = {}
        self.carry = {}

    def _sem(self, semkey):
        kind, e = semkey
        return self.csem[e] if kind == "c" else self.dpool[e]

    def _edge(self, eng, waits, token):
        if token is None:
            return
        semkey, val = token
        if semkey[0] == "c" and semkey[1] == "pe" and eng == "pe":
            return
        got = self.maxwait[eng].get(semkey, 0)
        if val <= got:
            return
        if semkey in waits:
            waits[semkey] = max(waits[semkey], val)
        else:
            waits[semkey] = val

    def emit(self, eng, fn, reads=(), writes=(), dma=False, tag=None):
        assert not dma or eng in self.DENG, f"engine {eng} cannot DMA"
        waits = {}
        for r in reads:
            self._edge(eng, waits, self.writer.get(r))
        for w in writes:
            self._edge(eng, waits, self.writer.get(w))
            for tok in self.readers.get(w, ()):
                self._edge(eng, waits, tok)
            for tok in self.carry.pop(w, ()):
                self._edge(eng, waits, tok)
        if dma:
            assert len(writes) == 1, f"dma needs exactly one write key, got {writes}"
            key = writes[0]
            if key not in self.dkey:
                idx = self.dlru.pop(0)
                for k2, (i2, b2) in list(self.dkey.items()):
                    if i2 == idx:
                        del self.dkey[k2]
                if self.dtotal[idx] > 0:
                    self._edge(eng, waits, (("d", idx), self.dtotal[idx]))
                self.dkey[key] = (idx, self.dtotal[idx])
            idx, base = self.dkey[key]
            self.dlru.remove(idx) if idx in self.dlru else None
            self.dlru.append(idx)
            self.dtotal[idx] += 16
            self.dlastq[idx] = eng
            self.dcnt[eng] += 16
            semkey = ("d", idx)
            inc_sem, inc_val, newval = self.dpool[idx], 16, self.dtotal[idx]
        else:
            self.cnt[eng] += 1
            semkey = ("c", eng)
            inc_sem, inc_val, newval = self.csem[eng], 1, self.cnt[eng]
        token = (semkey, newval)
        for sk, v in waits.items():
            self.maxwait[eng][sk] = max(self.maxwait[eng].get(sk, 0), v)
        wl = sorted(waits.items())

        def closure(eobj, wl=wl, fn=fn, inc_sem=inc_sem, inc_val=inc_val):
            for sk, v in wl:
                eobj.wait_ge(self._sem(sk), v)
            inst = fn(eobj)
            inst.then_inc(inc_sem, inc_val)
        self.ops[eng].append(closure)
        for r in reads:
            self.readers.setdefault(r, []).append(token)
        for w in writes:
            self.writer[w] = token
            self.readers[w] = []
        return token

    def barrier(self):
        snapshot = [(("c", e), self.cnt[e]) for e in self.CENG if self.cnt[e] > 0]
        snapshot += [(("d", i), self.dtotal[i]) for i in range(self.NDSEM) if self.dtotal[i] > 0]
        for e in self.eng:
            waits = {}
            for sk, v in snapshot:
                if sk == ("c", e):
                    continue
                got = self.maxwait[e].get(sk, 0)
                if v > got:
                    waits[sk] = v
                    self.maxwait[e][sk] = v
            if not waits:
                continue
            wl = sorted(waits.items())
            if e == "pe":
                def closure(eobj, wl=wl):
                    for sk, v in wl:
                        eobj.wait_ge(self._sem(sk), v)
                    eobj.nop()
            else:
                def closure(eobj, wl=wl):
                    for sk, v in wl:
                        eobj.wait_ge(self._sem(sk), v)
            self.ops[e].append(closure)

    def finalize(self):
        for i in range(self.NDSEM):
            if self.dtotal[i] > 0:
                e = self.dlastq[i] or "sp"
                got = self.maxwait[e].get(("d", i), 0)
                if got >= self.dtotal[i]:
                    continue
                def closure(eobj, i=i, v=self.dtotal[i]):
                    eobj.wait_ge(self.dpool[i], v)
                self.ops[e].append(closure)
        nc = self.nc
        with nc.Block() as block:
            @block.tensor
            def _(eobj):
                for f in self.ops["pe"]:
                    f(eobj)
            @block.scalar
            def _(eobj):
                for f in self.ops["act"]:
                    f(eobj)
            @block.vector
            def _(eobj):
                for f in self.ops["dve"]:
                    f(eobj)
            @block.gpsimd
            def _(eobj):
                for f in self.ops["pool"]:
                    f(eobj)
            @block.sync
            def _(eobj):
                for f in self.ops["sp"]:
                    f(eobj)

    def sw_dma(self, fn, reads=(), writes=()):
        sem = self.es.enter_context(self.nc.semaphore(f"sw{len(self.swsems)}"))
        self.swsems.append(sem)

        def op(E, fn=fn, sem=sem):
            fn(E, sem)
            E.wait_ge(sem, 16)
            return E.engine_nop()
        return self.emit("pool", op, reads=reads, writes=writes)

    def dma(self, eng, dst, src, reads=(), writes=(), slow=False):
        assert writes, "dma requires a destination key"
        return self.emit(eng, lambda E: E.dma_start(dst, src, allow_slow_non_contiguous=slow),
                         reads=reads, writes=writes, dma=True)

    def mm(self, out, lhsT, rhs, start, stop, reads=(), writes=()):
        return self.emit("pe", lambda E: E.matmul(out, lhsT, rhs, start=start, stop=stop),
                         reads=reads, writes=writes)

    def trans(self, out, in_, ident, reads=(), writes=()):
        return self.emit("pe", lambda E: E.transpose(out, in_, ident),
                         reads=reads, writes=writes)

    def act(self, out, in_, func, bias=None, scale=None, reads=(), writes=()):
        kw = {}
        if bias is not None:
            kw["bias"] = bias
        if scale is not None:
            kw["scale"] = scale
        return self.emit("act", lambda E: E.activation(out, in_, func, **kw),
                         reads=reads, writes=writes)

    def tt(self, eng, out, a, b, op, reads=(), writes=()):
        return self.emit(eng, lambda E: E.tensor_tensor(out, a, b, op),
                         reads=reads, writes=writes)

    def ts(self, eng, out, a, s1, s2, op0, op1=None, reads=(), writes=()):
        if op1 is None:
            return self.emit(eng, lambda E: E.tensor_scalar(out, a, s1, None, op0),
                             reads=reads, writes=writes)
        return self.emit(eng, lambda E: E.tensor_scalar(out, a, s1, s2, op0, op1),
                         reads=reads, writes=writes)

    def copy(self, eng, out, in_, reads=(), writes=()):
        return self.emit(eng, lambda E: E.tensor_copy(out, in_), reads=reads, writes=writes)

    def memset(self, eng, ap, val, reads=(), writes=()):
        return self.emit(eng, lambda E: E.memset(ap, val), reads=reads, writes=writes)



def host_constants():
    c = {}
    c["ident"] = np.eye(128, dtype=np.float32)
    c["ones_row"] = np.ones((1, 512), dtype=np.float32)
    c["ones_row_bf"] = np.ones((1, 512), dtype=BF)
    c["ones_col"] = np.ones((128, 1), dtype=np.float32)
    c["ones_col_bf"] = np.ones((128, 1), dtype=BF)
    lam = 0.01 * 0.01
    p = np.arange(128)[:, None]
    z = np.arange(8192)[None, :]
    c["decay"] = np.exp(-lam * np.abs(p - (z - 3968))).astype(np.float32)
    return c


def rotary_tables():
    DIM_HEAD, WINDOW = 64, 512
    n = T
    half = np.arange(0, DIM_HEAD, 2, dtype=np.float64)
    inv_freq = 1.0 / 10000.0 ** (half / DIM_HEAD)
    t = np.arange(n, dtype=np.float64)
    freqs = t[:, None] * inv_freq[None, :]
    freqs = np.concatenate([freqs, freqs], axis=-1)
    base = (half + 0.4 * DIM_HEAD) / (1.4 * DIM_HEAD)
    power = (t - n // 2) / float(WINDOW // 2)
    scale = base[None, :] ** power[:, None]
    scale = np.concatenate([scale, scale], axis=-1)
    cos, sin = np.cos(freqs), np.sin(freqs)
    out = {
        "rot_cq": (cos * scale).T.astype(np.float32),
        "rot_sq": (sin * scale).T.astype(np.float32),
        "rot_ck": (cos / scale).T.astype(np.float32),
        "rot_sk": (sin / scale).T.astype(np.float32),
    }
    R = np.zeros((64, 64), dtype=np.float32)
    for m in range(32):
        R[m, m + 32] = -1.0
    for m in range(32, 64):
        R[m, m - 32] = 1.0
    out["rotmatT"] = R.T.astype(BF)
    p = np.arange(128)[:, None]
    cc = np.arange(128)[None, :]
    out["maskA"] = np.where(cc >= p, 0.0, -1.0e30).astype(np.float32)
    out["maskB"] = np.where(cc <= p, 0.0, -1.0e30).astype(np.float32)
    return out



class Arena:
    def __init__(self, P, name, cols, dt):
        self.P = P
        self.t = P.es.enter_context(P.nc.sbuf_tensor(name, [128, cols], dt))
        self.cols = cols
        self.free = [(0, cols, [])]
        self.name = name
        self.live = {}

    def alloc(self, tag, n, keys=None):
        keys = list(keys) if keys else [tag]
        for i, (off, ln, toks) in enumerate(self.free):
            if ln >= n:
                self.free[i] = (off + n, ln - n, list(toks))
                self.live[tag] = (off, n, keys)
                for k in keys:
                    self.P.carry.setdefault(k, []).extend(toks)
                return self.t[:, off:off + n]
        raise RuntimeError(f"arena {self.name} OOM for {tag} ({n} cols); "
                           f"free={[(o, l) for o, l, _ in self.free]}")

    def release(self, tag):
        off, n, keys = self.live.pop(tag)
        toks = []
        for k in keys:
            w = self.P.writer.pop(k, None)
            if w is not None:
                toks.append(w)
            toks.extend(self.P.readers.pop(k, ()))
            self.P.carry.pop(k, None)
        self.free.append([off, n, toks])
        self.free = [list(x) for x in self.free]
        self.free.sort(key=lambda x: x[0])
        merged = []
        for off, ln, tk in self.free:
            if merged and merged[-1][0] + merged[-1][1] == off:
                merged[-1] = [merged[-1][0], merged[-1][1] + ln, merged[-1][2] + tk]
            else:
                merged.append([off, ln, list(tk)])
        self.free = merged


def ln_stats(P, C, src_tiles, n_feat):
    nc = P.nc
    ssum = C["pN0"][0:1, 0:512]
    ssq = C["pB"][0:1, 0:512]
    ones_c = C["ones_col"]
    ones_cb = C["ones_col_bf"]
    sqt = C["lnsq"]
    first = True
    for i, (ap, key) in enumerate(src_tiles):
        p = ap.shape[0]
        oc = ones_cb if ap.dtype == BF16 else ones_c
        ock = "ones_col_bf" if ap.dtype == BF16 else "ones_col"
        P.mm(ssum, oc[0:p, 0:1], ap, start=first, stop=(i == len(src_tiles) - 1),
             reads=(key, ock), writes=("pN0",))
        P.act(sqt[0:p, :], ap, AF.Square, reads=(key,), writes=("lnsq",))
        P.mm(ssq, ones_c[0:p, 0:1], sqt[0:p, :], start=first, stop=(i == len(src_tiles) - 1),
             reads=("lnsq", "ones_col"), writes=("pB",))
        first = False
    st = C["stat"]
    m = st[0:1, 0:512]; e = st[0:1, 512:1024]
    c2 = st[0:1, 1024:1536]; a = st[0:1, 1536:2048]
    inv = 1.0 / n_feat
    P.ts("dve", m, ssum, inv, None, ALU.mult, reads=("pN0",), writes=("st_m",))
    P.ts("dve", e, ssq, inv, None, ALU.mult, reads=("pB",), writes=("st_e",))
    P.tt("dve", c2, m, m, ALU.mult, reads=("st_m",), writes=("st_c",))
    P.tt("dve", e, e, c2, ALU.subtract, reads=("st_e", "st_c"), writes=("st_e",))
    P.ts("dve", e, e, EPS, None, ALU.add, reads=("st_e",), writes=("st_e",))
    P.emit("dve", lambda E: E.reciprocal(c2, e), reads=("st_e",), writes=("st_c",))
    P.act(a, c2, AF.Sqrt, reads=("st_c",), writes=("st_a",))
    b = c2
    P.tt("dve", b, m, a, ALU.mult, reads=("st_m", "st_a"), writes=("st_b",))
    P.ts("dve", b, b, -1.0, None, ALU.mult, reads=("st_b",), writes=("st_b",))
    return a, b


def ln_apply(P, C, a_row, b_row, src_tiles, dst_tiles):
    ps = C["pS0"]
    ab = ps[:, 0:512]
    bb = ps[:, 512:1024]
    orow = C["ones_row"]
    P.mm(ab, orow[0:1, 0:128], a_row, start=True, stop=True, reads=("st_a", "ones_row"), writes=("pS0",))
    P.mm(bb, orow[0:1, 0:128], b_row, start=True, stop=True, reads=("st_b", "ones_row"), writes=("pS0",))
    for (sap, skey), (dap, dkey) in zip(src_tiles, dst_tiles):
        p = sap.shape[0]
        tmpk = ("lnt",)
        tmp = C["lnsq"]
        P.tt("dve", tmp[0:p, :], sap, ab[0:p, :], ALU.mult, reads=(skey, "pS0"), writes=("lnsq",))
        P.tt("dve", dap, tmp[0:p, :], bb[0:p, :], ALU.add, reads=("lnsq", "pS0"), writes=(dkey,))




ARB_COLS = 71680
ARF_COLS = 12288

def prog_setup(nc, es, const_dram):
    P = Prog(nc, es)
    A = {"ARB": Arena(P, "ARB", ARB_COLS, BF16), "ARF": Arena(P, "ARF", ARF_COLS, F32)}
    C = {}
    for name, shape, dt in [
        ("ident", [128, 128], F32), ("ones_row", [1, 512], F32),
        ("ones_row_bf", [1, 512], BF16),
        ("ones_col", [128, 1], F32), ("ones_col_bf", [128, 1], BF16),
    ]:
        t = es.enter_context(nc.sbuf_tensor("c_" + name, shape, dt))
        P.dma("sp", t[:], const_dram[name], writes=(name,))
        C[name] = t
    C["pS0"] = es.enter_context(nc.psum_tensor("pS0", [128, 1024], F32))
    C["pS1"] = es.enter_context(nc.psum_tensor("pS1", [128, 1024], F32))
    C["pN0"] = es.enter_context(nc.psum_tensor("pN0", [65, 1024], F32))
    C["pA"] = es.enter_context(nc.psum_tensor("pA", [128, 512], F32))
    C["pB"] = es.enter_context(nc.psum_tensor("pB", [128, 512], F32))
    C["lnsq"] = es.enter_context(nc.sbuf_tensor("lnsq", [128, 512], F32))
    C["stat"] = es.enter_context(nc.sbuf_tensor("statrows", [1, 2048], F32))
    return P, A, C


def stream_rin_chunk(P, A, C, rin, rc, tc):
    tiles = []
    for dm in range(ND):
        v = rc[:, dm * 512:(dm + 1) * 512]
        P.dma("sp", v, rin[dm * 128:(dm + 1) * 128, tc * 512:(tc + 1) * 512],
              writes=(f"RC{dm}",))
        tiles.append((v, f"RC{dm}"))
    return tiles


def compute_xn(P, A, C, rin, xn_view, xn_key):
    rc = A["ARF"].alloc("RCx", ND * 512, keys=[f"RC{i}" for i in range(ND)])
    for tc in range(NCH):
        tiles = stream_rin_chunk(P, A, C, rin, rc, tc)
        a_row, b_row = ln_stats(P, C, tiles, D)
        dst = [(xn_view[:, dm * T + tc * 512: dm * T + (tc + 1) * 512], xn_key)
               for dm in range(ND)]
        ln_apply(P, C, a_row, b_row, tiles, dst)
    A["ARF"].release("RCx")


def sandwich_update(P, A, C, zc_tiles, rin, rout, tc, rc):
    rtiles = stream_rin_chunk(P, A, C, rin, rc, tc)
    a_row, b_row = ln_stats(P, C, zc_tiles, D)
    ps = C["pS0"]
    ab, bb = ps[:, 0:512], ps[:, 512:1024]
    orow = C["ones_row"]
    P.mm(ab, orow[0:1, 0:128], a_row, start=True, stop=True, reads=("st_a", "ones_row"), writes=("pS0",))
    P.mm(bb, orow[0:1, 0:128], b_row, start=True, stop=True, reads=("st_b", "ones_row"), writes=("pS0",))
    tmp = C["lnsq"]
    for dm in range(ND):
        zap, zkey = zc_tiles[dm]
        rap, rkey = rtiles[dm]
        P.tt("dve", tmp[:], zap, ab, ALU.mult, reads=(zkey, "pS0"), writes=("lnsq",))
        P.tt("dve", tmp[:], tmp[:], bb, ALU.add, reads=("lnsq", "pS0"), writes=("lnsq",))
        P.tt("dve", rap, tmp[:], rap, ALU.add, reads=("lnsq", rkey), writes=(rkey,))
        P.dma("pool", rout[dm * 128:(dm + 1) * 128, tc * 512:(tc + 1) * 512], rap,
              reads=(rkey,), writes=("rout",))


def load_cast_bf(P, A, C, w_dram, rows, cols, tag):
    nb = rows // 128
    stg = [A["ARF"].alloc(tag + f"_s{i}", cols) for i in range(2)]
    view = A["ARB"].alloc(tag, nb * cols)
    for r in range(nb):
        s = stg[r % 2]
        sk = tag + f"_s{r % 2}"
        P.dma("sp", s, w_dram[r * 128:(r + 1) * 128, :], writes=(sk,))
        P.copy("pool", view[:, r * cols:(r + 1) * cols], s, reads=(sk,), writes=(tag,))
    A["ARF"].release(tag + "_s0")
    A["ARF"].release(tag + "_s1")
    return view



def emit_stage_a(P, A, C, dram, rin, rout, scr):
    nc = P.nc
    NH = 16

    xn = A["ARB"].alloc("XN", ND * T)
    compute_xn(P, A, C, rin, xn, "XN")

    wsumT = A["ARB"].alloc("WsT", ND * D)
    stg0 = A["ARF"].alloc("wk_s", D)
    stg1 = A["ARF"].alloc("wv_s", D)
    for r in range(ND):
        P.dma("sp", stg0, dram["L0_attn_Wk"][r * 128:(r + 1) * 128, :], writes=("wk_s",))
        P.dma("sp", stg1, dram["L0_attn_Wv"][r * 128:(r + 1) * 128, :], writes=("wv_s",))
        P.tt("dve", stg0, stg0, stg1, ALU.add, reads=("wk_s", "wv_s"), writes=("wk_s",))
        for of in range(ND):
            pt = C["pS0"][:, 0:128]
            P.trans(pt, stg0[:, of * 128:(of + 1) * 128], C["ident"][:],
                    reads=("wk_s", "ident"), writes=("pS0",))
            P.act(wsumT[:, of * D + r * 128:of * D + (r + 1) * 128], pt, AF.Copy,
                  reads=("pS0",), writes=("WsT",))
    A["ARF"].release("wk_s")
    A["ARF"].release("wv_s")
    wdn = load_cast_bf(P, A, C, dram["L0_attn_W_down_kv"], D, 256, "Wdn")
    wcomb = A["ARB"].alloc("Wcomb", ND * 256)
    for i in range(ND):
        out = C["pS1"][:, 0:256]
        for m in range(ND):
            P.mm(out, wsumT[:, m * D + i * 128:m * D + (i + 1) * 128],
                 wdn[:, m * 256:(m + 1) * 256],
                 start=(m == 0), stop=(m == ND - 1), reads=("WsT", "Wdn"), writes=("pS1",))
        P.act(wcomb[:, i * 256:(i + 1) * 256], out, AF.Copy, reads=("pS1",), writes=("Wcomb",))
    A["ARB"].release("WsT")
    A["ARB"].release("Wdn")
    P.barrier()

    lat = A["ARB"].alloc("LAT", 2 * T)
    for lt in range(2):
        for tc in range(NCH):
            ps, pk = (C["pS0"], "pS0") if (lt * NCH + tc) % 2 == 0 else (C["pS1"], "pS1")
            out = ps[:, 0:512]
            for dm in range(ND):
                P.mm(out, wcomb[:, dm * 256 + lt * 128:dm * 256 + (lt + 1) * 128],
                     xn[:, dm * T + tc * 512:dm * T + (tc + 1) * 512],
                     start=(dm == 0), stop=(dm == ND - 1), reads=("Wcomb", "XN"), writes=(pk,))
            P.act(lat[:, lt * T + tc * 512:lt * T + (tc + 1) * 512], out, AF.Copy,
                  reads=(pk,), writes=("LAT",))
    A["ARB"].release("Wcomb")
    P.barrier()

    wq = load_cast_bf(P, A, C, dram["L0_attn_Wq"], D, D, "Wq")
    qstg = [A["ARB"].alloc(f"QS{i}", 512) for i in range(2)]
    for of in range(ND):
        for tc in range(NCH):
            i = of * NCH + tc
            ps, pk = (C["pS0"], "pS0") if i % 2 == 0 else (C["pS1"], "pS1")
            out = ps[:, 0:512]
            for dm in range(ND):
                P.mm(out, wq[:, dm * D + of * 128:dm * D + of * 128 + 128],
                     xn[:, dm * T + tc * 512:dm * T + (tc + 1) * 512],
                     start=(dm == 0), stop=(dm == ND - 1), reads=("Wq", "XN"), writes=(pk,))
            qs, qk = qstg[i % 2], f"QS{i % 2}"
            P.act(qs, out, AF.Copy, reads=(pk,), writes=(qk,))
            P.dma("pool", scr["Qs"][of * 128:(of + 1) * 128, tc * 512:(tc + 1) * 512],
                  qs, reads=(qk,), writes=("Qs",))
    A["ARB"].release("Wq")
    A["ARB"].release("XN")
    A["ARB"].release("QS0")
    A["ARB"].release("QS1")
    P.barrier()

    wupk = load_cast_bf(P, A, C, dram["L0_attn_W_up_k"], 256, D, "Wupk")
    wupv = load_cast_bf(P, A, C, dram["L0_attn_W_up_v"], 256, D, "Wupv")
    krec = A["ARB"].alloc("KRE", T)
    vaug = A["ARB"].alloc("VAU", NT * 65)
    qh = [A["ARB"].alloc(f"QH{i}", T) for i in range(2)]
    ohb = A["ARB"].alloc("OHB", 1024)
    pb = [A["ARB"].alloc(f"PB{i}", 1024) for i in range(2)]
    dky = A["ARF"].alloc("DKY", 8192)
    P.dma("sp", dky, dram["decay"], writes=("DKY",))
    pm = [A["ARF"].alloc(f"PM{i}", 1024) for i in range(2)]
    ncomb = A["ARF"].alloc("NCOMB", 1024, keys=["NC1"])
    nrr = A["ARF"].alloc("NRR", 1024, keys=["NRR"])
    pN0 = C["pN0"]
    for h in range(NH):
        qcur, qck = qh[h % 2], f"QH{h % 2}"
        P.dma("sp", qcur[0:64, :], scr["Qs"][h * 64:(h + 1) * 64, :], reads=("Qs",), writes=(qck,))
        for tc in range(NCH):
            out = C["pS0"][0:64, 0:512]
            for lt in range(2):
                P.mm(out, wupk[:, lt * D + h * 64:lt * D + h * 64 + 64],
                     lat[:, lt * T + tc * 512:lt * T + (tc + 1) * 512],
                     start=(lt == 0), stop=(lt == 1), reads=("Wupk", "LAT"), writes=("pS0",))
            P.act(krec[0:64, tc * 512:(tc + 1) * 512], out, AF.Copy, reads=("pS0",), writes=("KRE",))
        P.memset("pool", vaug[:], 1.0, writes=("VAU",))
        for tt in range(NT):
            out = C["pS1"][0:128, 0:64]
            for lt in range(2):
                P.mm(out, lat[:, lt * T + tt * 128:lt * T + (tt + 1) * 128],
                     wupv[:, lt * D + h * 64:lt * D + h * 64 + 64],
                     start=(lt == 0), stop=(lt == 1), reads=("Wupv", "LAT"), writes=("pS1",))
            P.act(vaug[:, tt * 65:tt * 65 + 64], out, AF.Copy, reads=("pS1",), writes=("VAU",))
        for qcb in range(4):
            def head_a(kt, qcb=qcb):
                sps = "pS0" if kt % 2 == 0 else "pS1"
                S = C[sps]
                for half in range(2):
                    P.mm(S[:, half * 512:(half + 1) * 512],
                         krec[0:64, kt * 128:(kt + 1) * 128],
                         qcur[0:64, qcb * 1024 + half * 512:qcb * 1024 + (half + 1) * 512],
                         start=True, stop=True, reads=("KRE", qck), writes=(sps,))
            def tail_a(kt, qcb=qcb):
                sps = "pS0" if kt % 2 == 0 else "pS1"
                S = C[sps]
                pmb, pmk = pm[kt % 2], f"PM{kt % 2}"
                pP, pPk = pb[kt % 2], f"PB{kt % 2}"
                doff = qcb * 1024 - 128 * kt + 3968
                P.tt("dve", pmb, S[:], dky[:, doff:doff + 1024], ALU.mult,
                     reads=(sps, "DKY"), writes=(pmk,))
                P.act(pP, pmb, AF.Exp, scale=1.0 / 32.0, reads=(pmk,), writes=(pPk,))
                for half in range(2):
                    sl = slice(half * 512, (half + 1) * 512)
                    P.mm(pN0[:, sl], vaug[:, kt * 65:kt * 65 + 65], pP[:, sl],
                         start=(kt == 0), stop=(kt == NT - 1), reads=("VAU", pPk), writes=("pN0",))
            for kt in range(NT):
                head_a(kt)
                if kt > 0:
                    tail_a(kt - 1)
            tail_a(NT - 1)
            t1 = ncomb[0:65, 0:1024]
            P.copy("dve", t1, pN0[:], reads=("pN0",), writes=("NC1",))
            rr = nrr[0:1, 0:1024]
            P.emit("dve", lambda E, rr=rr, t1=t1: E.reciprocal(rr, t1[64:65, :]),
                   reads=("NC1",), writes=("NRR",))
            bc = C["pS0"][0:64, 0:1024]
            for half in range(2):
                P.mm(bc[:, half * 512:(half + 1) * 512], C["ones_row"][0:1, 0:64],
                     rr[:, half * 512:(half + 1) * 512], start=True, stop=True,
                     reads=("NRR", "ones_row"), writes=("pS0",))
            P.tt("dve", ohb[0:64, :], t1[0:64, :], bc, ALU.mult, reads=("NC1", "pS0"), writes=("OHB",))
            P.dma("pool", scr["AOs"][h * 64:(h + 1) * 64, qcb * 1024:(qcb + 1) * 1024],
                  ohb[0:64, :], reads=("OHB",), writes=("AOs",))
    for nm in ["KRE", "VAU", "QH0", "QH1", "OHB", "PB0", "PB1", "LAT",
               "Wupk", "Wupv"]:
        A["ARB"].release(nm)
    A["ARF"].release("NCOMB")
    A["ARF"].release("NRR")
    A["ARF"].release("DKY")
    A["ARF"].release("PM0")
    A["ARF"].release("PM1")
    P.barrier()

    wo = load_cast_bf(P, A, C, dram["L0_attn_Wo"], D, D, "Wo")
    aostg = A["ARB"].alloc("AOC", ND * 512, keys=[f"AOC{i}" for i in range(ND)])
    zc = A["ARF"].alloc("ZC", ND * 512, keys=[f"ZC{i}" for i in range(ND)])
    rc = A["ARF"].alloc("RCs", ND * 512, keys=[f"RC{i}" for i in range(ND)])
    for tc in range(NCH):
        for dm in range(ND):
            P.dma("sp", aostg[:, dm * 512:(dm + 1) * 512],
                  scr["AOs"][dm * 128:(dm + 1) * 128, tc * 512:(tc + 1) * 512],
                  reads=("AOs",), writes=(f"AOC{dm}",))
        ztiles = []
        for of in range(ND):
            ps, pk = (C["pS1"], "pS1")
            out = ps[:, 0:512]
            for dm in range(ND):
                P.mm(out, wo[:, dm * D + of * 128:dm * D + of * 128 + 128],
                     aostg[:, dm * 512:(dm + 1) * 512],
                     start=(dm == 0), stop=(dm == ND - 1), reads=("Wo", f"AOC{dm}"), writes=(pk,))
            zv = zc[:, of * 512:(of + 1) * 512]
            P.act(zv, out, AF.Copy, reads=(pk,), writes=(f"ZC{of}",))
            ztiles.append((zv, f"ZC{of}"))
        sandwich_update(P, A, C, ztiles, rin, rout, tc, rc)
    A["ARB"].release("Wo")
    A["ARB"].release("AOC")
    A["ARF"].release("ZC")
    A["ARF"].release("RCs")
    P.barrier()


def xs_views(P, A, xi, xe, dm, cols):
    c0, ln = cols
    if dm < 4:
        return xi[:, dm * T + c0:dm * T + c0 + ln], "XI"
    if c0 == 0:
        return xe[:, (dm - 4) * 512:(dm - 4) * 512 + ln], "XE"
    return xi[:, dm * T + c0 - 1:dm * T + c0 - 1 + ln], "XI"


def build_xe(P, A, C, xi, xe):
    for dm in range(4, 8):
        sl = xe[:, (dm - 4) * 512:(dm - 3) * 512]
        P.memset("dve", sl[:, 0:1], 0.0, writes=("XE",))
        P.copy("dve", sl[:, 1:512], xi[:, dm * T:dm * T + 511], reads=("XI",), writes=("XE",))


def emit_stage_b(P, A, C, dram, rin, rout, scr):
    nc = P.nc
    NH, DH, WIN = 8, 64, 512
    HID = 2730
    NJ = 22
    for nm in ["maskA", "maskB"]:
        t = P.es.enter_context(nc.sbuf_tensor("b_" + nm, [128, 128], F32))
        P.dma("sp", t[:], dram[nm], writes=(nm,))
        C[nm] = t
    rott = {}
    t = P.es.enter_context(nc.sbuf_tensor("b_rotmatT", [64, 64], BF16))
    P.dma("sp", t[:], dram["rotmatT"], writes=("rotmatT",))
    rott["rotmatT"] = t

    for l in range(2):
        stg = A["ARF"].alloc("w1s", 5460)
        stgb = A["ARB"].alloc("w1sb", 5460)
        for r in range(ND):
            P.dma("sp", stg, dram["L0_ffn_w1"][l][r * 128:(r + 1) * 128, :], writes=("w1s",))
            P.copy("pool", stgb, stg, reads=("w1s",), writes=("w1sb",))
            for half, cb in ((0, 0), (1, HID)):
                P.dma("pool", scr["w1bf"][l][:, half, r][0:21, :, :].rearrange("j p c -> p j c"),
                      stgb[:, cb:cb + 2688].rearrange("p (j c) -> p j c", c=128), reads=("w1sb",), writes=("w1bf",))
                P.dma("pool", scr["w1bf"][l][21][half][r][:, 0:42],
                      stgb[:, cb + 2688:cb + 2730], reads=("w1sb",), writes=("w1bf",))
        A["ARF"].release("w1s")
        A["ARB"].release("w1sb")
        stg = A["ARF"].alloc("w2s", 1024)
        stgb = A["ARB"].alloc("w2sb", 1024)
        for r in range(NJ):
            p = min(128, HID - r * 128)
            P.dma("sp", stg[0:p, :], dram["L0_ffn_w2"][l][r * 128:r * 128 + p, :], writes=("w2s",))
            P.copy("pool", stgb[0:p, :], stg[0:p, :], reads=("w2s",), writes=("w2sb",))
            P.dma("pool", scr["w2bf"][l][:, r][:, 0:p, :].rearrange("o p c -> p o c"),
                  stgb[0:p, :].rearrange("p (o c) -> p o c", c=128), reads=("w2sb",), writes=("w2bf",))
        A["ARF"].release("w2s")
        A["ARB"].release("w2sb")

    xi = A["ARB"].alloc("XI", ND * T)
    compute_xn(P, A, C, rin, xi, "XI")
    xe = A["ARB"].alloc("XE", 4 * 512)
    wst = A["ARF"].alloc("WST", 64)
    pmb = A["ARF"].alloc("PMB", 640)
    pef = A["ARF"].alloc("PEF", 640)
    pef2 = A["ARF"].alloc("PEF2", 640)
    pef3 = A["ARF"].alloc("PEF3", 640)
    sqb = A["ARF"].alloc("SQB", 512)
    nrm = A["ARF"].alloc("NRM", 512)
    ncomb = A["ARF"].alloc("NCB", 128, keys=["NC1"])
    nrr = A["ARF"].alloc("NRR", 128, keys=["NRR"])
    qsc = P.es.enter_context(nc.sbuf_tensor("b_qsc", [64, 2], F32))
    ksc = P.es.enter_context(nc.sbuf_tensor("b_ksc", [64, 2], F32))
    for l in range(2):
        P.dma("sp", qsc[:, l:l + 1], dram["L0_ffn_q_scale"][l].unsqueeze(1), writes=("qsc",), slow=True)
        P.dma("sp", ksc[:, l:l + 1], dram["L0_ffn_k_scale"][l].unsqueeze(1), writes=("ksc",), slow=True)

    pS0, pS1, pN0 = C["pS0"], C["pS1"], C["pN0"]
    pA, pB = C["pA"], C["pB"]
    for l in range(2):
        build_xe(P, A, C, xi, xe)
        for nm in ["rot_cq", "rot_sq", "rot_ck", "rot_sk"]:
            v = A["ARB"].alloc(nm, T)
            P.dma("sp", v[0:64, :], dram[nm], writes=(nm,))
            rott[nm] = v
        whb = [A["ARB"].alloc(f"WHB{i}", 24 * 64) for i in range(2)]
        qraw = A["ARB"].alloc("QRAW", 512)
        kraw = A["ARB"].alloc("KRAW", 512)
        qfin = A["ARB"].alloc("QFIN", T)
        kfin = A["ARB"].alloc("KFIN", T)
        qtmp = A["ARB"].alloc("QTMP", 512)
        vaug = A["ARB"].alloc("VAU", NT * 65)
        ptb = A["ARB"].alloc("PTB", 2 * 128, keys=["PTB0", "PTB1"])
        ohb = A["ARB"].alloc("OHB", 128)
        for h in range(NH):
            wcur, wck = whb[h % 2], f"WHB{h % 2}"
            for gi, colbase in enumerate((h * 64, 512 + h * 64, 1024 + h * 64)):
                for dm in range(ND):
                    P.dma("sp", wst, dram["L0_ffn_wqkv"][l][dm * 128:(dm + 1) * 128,
                          colbase:colbase + 64], writes=("WST",))
                    P.copy("pool", wcur[:, (gi * 8 + dm) * 64:(gi * 8 + dm + 1) * 64],
                           wst, reads=("WST",), writes=(wck,))
            P.memset("pool", vaug[:], 1.0, writes=("VAU",))
            for tt in range(NT):
                out = pS1[0:128, 0:64]
                for dm in range(ND):
                    xv, xk = xs_views(P, A, xi, xe, dm, (tt * 128, 128))
                    P.mm(out, xv, wcur[:, (16 + dm) * 64:(16 + dm + 1) * 64],
                         start=(dm == 0), stop=(dm == ND - 1), reads=(xk, wck), writes=("pS1",))
                P.act(vaug[:, tt * 65:tt * 65 + 64], out, AF.Copy, reads=("pS1",), writes=("VAU",))
            for gi, raw, rk, fin, fk, sc, sck, cn, sn in (
                    (0, qraw, "QRAW", qfin, "QFIN", qsc, "qsc", "rot_cq", "rot_sq"),
                    (1, kraw, "KRAW", kfin, "KFIN", ksc, "ksc", "rot_ck", "rot_sk")):
                for tc in range(NCH):
                    sl = slice(tc * 512, (tc + 1) * 512)
                    out = pA[0:64, 0:512]
                    for dm in range(ND):
                        xv, xk = xs_views(P, A, xi, xe, dm, (tc * 512, 512))
                        P.mm(out, wcur[:, (gi * 8 + dm) * 64:(gi * 8 + dm + 1) * 64], xv,
                             start=(dm == 0), stop=(dm == ND - 1), reads=(wck, xk), writes=("pA",))
                    rsl = raw[0:64, 0:512]
                    P.act(rsl, out, AF.Copy, reads=("pA",), writes=(rk,))
                    P.act(sqb[0:64, :], rsl, AF.Square, reads=(rk,), writes=("SQB",))
                    P.mm(pB[0:1, 0:512], C["ones_col"][0:64, 0:1], sqb[0:64, :],
                         start=True, stop=True, reads=("SQB", "ones_col"), writes=("pB",))
                    r0 = nrm[0:1, 0:512]
                    P.act(r0, pB[0:1, 0:512], AF.Sqrt, reads=("pB",), writes=("NRM",))
                    P.ts("dve", r0, r0, 1e-12, None, ALU.max, reads=("NRM",), writes=("NRM",))
                    P.emit("dve", lambda E, r0=r0: E.reciprocal(r0, r0), reads=("NRM",), writes=("NRM",))
                    bc = pB[0:64, 0:512]
                    P.mm(bc, C["ones_row"][0:1, 0:64], r0, start=True, stop=True,
                         reads=("NRM", "ones_row"), writes=("pB",))
                    P.tt("dve", qtmp[0:64, :], rsl, bc, ALU.mult, reads=(rk, "pB"), writes=("QTMP",))
                    P.ts("dve", qtmp[0:64, :], qtmp[0:64, :], sc[:, l:l + 1], None, ALU.mult,
                         reads=("QTMP", sck), writes=("QTMP",))
                    rot = pA[0:64, 0:512]
                    P.mm(rot, rott["rotmatT"][:], qtmp[0:64, :], start=True, stop=True,
                         reads=("QTMP", "rotmatT"), writes=("pA",))
                    P.tt("dve", sqb[0:64, :], qtmp[0:64, :], rott[cn][0:64, sl], ALU.mult,
                         reads=("QTMP", cn), writes=("SQB",))
                    P.tt("dve", pmb[0:64, 0:512], rot, rott[sn][0:64, sl], ALU.mult,
                         reads=("pA", sn), writes=("PMB",))
                    P.tt("dve", fin[0:64, sl], sqb[0:64, :], pmb[0:64, 0:512], ALU.add,
                         reads=("SQB", "PMB"), writes=(fk,))
            iters = [(w, it) for w in range(8) for it in range(4)]
            def head_b(idx):
                w, it = iters[idx]
                par = idx % 2
                sps, spk = (pS0, "pS0") if par == 0 else (pS1, "pS1")
                jlo = it if w > 0 else max(it, 4)
                jhi = it + 4
                irng = slice(w * 512 + it * 128, w * 512 + (it + 1) * 128)
                for jc in range(jlo, jhi + 1):
                    jsl = slice((jc - jlo) * 128, (jc - jlo + 1) * 128)
                    gtok = 512 * (w - 1) + 128 * jc
                    P.mm(sps[:, jsl], qfin[0:64, irng], kfin[0:64, gtok:gtok + 128],
                         start=True, stop=True, reads=("QFIN", "KFIN"), writes=(spk,))
            def tail_b(idx):
                w, it = iters[idx]
                par = idx % 2
                sps, spk = (pS0, "pS0") if par == 0 else (pS1, "pS1")
                pmx, pmxk = (pmb, "PMB") if par == 0 else (pef, "PEF")
                pex, pexk = (pef2, "PEF2") if par == 0 else (pef3, "PEF3")
                ptp, ptpk = (pA, "pA") if par == 0 else (pB, "pB")
                jlo = it if w > 0 else max(it, 4)
                jhi = it + 4
                njc = jhi - jlo + 1
                wid = njc * 128
                for jc in range(jlo, jhi + 1):
                    jsl = slice((jc - jlo) * 128, (jc - jlo + 1) * 128)
                    if jc == it:
                        P.tt("dve", pmx[:, jsl], sps[:, jsl], C["maskA"][:], ALU.add,
                             reads=(spk, "maskA"), writes=(pmxk,))
                    elif jc == jhi:
                        P.tt("dve", pmx[:, jsl], sps[:, jsl], C["maskB"][:], ALU.add,
                             reads=(spk, "maskB"), writes=(pmxk,))
                    else:
                        P.copy("dve", pmx[:, jsl], sps[:, jsl], reads=(spk,), writes=(pmxk,))
                mrow = nrm[0:128, par:par + 1]
                P.emit("dve", lambda E, mrow=mrow, w_=wid, pmx=pmx: E.reduce_max(
                    mrow, pmx[:, 0:w_], axis=AX.X), reads=(pmxk,), writes=("NRM",))
                P.ts("dve", mrow, mrow, -8.0, None, ALU.mult, reads=("NRM",), writes=("NRM",))
                P.act(pex[:, 0:wid], pmx[:, 0:wid], AF.Exp, bias=mrow, scale=8.0,
                      reads=(pmxk, "NRM"), writes=(pexk,))
                for jc in range(jlo, jhi + 1):
                    jsl = slice((jc - jlo) * 128, (jc - jlo + 1) * 128)
                    pt = ptp[:, 0:128]
                    P.trans(pt, pex[:, jsl], C["ident"][:], reads=(pexk, "ident"), writes=(ptpk,))
                    slot = (jc % 2)
                    P.act(ptb[:, slot * 128:(slot + 1) * 128], pt, AF.Copy,
                          reads=(ptpk,), writes=(f"PTB{slot}",))
                    tglob = 4 * (w - 1) + jc
                    P.mm(pN0[0:65, 0:128], vaug[:, tglob * 65:tglob * 65 + 65],
                         ptb[:, slot * 128:(slot + 1) * 128],
                         start=(jc == jlo), stop=(jc == jhi),
                         reads=("VAU", f"PTB{slot}"), writes=("pN0",))
                t1 = ncomb[0:65, 0:128]
                P.copy("dve", t1, pN0[0:65, 0:128], reads=("pN0",), writes=("NC1",))
                rr = nrr[0:1, 0:128]
                P.emit("dve", lambda E, rr=rr, t1=t1: E.reciprocal(rr, t1[64:65, :]),
                       reads=("NC1",), writes=("NRR",))
                bc2 = ptp[0:64, 128:256]
                P.mm(bc2, C["ones_row"][0:1, 0:64], rr, start=True, stop=True,
                     reads=("NRR", "ones_row"), writes=(ptpk,))
                P.tt("dve", ohb[0:64, :], t1[0:64, :], bc2, ALU.mult,
                     reads=("NC1", ptpk), writes=("OHB",))
                P.dma("pool", scr["AOs"][h * 64:(h + 1) * 64,
                      w * 512 + it * 128:w * 512 + (it + 1) * 128],
                      ohb[0:64, :], reads=("OHB",), writes=("AOs",))
            for idx in range(32):
                head_b(idx)
                if idx > 0:
                    tail_b(idx - 1)
            tail_b(31)
        for nm in ["WHB0", "WHB1", "QRAW", "KRAW", "QFIN", "KFIN", "QTMP",
                   "VAU", "PTB", "OHB", "rot_cq", "rot_sq", "rot_ck", "rot_sk"]:
            A["ARB"].release(nm)
        P.barrier()
        wo = load_cast_bf(P, A, C, dram["L0_ffn_wo"][l], 512, D, "Wo")
        aostg = A["ARB"].alloc("AOC", 4 * 512, keys=[f"AOC{i}" for i in range(4)])
        for tc in range(NCH):
            for dm in range(4):
                P.dma("sp", aostg[:, dm * 512:(dm + 1) * 512],
                      scr["AOs"][dm * 128:(dm + 1) * 128, tc * 512:(tc + 1) * 512],
                      reads=("AOs",), writes=(f"AOC{dm}",))
            for of in range(ND):
                out = pS1[:, 0:512]
                for dm in range(4):
                    P.mm(out, wo[:, dm * D + of * 128:dm * D + of * 128 + 128],
                         aostg[:, dm * 512:(dm + 1) * 512],
                         start=(dm == 0), stop=(dm == 3), reads=("Wo", f"AOC{dm}"), writes=("pS1",))
                xsl = xi[:, of * T + tc * 512:of * T + (tc + 1) * 512]
                P.tt("dve", xsl, xsl, out, ALU.add, reads=("XI", "pS1"), writes=("XI",))
        A["ARB"].release("Wo")
        A["ARB"].release("AOC")
        P.barrier()
        build_xe(P, A, C, xi, xe)
        lgb = P.es.enter_context(nc.sbuf_tensor(f"b_lgb{l}", [128, 16], F32))
        P.dma("sp", lgb[:, 0:8], dram["L0_ffn_ln_g"][l].rearrange("(a p) -> p a", p=128),
              writes=("lgb",), slow=True)
        P.dma("sp", lgb[:, 8:16], dram["L0_ffn_ln_b"][l].rearrange("(a p) -> p a", p=128),
              writes=("lgb",), slow=True)
        b1a = A["ARF"].alloc("B1A", 22)
        b1g = A["ARF"].alloc("B1G", 22)
        P.dma("sp", b1a[:, 0:21], dram["L0_ffn_b1"][l][0:2688].rearrange("(a p) -> p a", p=128),
              writes=("B1A",), slow=True)
        P.dma("sp", b1a[0:42, 21:22], dram["L0_ffn_b1"][l][2688:2730].unsqueeze(1),
              writes=("B1A",), slow=True)
        P.dma("sp", b1g[:, 0:21], dram["L0_ffn_b1"][l][2730:5418].rearrange("(a p) -> p a", p=128),
              writes=("B1G",), slow=True)
        P.dma("sp", b1g[0:42, 21:22], dram["L0_ffn_b1"][l][5418:5460].unsqueeze(1),
              writes=("B1G",), slow=True)
        b2c = A["ARF"].alloc("B2C", 8)
        P.dma("sp", b2c[:], dram["L0_ffn_b2"][l].rearrange("(a p) -> p a", p=128),
              writes=("B2C",), slow=True)
        zb = A["ARB"].alloc("ZB", ND * 512)
        mch = A["ARB"].alloc("MCH", NJ * 512)
        w1r = [A["ARB"].alloc(f"W1R{i}", 16 * 128) for i in range(2)]
        w2r = [A["ARB"].alloc(f"W2R{i}", NJ * 128) for i in range(2)]
        for tc in range(NCH):
            tiles = []
            for dm in range(ND):
                xv, xk = xs_views(P, A, xi, xe, dm, (tc * 512, 512))
                tiles.append((xv, xk))
            a_row, b_row = ln_stats(P, C, tiles, D)
            ab = pS0[:, 0:512]
            bb = pS0[:, 512:1024]
            P.mm(ab, C["ones_row"][0:1, 0:128], a_row, start=True, stop=True,
                 reads=("st_a", "ones_row"), writes=("pS0",))
            P.mm(bb, C["ones_row"][0:1, 0:128], b_row, start=True, stop=True,
                 reads=("st_b", "ones_row"), writes=("pS0",))
            for dm in range(ND):
                xv, xk = tiles[dm]
                t0 = C["lnsq"]
                P.tt("dve", t0[:], xv, ab, ALU.mult, reads=(xk, "pS0"), writes=("lnsq",))
                P.tt("dve", t0[:], t0[:], bb, ALU.add, reads=("lnsq", "pS0"), writes=("lnsq",))
                P.ts("dve", zb[:, dm * 512:(dm + 1) * 512], t0[:],
                     lgb[:, dm:dm + 1], lgb[:, 8 + dm:9 + dm], ALU.mult, ALU.add,
                     reads=("lnsq", "lgb"), writes=("ZB",))
            for j in range(NJ):
                wj = min(128, HID - j * 128)
                wr, wrk = w1r[j % 2], f"W1R{j % 2}"
                P.dma("sp", wr.rearrange("p (h d c) -> p h d c", h=2, d=8)[:, :, :, 0:wj],
                      scr["w1bf"][l][j].rearrange("h d p c -> p h d c")[:, :, :, 0:wj],
                      reads=("w1bf",), writes=(wrk,))
                pa = pS1[:, 0:512]
                for dm in range(ND):
                    P.mm(pa[0:wj, :] if wj < 128 else pa,
                         wr[:, dm * 128:dm * 128 + wj],
                         zb[:, dm * 512:(dm + 1) * 512],
                         start=(dm == 0), stop=(dm == ND - 1), reads=(wrk, "ZB"), writes=("pS1",))
                pg = pS1[0:wj, 512:1024]
                for dm in range(ND):
                    P.mm(pg, wr[:, (8 + dm) * 128:(8 + dm) * 128 + wj],
                         zb[:, dm * 512:(dm + 1) * 512],
                         start=(dm == 0), stop=(dm == ND - 1), reads=(wrk, "ZB"), writes=("pS1",))
                gact = pmb[0:wj, 0:512]
                P.act(gact, pg, AF.Gelu, bias=b1g[0:wj, j:j + 1], scale=1.0,
                      reads=("pS1", "B1G"), writes=("PMB",))
                P.emit("dve", lambda E, o=mch[0:wj, j * 512:j * 512 + 512],
                       a=(pa[0:wj, :] if wj < 128 else pa),
                       s=b1a[0:wj, j:j + 1], g=gact: E.scalar_tensor_tensor(
                           o, a, s, g, ALU.add, ALU.mult),
                       reads=("pS1", "B1A", "PMB"), writes=("MCH",))
            for of in range(ND):
                wr2, wrk2 = w2r[of % 2], f"W2R{of % 2}"
                P.dma("sp", wr2.rearrange("p (j c) -> p j c", c=128)[:, 0:21, :],
                      scr["w2bf"][l][of][0:21].rearrange("j p c -> p j c"),
                      reads=("w2bf",), writes=(wrk2,))
                P.dma("sp", wr2[0:42, 21 * 128:22 * 128],
                      scr["w2bf"][l][of][21][0:42, :], reads=("w2bf",), writes=(wrk2,))
                out = pS0[:, 0:512] if of % 2 == 0 else pS0[:, 512:1024]
                ok = "pS0"
                for j in range(NJ):
                    wj = min(128, HID - j * 128)
                    P.mm(out, wr2[:, j * 128:j * 128 + 128][0:wj, :],
                         mch[0:wj, j * 512:(j + 1) * 512],
                         start=(j == 0), stop=(j == NJ - 1), reads=(wrk2, "MCH"), writes=(ok,))
                xsl = xi[:, of * T + tc * 512:of * T + (tc + 1) * 512]
                P.emit("dve", lambda E, o=xsl, a=out, s=b2c[:, of:of + 1], g=xsl:
                       E.scalar_tensor_tensor(o, a, s, g, ALU.add, ALU.add),
                       reads=(ok, "B2C", "XI"), writes=("XI",))
        A["ARF"].release("B1A")
        A["ARF"].release("B1G")
        A["ARF"].release("B2C")
        A["ARB"].release("ZB")
        A["ARB"].release("MCH")
        A["ARB"].release("W1R0")
        A["ARB"].release("W1R1")
        A["ARB"].release("W2R0")
        A["ARB"].release("W2R1")
        P.barrier()
    rc = A["ARF"].alloc("RCs", ND * 512, keys=[f"RC{i}" for i in range(ND)])
    for tc in range(NCH):
        ztiles = [(xi[:, dm * T + tc * 512:dm * T + (tc + 1) * 512], "XI") for dm in range(ND)]
        sandwich_update(P, A, C, ztiles, rin, rout, tc, rc)
    A["ARF"].release("RCs")
    for nm in ["XI", "XE"]:
        A["ARB"].release(nm)
    for nm in ["WST", "PMB", "PEF", "PEF2", "PEF3", "SQB", "NRM", "NCB", "NRR"]:
        A["ARF"].release(nm)
    P.barrier()

def emit_stage_c(P, A, C, dram, rin, rout, scr):
    nc = P.nc
    NH = 16
    gb = P.es.enter_context(nc.sbuf_tensor("c_gb", [128, NT], F32))
    P.dma("sp", gb[:], dram["L1_attn_pos_rev"], writes=("GB",))
    wb = P.es.enter_context(nc.sbuf_tensor("c_wb", [64, 1], F32))
    P.dma("sp", wb[:], dram["L1_attn_w_pos_rev"], writes=("WB",))
    bvr = P.es.enter_context(nc.sbuf_tensor("c_bvr", [1, D], F32))
    P.dma("sp", bvr[:], dram["L1_attn_b_v"].unsqueeze(0), writes=("BVR",))
    xn = A["ARB"].alloc("XN", ND * T)
    compute_xn(P, A, C, rin, xn, "XN")
    whb = A["ARB"].alloc("WHB", 40 * 64)
    bcolv = A["ARF"].alloc("BCOL", 4)
    wst = A["ARF"].alloc("WST", 64)
    qh = A["ARB"].alloc("WQH", T)
    kh = A["ARB"].alloc("WKH", T)
    qg = A["ARB"].alloc("GQH", T)
    kg = A["ARB"].alloc("GKH", T)
    vaug = A["ARB"].alloc("VAU", NT * 65)
    vaug64 = A["ARB"].alloc("VA64", NT * 65)
    owh = A["ARB"].alloc("OWH", T)
    pwb = A["ARB"].alloc("PWB", 512)
    pb = [A["ARB"].alloc(f"PB{i}", 1024) for i in range(2)]
    ohb = A["ARB"].alloc("OHB", 1024)
    ncomb = A["ARF"].alloc("NCOMB", 1024, keys=["NC1"])
    nrr = A["ARF"].alloc("NRR", 1024, keys=["NRR"])
    pS0, pS1, pN0 = C["pS0"], C["pS1"], C["pN0"]
    pA, pB = C["pA"], C["pB"]
    for h in range(NH):
        for gi, (wn, cb) in enumerate([("L1_attn_w_wqk", h * 64), ("L1_attn_w_wqk", D + h * 64),
                                       ("L1_attn_w_qk", h * 64), ("L1_attn_w_qk", D + h * 64),
                                       ("L1_attn_w_v", h * 64)]):
            for dm in range(ND):
                P.dma("sp", wst, dram[wn][dm * 128:(dm + 1) * 128, cb:cb + 64], writes=("WST",))
                P.copy("pool", whb[:, (gi * 8 + dm) * 64:(gi * 8 + dm + 1) * 64],
                       wst, reads=("WST",), writes=("WHB",))
        for gi, (bn, off) in enumerate([("L1_attn_b_wqk", h * 64), ("L1_attn_b_wqk", D + h * 64),
                                        ("L1_attn_b_qk", h * 64), ("L1_attn_b_qk", D + h * 64)]):
            P.dma("sp", bcolv[0:64, gi:gi + 1], dram[bn][off:off + 64].unsqueeze(1),
                  writes=("BCOL",), slow=True)
        for gi, dst, dk in ((0, qh, "WQH"), (1, kh, "WKH"), (2, qg, "GQH"), (3, kg, "GKH")):
            for tc in range(NCH):
                out = pA[0:64, 0:512]
                for dm in range(ND):
                    P.mm(out, whb[:, (gi * 8 + dm) * 64:(gi * 8 + dm + 1) * 64],
                         xn[:, dm * T + tc * 512:dm * T + (tc + 1) * 512],
                         start=(dm == 0), stop=(dm == ND - 1), reads=("WHB", "XN"), writes=("pA",))
                P.act(dst[0:64, tc * 512:(tc + 1) * 512], out, AF.Identity,
                      bias=bcolv[0:64, gi:gi + 1], reads=("pA", "BCOL"), writes=(dk,))
        P.memset("pool", vaug[:], 1.0, writes=("VAU",))
        P.memset("pool", vaug64[:], 1.0, writes=("VA64",))
        for tt in range(NT):
            out = pS1[0:128, 0:64]
            for dm in range(ND):
                P.mm(out, xn[:, dm * T + tt * 128:dm * T + (tt + 1) * 128],
                     whb[:, (32 + dm) * 64:(32 + dm + 1) * 64],
                     start=(dm == 0), stop=False, reads=("XN", "WHB"), writes=("pS1",))
            P.mm(out, C["ones_row"][0:1, 0:128], bvr[0:1, h * 64:(h + 1) * 64],
                 start=False, stop=True, reads=("ones_row", "BVR"), writes=("pS1",))
            P.act(vaug[:, tt * 65:tt * 65 + 64], out, AF.Copy, reads=("pS1",), writes=("VAU",))
        for tt in range(NT):
            P.dma("pool", vaug64[0:64, tt * 65:tt * 65 + 65],
                  vaug[:, tt * 65:tt * 65 + 65][64:128, :], reads=("VAU",), writes=("VA64",))
        for nb in range(8):
            sp = pS0[0:64, 0:512]
            for wloc in range(8):
                n = nb * 8 + wloc
                P.mm(sp[:, wloc * 64:(wloc + 1) * 64],
                     kh[0:64, n * 64:(n + 1) * 64], qh[0:64, n * 64:(n + 1) * 64],
                     start=True, stop=True, reads=("WKH", "WQH"), writes=("pS0",))
            P.act(pwb[0:64, :], sp, AF.Exp, bias=wb[:], scale=0.125,
                  reads=("pS0", "WB"), writes=("PWB",))
            nps = pN0[0:65, 0:512]
            for wloc in range(8):
                n = nb * 8 + wloc
                tt = n // 2
                if n % 2 == 0:
                    vsl, vk = vaug[:, tt * 65:tt * 65 + 65][0:64, :], "VAU"
                else:
                    vsl, vk = vaug64[0:64, tt * 65:tt * 65 + 65], "VA64"
                P.mm(nps[:, wloc * 64:(wloc + 1) * 64], vsl,
                     pwb[0:64, wloc * 64:(wloc + 1) * 64],
                     start=True, stop=True, reads=(vk, "PWB"), writes=("pN0",))
            t1 = ncomb[0:65, 0:512]
            P.copy("dve", t1, nps, reads=("pN0",), writes=("NC1",))
            rr = nrr[0:1, 0:512]
            P.emit("dve", lambda E, rr=rr, t1=t1: E.reciprocal(rr, t1[64:65, :]),
                   reads=("NC1",), writes=("NRR",))
            bc = pB[0:64, 0:512]
            P.mm(bc, C["ones_row"][0:1, 0:64], rr, start=True, stop=True,
                 reads=("NRR", "ones_row"), writes=("pB",))
            P.tt("dve", owh[0:64, nb * 512:(nb + 1) * 512], t1[0:64, :], bc, ALU.mult,
                 reads=("NC1", "pB"), writes=("OWH",))
        for qcb in range(4):
            def head_c(kt, qcb=qcb):
                sps = "pS0" if kt % 2 == 0 else "pS1"
                S = C[sps]
                for half in range(2):
                    P.mm(S[:, half * 512:(half + 1) * 512],
                         kg[0:64, kt * 128:(kt + 1) * 128],
                         qg[0:64, qcb * 1024 + half * 512:qcb * 1024 + (half + 1) * 512],
                         start=True, stop=True, reads=("GKH", "GQH"), writes=(sps,))
            def tail_c(kt, qcb=qcb):
                sps = "pS0" if kt % 2 == 0 else "pS1"
                S = C[sps]
                pP, pPk = pb[kt % 2], f"PB{kt % 2}"
                P.act(pP, S[:], AF.Exp, bias=gb[:, kt:kt + 1], scale=1.0,
                      reads=(sps, "GB"), writes=(pPk,))
                for half in range(2):
                    sl = slice(half * 512, (half + 1) * 512)
                    P.mm(pN0[:, sl], vaug[:, kt * 65:kt * 65 + 65], pP[:, sl],
                         start=(kt == 0), stop=(kt == NT - 1), reads=("VAU", pPk), writes=("pN0",))
            for kt in range(NT):
                head_c(kt)
                if kt > 0:
                    tail_c(kt - 1)
            tail_c(NT - 1)
            t1 = ncomb[0:65, 0:1024]
            P.copy("dve", t1, pN0[:], reads=("pN0",), writes=("NC1",))
            rr = nrr[0:1, 0:1024]
            P.emit("dve", lambda E, rr=rr, t1=t1: E.reciprocal(rr, t1[64:65, :]),
                   reads=("NC1",), writes=("NRR",))
            bc = pS0[0:64, 0:1024]
            for half in range(2):
                P.mm(bc[:, half * 512:(half + 1) * 512], C["ones_row"][0:1, 0:64],
                     rr[:, half * 512:(half + 1) * 512], start=True, stop=True,
                     reads=("NRR", "ones_row"), writes=("pS0",))
            P.tt("dve", ohb[0:64, :], t1[0:64, :], bc, ALU.mult, reads=("NC1", "pS0"), writes=("OHB",))
            qsl = slice(qcb * 1024, (qcb + 1) * 1024)
            P.tt("dve", ohb[0:64, :], ohb[0:64, :], owh[0:64, qsl], ALU.add,
                 reads=("OHB", "OWH"), writes=("OHB",))
            P.dma("pool", scr["AOs"][h * 64:(h + 1) * 64, qsl], ohb[0:64, :],
                  reads=("OHB",), writes=("AOs",))
    for nm in ["WHB", "WQH", "WKH", "GQH", "GKH", "VAU", "VA64", "OWH", "PWB",
               "PB0", "PB1", "OHB"]:
        A["ARB"].release(nm)
    A["ARF"].release("NCOMB")
    A["ARF"].release("NRR")
    A["ARF"].release("BCOL")
    A["ARF"].release("WST")
    A["ARB"].release("XN")
    P.barrier()
    wout = load_cast_bf(P, A, C, dram["L1_attn_w_out"], D, D, "WoutC")
    boc = A["ARF"].alloc("BOC", 8)
    P.dma("sp", boc[:], dram["L1_attn_b_out"].rearrange("(a p) -> p a", p=128),
          writes=("BOC",), slow=True)
    aostg = A["ARB"].alloc("AOC", ND * 512, keys=[f"AOC{i}" for i in range(ND)])
    zc = A["ARF"].alloc("ZC", ND * 512, keys=[f"ZC{i}" for i in range(ND)])
    rc = A["ARF"].alloc("RCs", ND * 512, keys=[f"RC{i}" for i in range(ND)])
    for tc in range(NCH):
        for dm in range(ND):
            P.dma("sp", aostg[:, dm * 512:(dm + 1) * 512],
                  scr["AOs"][dm * 128:(dm + 1) * 128, tc * 512:(tc + 1) * 512],
                  reads=("AOs",), writes=(f"AOC{dm}",))
        ztiles = []
        for of in range(ND):
            out = C["pS1"][:, 0:512]
            for dm in range(ND):
                P.mm(out, wout[:, dm * D + of * 128:dm * D + of * 128 + 128],
                     aostg[:, dm * 512:(dm + 1) * 512],
                     start=(dm == 0), stop=(dm == ND - 1), reads=("WoutC", f"AOC{dm}"),
                     writes=("pS1",))
            zv = zc[:, of * 512:(of + 1) * 512]
            P.act(zv, out, AF.Identity, bias=boc[:, of:of + 1], reads=("pS1", "BOC"),
                  writes=(f"ZC{of}",))
            ztiles.append((zv, f"ZC{of}"))
        sandwich_update(P, A, C, ztiles, rin, rout, tc, rc)
    A["ARB"].release("WoutC")
    A["ARB"].release("AOC")
    A["ARF"].release("ZC")
    A["ARF"].release("RCs")
    A["ARF"].release("BOC")
    P.barrier()

def emit_stage_d(P, A, C, dram, rin, rout, scr):
    nc = P.nc
    E, F, NF = 8, 4096, 32
    CAP = 768
    NST = CAP // 128
    cts = P.es.enter_context(nc.sbuf_tensor("d_cts", [128, 64], F32))
    P.dma("sp", cts[:], dram["L1_ffn_centT"], writes=("CTS",))
    icol = P.es.enter_context(nc.sbuf_tensor("d_icol", [128, 1], F32))
    P.dma("sp", icol[:], dram["iotacol"], writes=("ICOL",))
    b1ct = P.es.enter_context(nc.sbuf_tensor("d_b1ct", [128, 256], F32))
    P.dma("sp", b1ct[:], dram["L1_ffn_b1c"], writes=("B1CT",))
    b2ct = P.es.enter_context(nc.sbuf_tensor("d_b2ct", [128, 64], F32))
    P.dma("sp", b2ct[:], dram["L1_ffn_b2c"], writes=("B2CT",))
    g8s = P.es.enter_context(nc.sbuf_tensor("d_g8s", [8, D], F32))
    P.dma("sp", g8s[:], dram["L1_ffn_g8"], writes=("G8S",))
    b8s = P.es.enter_context(nc.sbuf_tensor("d_b8s", [8, D], F32))
    P.dma("sp", b8s[:], dram["L1_ffn_b8"], writes=("B8S",))
    g8 = P.es.enter_context(nc.sbuf_tensor("d_g8", [8, D], BF16))
    P.copy("pool", g8[:], g8s[:], reads=("G8S",), writes=("G8",))
    b8 = P.es.enter_context(nc.sbuf_tensor("d_b8", [8, D], BF16))
    P.copy("pool", b8[:], b8s[:], reads=("B8S",), writes=("B8",))
    for e in range(E):
        stg = A["ARF"].alloc("wts", 4096)
        stgb = A["ARB"].alloc("wtsb", 4096)
        for r in range(ND):
            P.dma("sp", stg, dram["L1_ffn_W1T"][e][r * 128:(r + 1) * 128, :], writes=("wts",))
            P.copy("pool", stgb, stg, reads=("wts",), writes=("wtsb",))
            P.dma("pool", scr["W1Tbf"][e][:, r].rearrange("f p c -> p f c"),
                  stgb.rearrange("p (f c) -> p f c", c=128),
                  reads=("wtsb",), writes=("W1Tbf",))
        A["ARF"].release("wts")
        A["ARB"].release("wtsb")
        stg = A["ARF"].alloc("wts", 1024)
        stgb = A["ARB"].alloc("wtsb", 1024)
        for r in range(NF):
            P.dma("sp", stg, dram["L1_ffn_W2T"][e][r * 128:(r + 1) * 128, :], writes=("wts",))
            P.copy("pool", stgb, stg, reads=("wts",), writes=("wtsb",))
            P.dma("pool", scr["W2Tbf"][e][:, r].rearrange("o p c -> p o c"),
                  stgb.rearrange("p (o c) -> p o c", c=128),
                  reads=("wtsb",), writes=("W2Tbf",))
        A["ARF"].release("wts")
        A["ARB"].release("wtsb")

    pS0, pS1, pN0 = C["pS0"], C["pS1"], C["pN0"]
    pA, pB = C["pA"], C["pB"]
    zsel = A["ARB"].alloc("ZSEL", NT * 1024)
    mfm = A["ARB"].alloc("MFM", T)
    alr = A["ARB"].alloc("ALR", T)
    s2r = A["ARB"].alloc("S2R", T)
    mu2r = A["ARB"].alloc("MU2R", T)
    sxu = P.es.enter_context(nc.sbuf_tensor("d_sxu", [128, 32], mybir.dt.uint32))
    sxf = P.es.enter_context(nc.sbuf_tensor("d_sxf", [128, 32], F32))
    alc = A["ARF"].alloc("ALC", 4)
    alt4 = A["ARF"].alloc("ALT4", 128)
    mxv = A["ARF"].alloc("MXV", 1)
    mtm = A["ARF"].alloc("MTM", 8)
    zsf = A["ARF"].alloc("ZSF", 512)
    rc = A["ARF"].alloc("RCs", ND * 512, keys=[f"RC{i}" for i in range(ND)])
    xhc = A["ARB"].alloc("XHC", ND * 512)
    fcb = A["ARB"].alloc("FCB", ND * 512)
    for tc in range(NCH):
        fc = A["ARF"].alloc("FC", ND * 512, keys=[f"FC{i}" for i in range(ND)])
        rtiles = stream_rin_chunk(P, A, C, rin, rc, tc)
        a_row, b_row = ln_stats(P, C, rtiles, D)
        ab, bb = pS0[:, 0:512], pS0[:, 512:1024]
        P.mm(ab, C["ones_row"][0:1, 0:128], a_row, start=True, stop=True,
             reads=("st_a", "ones_row"), writes=("pS0",))
        P.mm(bb, C["ones_row"][0:1, 0:128], b_row, start=True, stop=True,
             reads=("st_b", "ones_row"), writes=("pS0",))
        ftiles = []
        for dm in range(ND):
            rap, rkey = rtiles[dm]
            fv = fc[:, dm * 512:(dm + 1) * 512]
            P.tt("dve", C["lnsq"][:], rap, ab, ALU.mult, reads=(rkey, "pS0"), writes=("lnsq",))
            P.tt("dve", fv, C["lnsq"][:], bb, ALU.add, reads=("lnsq", "pS0"), writes=(f"FC{dm}",))
            P.copy("dve", fcb[:, dm * 512:(dm + 1) * 512], fv, reads=(f"FC{dm}",), writes=("FCB",))
            ftiles.append((fv, f"FC{dm}"))
        for t4 in range(4):
            sc = pS1[0:128, 0:8]
            for dm in range(ND):
                P.mm(sc, fc[:, dm * 512 + t4 * 128:dm * 512 + (t4 + 1) * 128],
                     cts[:, dm * 8:(dm + 1) * 8],
                     start=(dm == 0), stop=(dm == ND - 1), reads=(f"FC{dm}", "CTS"),
                     writes=("pS1",))
            mv = mxv[0:128, 0:1]
            P.emit("dve", lambda E_, mv=mv, sc=sc: E_.reduce_max(mv, sc, axis=AX.X),
                   reads=("pS1",), writes=("MXV",))
            P.emit("dve", lambda E_, sc=sc, mv=mv: E_.tensor_scalar(
                mtm[0:128, 0:8], sc, mv, None, ALU.is_ge),
                reads=("pS1", "MXV"), writes=("MTM",))
            P.act(alc[0:128, t4:t4 + 1], mv, AF.Sigmoid, reads=("MXV",), writes=("ALC",))
            pt = pA[0:8, 0:128]
            P.trans(pt, mtm[0:128, 0:8], C["ident"][:], reads=("MTM", "ident"), writes=("pA",))
            P.copy("dve", mfm[0:8, tc * 512 + t4 * 128:tc * 512 + (t4 + 1) * 128], pt,
                   reads=("pA",), writes=("MFM",))
        pt = pA[0:4, 128:256]
        P.trans(pt, alc[0:128, 0:4], C["ident"][:], reads=("ALC", "ident"), writes=("pA",))
        P.copy("dve", alt4[0:4, 0:128], pt, reads=("pA",), writes=("ALT4",))
        for t4 in range(4):
            P.dma("pool", alr[0:1, tc * 512 + t4 * 128:tc * 512 + (t4 + 1) * 128],
                  alt4[t4:t4 + 1, 0:128], reads=("ALT4",), writes=("ALR",))
        a2, b2 = ln_stats(P, C, ftiles, D)
        P.emit("dve", lambda E_, a2=a2: E_.reciprocal(C["stat"][0:1, 0:512], a2),
               reads=("st_a",), writes=("st_c",))
        P.copy("dve", s2r[0:1, tc * 512:(tc + 1) * 512], C["stat"][0:1, 0:512],
               reads=("st_c",), writes=("S2R",))
        P.tt("dve", C["stat"][0:1, 0:512], b2, C["stat"][0:1, 0:512], ALU.mult,
             reads=("st_b", "st_c"), writes=("st_c",))
        P.ts("dve", mu2r[0:1, tc * 512:(tc + 1) * 512], C["stat"][0:1, 0:512], -1.0, None,
             ALU.mult, reads=("st_c",), writes=("MU2R",))
        ab2, bb2 = pS0[:, 0:512], pS0[:, 512:1024]
        P.mm(ab2, C["ones_row"][0:1, 0:128], a2, start=True, stop=True,
             reads=("st_a", "ones_row"), writes=("pS0",))
        P.mm(bb2, C["ones_row"][0:1, 0:128], b2, start=True, stop=True,
             reads=("st_b", "ones_row"), writes=("pS0",))
        for dm in range(ND):
            fv = fc[:, dm * 512:(dm + 1) * 512]
            P.tt("dve", C["lnsq"][:], fv, ab2, ALU.mult, reads=(f"FC{dm}", "pS0"), writes=("lnsq",))
            P.tt("dve", xhc[:, dm * 512:(dm + 1) * 512], C["lnsq"][:], bb2, ALU.add,
                 reads=("lnsq", "pS0"), writes=("XHC",))
            P.dma("pool", scr["XH"][dm * 128:(dm + 1) * 128, tc * 512:(tc + 1) * 512],
                  xhc[:, dm * 512:(dm + 1) * 512], reads=("XHC",), writes=("XH",))
        for dm in range(ND):
            gse = pA[:, 0:512]
            P.mm(gse, g8[0:8, dm * 128:(dm + 1) * 128], mfm[0:8, tc * 512:(tc + 1) * 512],
                 start=True, stop=True, reads=("G8", "MFM"), writes=("pA",))
            bse = pB[:, 0:512]
            P.mm(bse, b8[0:8, dm * 128:(dm + 1) * 128], mfm[0:8, tc * 512:(tc + 1) * 512],
                 start=True, stop=True, reads=("B8", "MFM"), writes=("pB",))
            P.tt("dve", C["lnsq"][:], xhc[:, dm * 512:(dm + 1) * 512], gse, ALU.mult,
                 reads=("XHC", "pA"), writes=("lnsq",))
            P.tt("dve", zsf[:], C["lnsq"][:], bse, ALU.add, reads=("lnsq", "pB"), writes=("ZSF",))
            for t4 in range(4):
                pt = pS1[:, 0:128]
                P.trans(pt, zsf[:, t4 * 128:(t4 + 1) * 128], C["ident"][:],
                        reads=("ZSF", "ident"), writes=("pS1",))
                P.act(zsel[:, (tc * 4 + t4) * 1024 + dm * 128:(tc * 4 + t4) * 1024 + (dm + 1) * 128],
                      pt, AF.Copy, reads=("pS1",), writes=("ZSEL",))
        A["ARF"].release("FC")
    scn = A["ARF"].alloc("SCN", T)
    rnk = A["ARF"].alloc("RNK", T)
    rowa = A["ARF"].alloc("ROWA", 512)
    rowb = A["ARF"].alloc("ROWB", 512)
    rowc = A["ARF"].alloc("ROWC", 512)
    slr = A["ARF"].alloc("SLR", T)
    P.memset("dve", scn[0:8, :], 1.0, writes=("SCN",))
    P.emit("dve", lambda E_: E_.tensor_tensor_scan(rnk[0:8, :], scn[0:8, :], mfm[0:8, :],
           0.0, ALU.mult, ALU.add), reads=("SCN", "MFM"), writes=("RNK",))
    P.tt("dve", scn[0:8, :], rnk[0:8, :], mfm[0:8, :], ALU.mult, reads=("RNK", "MFM"), writes=("SCN",))
    for ch in range(NCH):
        sl = slice(ch * 512, (ch + 1) * 512)
        P.mm(pN0[0:1, 0:512], C["ones_col"][0:8, 0:1], scn[0:8, sl], start=True, stop=True,
             reads=("SCN", "ones_col"), writes=("pN0",))
        P.copy("dve", rowa[0:1, :], pN0[0:1, 0:512], reads=("pN0",), writes=("ROWA",))
        P.ts("dve", C["lnsq"][0:8, :], mfm[0:8, sl], icol[0:8, 0:1], None, ALU.mult,
             reads=("MFM", "ICOL"), writes=("lnsq",))
        P.mm(pN0[0:1, 512:1024], C["ones_col"][0:8, 0:1], C["lnsq"][0:8, 0:512],
             start=True, stop=True, reads=("lnsq", "ones_col"), writes=("pN0",))
        P.emit("dve", lambda E_, sl=sl: E_.tensor_scalar(rowb[0:1, :], pN0[0:1, 512:1024],
               float(CAP), None, ALU.mult), reads=("pN0",), writes=("ROWB",))
        P.ts("dve", rowc[0:1, :], rowa[0:1, :], float(CAP), None, ALU.is_gt,
             reads=("ROWA",), writes=("ROWC",))
        P.ts("dve", rowc[0:1, :], rowc[0:1, :], 1e9, None, ALU.mult, reads=("ROWC",), writes=("ROWC",))
        P.tt("dve", rowb[0:1, :], rowb[0:1, :], rowa[0:1, :], ALU.add,
             reads=("ROWB", "ROWA"), writes=("ROWB",))
        P.ts("dve", rowb[0:1, :], rowb[0:1, :], -1.0, None, ALU.add, reads=("ROWB",), writes=("ROWB",))
        P.tt("dve", slr[0:1, sl], rowb[0:1, :], rowc[0:1, :], ALU.add,
             reads=("ROWB", "ROWC"), writes=("SLR",))
    for tt in range(NT):
        pt = pA[0:128, 0:1]
        P.trans(pt, slr[0:1, tt * 128:(tt + 1) * 128], C["ident"][:],
                reads=("SLR", "ident"), writes=("pA",))
        P.act(sxf[:, tt:tt + 1], pt, AF.Copy, reads=("pA",), writes=("SXF",))
    P.copy("pool", sxu[:], sxf[:], reads=("SXF",), writes=("SXU",))
    zero = A["ARB"].alloc("ZERO", 1024)
    P.memset("dve", zero[:], 0.0, writes=("ZERO",))
    for r in range(6144 // 128):
        P.dma("sp", scr["zperm"][r * 128:(r + 1) * 128, :], zero[:], reads=("ZERO",),
              writes=("zperm",))
    for tt in range(NT):
        def scat(E_, sem, tt=tt):
            E_.indirect_dma_start(
                scr["zperm"], bass.IndirectOffsetOnAxis(ap=sxu[:, tt:tt + 1], axis=0),
                zsel[:, tt * 1024:(tt + 1) * 1024], None,
                oob_is_err=False, bounds_check=E * CAP - 1).then_inc(sem, 16)
        P.sw_dma(scat, reads=("SXU", "ZSEL", "zperm"), writes=("zperm",))
    A["ARB"].release("ZSEL")
    A["ARB"].release("ZERO")
    P.barrier()
    zpt = A["ARB"].alloc("ZPT", 1024)
    zfm = A["ARB"].alloc("ZFM", ND * CAP)
    rlf = A["ARB"].alloc("RLF", NF * CAP)
    ypt = A["ARB"].alloc("YPT", 1024)
    w1r = [A["ARB"].alloc(f"W1R{i}", 8 * 128) for i in range(2)]
    w2r = [A["ARB"].alloc(f"W2R{i}", 32 * 128) for i in range(2)]
    zpf = A["ARF"].alloc("ZPF", 1024)
    yfm = A["ARF"].alloc("YFM", ND * CAP)
    for e in range(E):
        for st in range(NST):
            P.dma("sp", zpt[:], scr["zperm"][e * CAP + st * 128:e * CAP + (st + 1) * 128, :],
                  reads=("zperm",), writes=("ZPT",))
            P.copy("dve", zpf[:], zpt[:], reads=("ZPT",), writes=("ZPF",))
            for dm in range(ND):
                pt = pA[:, 0:128] if dm % 2 == 0 else pB[:, 0:128]
                ptk = "pA" if dm % 2 == 0 else "pB"
                P.trans(pt, zpf[:, dm * 128:(dm + 1) * 128], C["ident"][:],
                        reads=("ZPF", "ident"), writes=(ptk,))
                P.act(zfm[:, dm * CAP + st * 128:dm * CAP + (st + 1) * 128], pt, AF.Copy,
                      reads=(ptk,), writes=("ZFM",))
        for ft in range(NF):
            wr, wrk = w1r[ft % 2], f"W1R{ft % 2}"
            P.dma("sp", wr.rearrange("p (d c) -> p d c", c=128),
                  scr["W1Tbf"][e][ft].rearrange("d p c -> p d c"),
                  reads=("W1Tbf",), writes=(wrk,))
            hp, hpk = (pS0, "pS0") if ft % 2 == 0 else (pS1, "pS1")
            for dm in range(ND):
                for half, (c0, cw) in enumerate(((0, 512), (512, CAP - 512))):
                    P.mm(hp[:, c0:c0 + cw], wr[:, dm * 128:(dm + 1) * 128],
                         zfm[:, dm * CAP + c0:dm * CAP + c0 + cw],
                         start=(dm == 0), stop=(dm == ND - 1), reads=(wrk, "ZFM"),
                         writes=(hpk,))
            P.act(rlf[:, ft * CAP:(ft + 1) * CAP], hp[:, 0:CAP], AF.Relu,
                  bias=b1ct[:, e * 32 + ft:e * 32 + ft + 1], reads=(hpk, "B1CT"), writes=("RLF",))
        for of in range(ND):
            wr2, wrk2 = w2r[of % 2], f"W2R{of % 2}"
            P.dma("sp", wr2.rearrange("p (f c) -> p f c", c=128),
                  scr["W2Tbf"][e][of].rearrange("f p c -> p f c"),
                  reads=("W2Tbf",), writes=(wrk2,))
            yp, ypk = (pS0, "pS0") if of % 2 == 0 else (pS1, "pS1")
            for ft in range(NF):
                for half, (c0, cw) in enumerate(((0, 512), (512, CAP - 512))):
                    P.mm(yp[:, c0:c0 + cw], wr2[:, ft * 128:(ft + 1) * 128],
                         rlf[:, ft * CAP + c0:ft * CAP + c0 + cw],
                         start=(ft == 0), stop=(ft == NF - 1), reads=(wrk2, "RLF"),
                         writes=(ypk,))
            P.act(yfm[:, of * CAP:(of + 1) * CAP], yp[:, 0:CAP], AF.Identity,
                  bias=b2ct[:, e * 8 + of:e * 8 + of + 1], reads=(ypk, "B2CT"), writes=("YFM",))
        for st in range(NST):
            for dm in range(ND):
                pt = pA[:, 0:128] if dm % 2 == 0 else pB[:, 0:128]
                ptk = "pA" if dm % 2 == 0 else "pB"
                P.trans(pt, yfm[:, dm * CAP + st * 128:dm * CAP + (st + 1) * 128],
                        C["ident"][:], reads=("YFM", "ident"), writes=(ptk,))
                P.act(ypt[:, dm * 128:(dm + 1) * 128], pt, AF.Copy, reads=(ptk,), writes=("YPT",))
            P.dma("pool", scr["yperm"][e * CAP + st * 128:e * CAP + (st + 1) * 128, :],
                  ypt[:], reads=("YPT",), writes=("yperm",))
    for nm in ["ZPT", "ZFM", "RLF", "YPT", "W1R0", "W1R1", "W2R0", "W2R1"]:
        A["ARB"].release(nm)
    A["ARF"].release("ZPF")
    A["ARF"].release("YFM")
    P.barrier()
    ygb = A["ARB"].alloc("YGB", 1024)
    ygf = A["ARF"].alloc("YGF", 1024)
    ych = A["ARB"].alloc("YCH", ND * 512)
    zc = A["ARF"].alloc("ZC", ND * 512, keys=[f"ZC{i}" for i in range(ND)])
    for tc in range(NCH):
        for t4 in range(4):
            tt = tc * 4 + t4
            P.memset("dve", ygb[:], 0.0, writes=("YGB",))
            def gath(E_, sem, tt=tt):
                E_.indirect_dma_start(
                    ygb[:], None, scr["yperm"],
                    bass.IndirectOffsetOnAxis(ap=sxu[:, tt:tt + 1], axis=0),
                    oob_is_err=False, bounds_check=E * CAP - 1).then_inc(sem, 16)
            P.sw_dma(gath, reads=("SXU", "yperm", "YGB"), writes=("YGB",))
            P.copy("dve", ygf[:], ygb[:], reads=("YGB",), writes=("YGF",))
            for dm in range(ND):
                pt = pA[:, 0:128] if dm % 2 == 0 else pB[:, 0:128]
                ptk = "pA" if dm % 2 == 0 else "pB"
                P.trans(pt, ygf[:, dm * 128:(dm + 1) * 128], C["ident"][:],
                        reads=("YGF", "ident"), writes=(ptk,))
                P.act(ych[:, dm * 512 + t4 * 128:dm * 512 + (t4 + 1) * 128], pt, AF.Copy,
                      reads=(ptk,), writes=("YCH",))
        for dm in range(ND):
            P.dma("sp", xhc[:, dm * 512:(dm + 1) * 512],
                  scr["XH"][dm * 128:(dm + 1) * 128, tc * 512:(tc + 1) * 512],
                  reads=("XH",), writes=("XHC",))
        s2b, mu2b = pS0[:, 0:512], pS0[:, 512:1024]
        alb = pS1[:, 0:512]
        P.mm(s2b, C["ones_row_bf"][0:1, 0:128], s2r[0:1, tc * 512:(tc + 1) * 512],
             start=True, stop=True, reads=("S2R", "ones_row_bf"), writes=("pS0",))
        P.mm(mu2b, C["ones_row_bf"][0:1, 0:128], mu2r[0:1, tc * 512:(tc + 1) * 512],
             start=True, stop=True, reads=("MU2R", "ones_row_bf"), writes=("pS0",))
        P.mm(alb, C["ones_row_bf"][0:1, 0:128], alr[0:1, tc * 512:(tc + 1) * 512],
             start=True, stop=True, reads=("ALR", "ones_row_bf"), writes=("pS1",))
        ztiles = []
        for dm in range(ND):
            zv = zc[:, dm * 512:(dm + 1) * 512]
            P.tt("dve", C["lnsq"][:], xhc[:, dm * 512:(dm + 1) * 512], s2b, ALU.mult,
                 reads=("XHC", "pS0"), writes=("lnsq",))
            P.tt("dve", C["lnsq"][:], C["lnsq"][:], mu2b, ALU.add, reads=("lnsq", "pS0"),
                 writes=("lnsq",))
            P.tt("dve", zv, ych[:, dm * 512:(dm + 1) * 512], alb, ALU.mult,
                 reads=("YCH", "pS1"), writes=(f"ZC{dm}",))
            P.tt("dve", zv, zv, C["lnsq"][:], ALU.add, reads=(f"ZC{dm}", "lnsq"),
                 writes=(f"ZC{dm}",))
            ztiles.append((zv, f"ZC{dm}"))
        sandwich_update(P, A, C, ztiles, rin, rout, tc, rc)
    for nm in ["YGB", "YCH", "XHC", "FCB", "MFM", "ALR", "S2R", "MU2R"]:
        A["ARB"].release(nm)
    for nm in ["YGF", "ZC", "ALC", "ALT4", "MXV", "MTM", "ZSF", "RCs", "SCN", "RNK",
               "ROWA", "ROWB", "ROWC", "SLR"]:
        A["ARF"].release(nm)
    P.barrier()



ARB_COLS = 71680
ARF_COLS = 12288

def prog_setup(nc, es, const_dram):
    P = Prog(nc, es)
    A = {"ARB": Arena(P, "ARB", ARB_COLS, BF16), "ARF": Arena(P, "ARF", ARF_COLS, F32)}
    C = {}
    for name, shape, dt in [
        ("ident", [128, 128], F32), ("ones_row", [1, 512], F32),
        ("ones_row_bf", [1, 512], BF16),
        ("ones_col", [128, 1], F32), ("ones_col_bf", [128, 1], BF16),
    ]:
        t = es.enter_context(nc.sbuf_tensor("c_" + name, shape, dt))
        P.dma("sp", t[:], const_dram[name], writes=(name,))
        C[name] = t
    C["pS0"] = es.enter_context(nc.psum_tensor("pS0", [128, 1024], F32))
    C["pS1"] = es.enter_context(nc.psum_tensor("pS1", [128, 1024], F32))
    C["pN0"] = es.enter_context(nc.psum_tensor("pN0", [65, 1024], F32))
    C["pA"] = es.enter_context(nc.psum_tensor("pA", [128, 512], F32))
    C["pB"] = es.enter_context(nc.psum_tensor("pB", [128, 512], F32))
    C["lnsq"] = es.enter_context(nc.sbuf_tensor("lnsq", [128, 512], F32))
    C["stat"] = es.enter_context(nc.sbuf_tensor("statrows", [1, 2048], F32))
    return P, A, C


def stream_rin_chunk(P, A, C, rin, rc, tc):
    tiles = []
    for dm in range(ND):
        v = rc[:, dm * 512:(dm + 1) * 512]
        P.dma("sp", v, rin[dm * 128:(dm + 1) * 128, tc * 512:(tc + 1) * 512],
              writes=(f"RC{dm}",))
        tiles.append((v, f"RC{dm}"))
    return tiles


def compute_xn(P, A, C, rin, xn_view, xn_key):
    rc = A["ARF"].alloc("RCx", ND * 512, keys=[f"RC{i}" for i in range(ND)])
    for tc in range(NCH):
        tiles = stream_rin_chunk(P, A, C, rin, rc, tc)
        a_row, b_row = ln_stats(P, C, tiles, D)
        dst = [(xn_view[:, dm * T + tc * 512: dm * T + (tc + 1) * 512], xn_key)
               for dm in range(ND)]
        ln_apply(P, C, a_row, b_row, tiles, dst)
    A["ARF"].release("RCx")


def sandwich_update(P, A, C, zc_tiles, rin, rout, tc, rc):
    rtiles = stream_rin_chunk(P, A, C, rin, rc, tc)
    a_row, b_row = ln_stats(P, C, zc_tiles, D)
    ps = C["pS0"]
    ab, bb = ps[:, 0:512], ps[:, 512:1024]
    orow = C["ones_row"]
    P.mm(ab, orow[0:1, 0:128], a_row, start=True, stop=True, reads=("st_a", "ones_row"), writes=("pS0",))
    P.mm(bb, orow[0:1, 0:128], b_row, start=True, stop=True, reads=("st_b", "ones_row"), writes=("pS0",))
    tmp = C["lnsq"]
    for dm in range(ND):
        zap, zkey = zc_tiles[dm]
        rap, rkey = rtiles[dm]
        P.tt("dve", tmp[:], zap, ab, ALU.mult, reads=(zkey, "pS0"), writes=("lnsq",))
        P.tt("dve", tmp[:], tmp[:], bb, ALU.add, reads=("lnsq", "pS0"), writes=("lnsq",))
        P.tt("dve", rap, tmp[:], rap, ALU.add, reads=("lnsq", rkey), writes=(rkey,))
        P.dma("pool", rout[dm * 128:(dm + 1) * 128, tc * 512:(tc + 1) * 512], rap,
              reads=(rkey,), writes=("rout",))


def load_cast_bf(P, A, C, w_dram, rows, cols, tag):
    nb = rows // 128
    stg = [A["ARF"].alloc(tag + f"_s{i}", cols) for i in range(2)]
    view = A["ARB"].alloc(tag, nb * cols)
    for r in range(nb):
        s = stg[r % 2]
        sk = tag + f"_s{r % 2}"
        P.dma("sp", s, w_dram[r * 128:(r + 1) * 128, :], writes=(sk,))
        P.copy("pool", view[:, r * cols:(r + 1) * cols], s, reads=(sk,), writes=(tag,))
    A["ARF"].release(tag + "_s0")
    A["ARF"].release(tag + "_s1")
    return view



def emit_stage_a(P, A, C, dram, rin, rout, scr):
    nc = P.nc
    NH = 16

    xn = A["ARB"].alloc("XN", ND * T)
    compute_xn(P, A, C, rin, xn, "XN")

    wsumT = A["ARB"].alloc("WsT", ND * D)
    stg0 = A["ARF"].alloc("wk_s", D)
    stg1 = A["ARF"].alloc("wv_s", D)
    for r in range(ND):
        P.dma("sp", stg0, dram["L0_attn_Wk"][r * 128:(r + 1) * 128, :], writes=("wk_s",))
        P.dma("sp", stg1, dram["L0_attn_Wv"][r * 128:(r + 1) * 128, :], writes=("wv_s",))
        P.tt("dve", stg0, stg0, stg1, ALU.add, reads=("wk_s", "wv_s"), writes=("wk_s",))
        for of in range(ND):
            pt = C["pS0"][:, 0:128]
            P.trans(pt, stg0[:, of * 128:(of + 1) * 128], C["ident"][:],
                    reads=("wk_s", "ident"), writes=("pS0",))
            P.act(wsumT[:, of * D + r * 128:of * D + (r + 1) * 128], pt, AF.Copy,
                  reads=("pS0",), writes=("WsT",))
    A["ARF"].release("wk_s")
    A["ARF"].release("wv_s")
    wdn = load_cast_bf(P, A, C, dram["L0_attn_W_down_kv"], D, 256, "Wdn")
    wcomb = A["ARB"].alloc("Wcomb", ND * 256)
    for i in range(ND):
        out = C["pS1"][:, 0:256]
        for m in range(ND):
            P.mm(out, wsumT[:, m * D + i * 128:m * D + (i + 1) * 128],
                 wdn[:, m * 256:(m + 1) * 256],
                 start=(m == 0), stop=(m == ND - 1), reads=("WsT", "Wdn"), writes=("pS1",))
        P.act(wcomb[:, i * 256:(i + 1) * 256], out, AF.Copy, reads=("pS1",), writes=("Wcomb",))
    A["ARB"].release("WsT")
    A["ARB"].release("Wdn")
    P.barrier()

    lat = A["ARB"].alloc("LAT", 2 * T)
    for lt in range(2):
        for tc in range(NCH):
            ps, pk = (C["pS0"], "pS0") if (lt * NCH + tc) % 2 == 0 else (C["pS1"], "pS1")
            out = ps[:, 0:512]
            for dm in range(ND):
                P.mm(out, wcomb[:, dm * 256 + lt * 128:dm * 256 + (lt + 1) * 128],
                     xn[:, dm * T + tc * 512:dm * T + (tc + 1) * 512],
                     start=(dm == 0), stop=(dm == ND - 1), reads=("Wcomb", "XN"), writes=(pk,))
            P.act(lat[:, lt * T + tc * 512:lt * T + (tc + 1) * 512], out, AF.Copy,
                  reads=(pk,), writes=("LAT",))
    A["ARB"].release("Wcomb")
    P.barrier()

    wq = load_cast_bf(P, A, C, dram["L0_attn_Wq"], D, D, "Wq")
    qstg = [A["ARB"].alloc(f"QS{i}", 512) for i in range(2)]
    for of in range(ND):
        for tc in range(NCH):
            i = of * NCH + tc
            ps, pk = (C["pS0"], "pS0") if i % 2 == 0 else (C["pS1"], "pS1")
            out = ps[:, 0:512]
            for dm in range(ND):
                P.mm(out, wq[:, dm * D + of * 128:dm * D + of * 128 + 128],
                     xn[:, dm * T + tc * 512:dm * T + (tc + 1) * 512],
                     start=(dm == 0), stop=(dm == ND - 1), reads=("Wq", "XN"), writes=(pk,))
            qs, qk = qstg[i % 2], f"QS{i % 2}"
            P.act(qs, out, AF.Copy, reads=(pk,), writes=(qk,))
            P.dma("pool", scr["Qs"][of * 128:(of + 1) * 128, tc * 512:(tc + 1) * 512],
                  qs, reads=(qk,), writes=("Qs",))
    A["ARB"].release("Wq")
    A["ARB"].release("XN")
    A["ARB"].release("QS0")
    A["ARB"].release("QS1")
    P.barrier()

    wupk = load_cast_bf(P, A, C, dram["L0_attn_W_up_k"], 256, D, "Wupk")
    wupv = load_cast_bf(P, A, C, dram["L0_attn_W_up_v"], 256, D, "Wupv")
    krec = A["ARB"].alloc("KRE", T)
    vaug = A["ARB"].alloc("VAU", NT * 65)
    qh = [A["ARB"].alloc(f"QH{i}", T) for i in range(2)]
    ohb = A["ARB"].alloc("OHB", 1024)
    pb = [A["ARB"].alloc(f"PB{i}", 1024) for i in range(2)]
    dky = A["ARF"].alloc("DKY", 8192)
    P.dma("sp", dky, dram["decay"], writes=("DKY",))
    pm = [A["ARF"].alloc(f"PM{i}", 1024) for i in range(2)]
    ncomb = A["ARF"].alloc("NCOMB", 1024, keys=["NC1"])
    nrr = A["ARF"].alloc("NRR", 1024, keys=["NRR"])
    pN0 = C["pN0"]
    for h in range(NH):
        qcur, qck = qh[h % 2], f"QH{h % 2}"
        P.dma("sp", qcur[0:64, :], scr["Qs"][h * 64:(h + 1) * 64, :], reads=("Qs",), writes=(qck,))
        for tc in range(NCH):
            out = C["pS0"][0:64, 0:512]
            for lt in range(2):
                P.mm(out, wupk[:, lt * D + h * 64:lt * D + h * 64 + 64],
                     lat[:, lt * T + tc * 512:lt * T + (tc + 1) * 512],
                     start=(lt == 0), stop=(lt == 1), reads=("Wupk", "LAT"), writes=("pS0",))
            P.act(krec[0:64, tc * 512:(tc + 1) * 512], out, AF.Copy, reads=("pS0",), writes=("KRE",))
        P.memset("pool", vaug[:], 1.0, writes=("VAU",))
        for tt in range(NT):
            out = C["pS1"][0:128, 0:64]
            for lt in range(2):
                P.mm(out, lat[:, lt * T + tt * 128:lt * T + (tt + 1) * 128],
                     wupv[:, lt * D + h * 64:lt * D + h * 64 + 64],
                     start=(lt == 0), stop=(lt == 1), reads=("Wupv", "LAT"), writes=("pS1",))
            P.act(vaug[:, tt * 65:tt * 65 + 64], out, AF.Copy, reads=("pS1",), writes=("VAU",))
        for qcb in range(4):
            def head_a(kt, qcb=qcb):
                sps = "pS0" if kt % 2 == 0 else "pS1"
                S = C[sps]
                for half in range(2):
                    P.mm(S[:, half * 512:(half + 1) * 512],
                         krec[0:64, kt * 128:(kt + 1) * 128],
                         qcur[0:64, qcb * 1024 + half * 512:qcb * 1024 + (half + 1) * 512],
                         start=True, stop=True, reads=("KRE", qck), writes=(sps,))
            def tail_a(kt, qcb=qcb):
                sps = "pS0" if kt % 2 == 0 else "pS1"
                S = C[sps]
                pmb, pmk = pm[kt % 2], f"PM{kt % 2}"
                pP, pPk = pb[kt % 2], f"PB{kt % 2}"
                doff = qcb * 1024 - 128 * kt + 3968
                P.tt("dve", pmb, S[:], dky[:, doff:doff + 1024], ALU.mult,
                     reads=(sps, "DKY"), writes=(pmk,))
                P.act(pP, pmb, AF.Exp, scale=1.0 / 32.0, reads=(pmk,), writes=(pPk,))
                for half in range(2):
                    sl = slice(half * 512, (half + 1) * 512)
                    P.mm(pN0[:, sl], vaug[:, kt * 65:kt * 65 + 65], pP[:, sl],
                         start=(kt == 0), stop=(kt == NT - 1), reads=("VAU", pPk), writes=("pN0",))
            for kt in range(NT):
                head_a(kt)
                if kt > 0:
                    tail_a(kt - 1)
            tail_a(NT - 1)
            t1 = ncomb[0:65, 0:1024]
            P.copy("dve", t1, pN0[:], reads=("pN0",), writes=("NC1",))
            rr = nrr[0:1, 0:1024]
            P.emit("dve", lambda E, rr=rr, t1=t1: E.reciprocal(rr, t1[64:65, :]),
                   reads=("NC1",), writes=("NRR",))
            bc = C["pS0"][0:64, 0:1024]
            for half in range(2):
                P.mm(bc[:, half * 512:(half + 1) * 512], C["ones_row"][0:1, 0:64],
                     rr[:, half * 512:(half + 1) * 512], start=True, stop=True,
                     reads=("NRR", "ones_row"), writes=("pS0",))
            P.tt("dve", ohb[0:64, :], t1[0:64, :], bc, ALU.mult, reads=("NC1", "pS0"), writes=("OHB",))
            P.dma("pool", scr["AOs"][h * 64:(h + 1) * 64, qcb * 1024:(qcb + 1) * 1024],
                  ohb[0:64, :], reads=("OHB",), writes=("AOs",))
    for nm in ["KRE", "VAU", "QH0", "QH1", "OHB", "PB0", "PB1", "LAT",
               "Wupk", "Wupv"]:
        A["ARB"].release(nm)
    A["ARF"].release("NCOMB")
    A["ARF"].release("NRR")
    A["ARF"].release("DKY")
    A["ARF"].release("PM0")
    A["ARF"].release("PM1")
    P.barrier()

    wo = load_cast_bf(P, A, C, dram["L0_attn_Wo"], D, D, "Wo")
    aostg = A["ARB"].alloc("AOC", ND * 512, keys=[f"AOC{i}" for i in range(ND)])
    zc = A["ARF"].alloc("ZC", ND * 512, keys=[f"ZC{i}" for i in range(ND)])
    rc = A["ARF"].alloc("RCs", ND * 512, keys=[f"RC{i}" for i in range(ND)])
    for tc in range(NCH):
        for dm in range(ND):
            P.dma("sp", aostg[:, dm * 512:(dm + 1) * 512],
                  scr["AOs"][dm * 128:(dm + 1) * 128, tc * 512:(tc + 1) * 512],
                  reads=("AOs",), writes=(f"AOC{dm}",))
        ztiles = []
        for of in range(ND):
            ps, pk = (C["pS1"], "pS1")
            out = ps[:, 0:512]
            for dm in range(ND):
                P.mm(out, wo[:, dm * D + of * 128:dm * D + of * 128 + 128],
                     aostg[:, dm * 512:(dm + 1) * 512],
                     start=(dm == 0), stop=(dm == ND - 1), reads=("Wo", f"AOC{dm}"), writes=(pk,))
            zv = zc[:, of * 512:(of + 1) * 512]
            P.act(zv, out, AF.Copy, reads=(pk,), writes=(f"ZC{of}",))
            ztiles.append((zv, f"ZC{of}"))
        sandwich_update(P, A, C, ztiles, rin, rout, tc, rc)
    A["ARB"].release("Wo")
    A["ARB"].release("AOC")
    A["ARF"].release("ZC")
    A["ARF"].release("RCs")
    P.barrier()


def xs_views(P, A, xi, xe, dm, cols):
    c0, ln = cols
    if dm < 4:
        return xi[:, dm * T + c0:dm * T + c0 + ln], "XI"
    if c0 == 0:
        return xe[:, (dm - 4) * 512:(dm - 4) * 512 + ln], "XE"
    return xi[:, dm * T + c0 - 1:dm * T + c0 - 1 + ln], "XI"


def build_xe(P, A, C, xi, xe):
    for dm in range(4, 8):
        sl = xe[:, (dm - 4) * 512:(dm - 3) * 512]
        P.memset("dve", sl[:, 0:1], 0.0, writes=("XE",))
        P.copy("dve", sl[:, 1:512], xi[:, dm * T:dm * T + 511], reads=("XI",), writes=("XE",))


def emit_stage_b(P, A, C, dram, rin, rout, scr):
    nc = P.nc
    NH, DH, WIN = 8, 64, 512
    HID = 2730
    NJ = 22
    for nm in ["maskA", "maskB"]:
        t = P.es.enter_context(nc.sbuf_tensor("b_" + nm, [128, 128], F32))
        P.dma("sp", t[:], dram[nm], writes=(nm,))
        C[nm] = t
    rott = {}
    t = P.es.enter_context(nc.sbuf_tensor("b_rotmatT", [64, 64], BF16))
    P.dma("sp", t[:], dram["rotmatT"], writes=("rotmatT",))
    rott["rotmatT"] = t

    for l in range(2):
        stg = A["ARF"].alloc("w1s", 5460)
        stgb = A["ARB"].alloc("w1sb", 5460)
        for r in range(ND):
            P.dma("sp", stg, dram["L0_ffn_w1"][l][r * 128:(r + 1) * 128, :], writes=("w1s",))
            P.copy("pool", stgb, stg, reads=("w1s",), writes=("w1sb",))
            for half, cb in ((0, 0), (1, HID)):
                P.dma("pool", scr["w1bf"][l][:, half, r][0:21, :, :].rearrange("j p c -> p j c"),
                      stgb[:, cb:cb + 2688].rearrange("p (j c) -> p j c", c=128), reads=("w1sb",), writes=("w1bf",))
                P.dma("pool", scr["w1bf"][l][21][half][r][:, 0:42],
                      stgb[:, cb + 2688:cb + 2730], reads=("w1sb",), writes=("w1bf",))
        A["ARF"].release("w1s")
        A["ARB"].release("w1sb")
        stg = A["ARF"].alloc("w2s", 1024)
        stgb = A["ARB"].alloc("w2sb", 1024)
        for r in range(NJ):
            p = min(128, HID - r * 128)
            P.dma("sp", stg[0:p, :], dram["L0_ffn_w2"][l][r * 128:r * 128 + p, :], writes=("w2s",))
            P.copy("pool", stgb[0:p, :], stg[0:p, :], reads=("w2s",), writes=("w2sb",))
            P.dma("pool", scr["w2bf"][l][:, r][:, 0:p, :].rearrange("o p c -> p o c"),
                  stgb[0:p, :].rearrange("p (o c) -> p o c", c=128), reads=("w2sb",), writes=("w2bf",))
        A["ARF"].release("w2s")
        A["ARB"].release("w2sb")

    xi = A["ARB"].alloc("XI", ND * T)
    compute_xn(P, A, C, rin, xi, "XI")
    xe = A["ARB"].alloc("XE", 4 * 512)
    wst = A["ARF"].alloc("WST", 64)
    pmb = A["ARF"].alloc("PMB", 640)
    pef = A["ARF"].alloc("PEF", 640)
    pef2 = A["ARF"].alloc("PEF2", 640)
    pef3 = A["ARF"].alloc("PEF3", 640)
    sqb = A["ARF"].alloc("SQB", 512)
    nrm = A["ARF"].alloc("NRM", 512)
    ncomb = A["ARF"].alloc("NCB", 128, keys=["NC1"])
    nrr = A["ARF"].alloc("NRR", 128, keys=["NRR"])
    qsc = P.es.enter_context(nc.sbuf_tensor("b_qsc", [64, 2], F32))
    ksc = P.es.enter_context(nc.sbuf_tensor("b_ksc", [64, 2], F32))
    for l in range(2):
        P.dma("sp", qsc[:, l:l + 1], dram["L0_ffn_q_scale"][l].unsqueeze(1), writes=("qsc",), slow=True)
        P.dma("sp", ksc[:, l:l + 1], dram["L0_ffn_k_scale"][l].unsqueeze(1), writes=("ksc",), slow=True)

    pS0, pS1, pN0 = C["pS0"], C["pS1"], C["pN0"]
    pA, pB = C["pA"], C["pB"]
    for l in range(2):
        build_xe(P, A, C, xi, xe)
        for nm in ["rot_cq", "rot_sq", "rot_ck", "rot_sk"]:
            v = A["ARB"].alloc(nm, T)
            P.dma("sp", v[0:64, :], dram[nm], writes=(nm,))
            rott[nm] = v
        whb = [A["ARB"].alloc(f"WHB{i}", 24 * 64) for i in range(2)]
        qraw = A["ARB"].alloc("QRAW", 512)
        kraw = A["ARB"].alloc("KRAW", 512)
        qfin = A["ARB"].alloc("QFIN", T)
        kfin = A["ARB"].alloc("KFIN", T)
        qtmp = A["ARB"].alloc("QTMP", 512)
        vaug = A["ARB"].alloc("VAU", NT * 65)
        ptb = A["ARB"].alloc("PTB", 2 * 128, keys=["PTB0", "PTB1"])
        ohb = A["ARB"].alloc("OHB", 128)
        for h in range(NH):
            wcur, wck = whb[h % 2], f"WHB{h % 2}"
            for gi, colbase in enumerate((h * 64, 512 + h * 64, 1024 + h * 64)):
                for dm in range(ND):
                    P.dma("sp", wst, dram["L0_ffn_wqkv"][l][dm * 128:(dm + 1) * 128,
                          colbase:colbase + 64], writes=("WST",))
                    P.copy("pool", wcur[:, (gi * 8 + dm) * 64:(gi * 8 + dm + 1) * 64],
                           wst, reads=("WST",), writes=(wck,))
            P.memset("pool", vaug[:], 1.0, writes=("VAU",))
            for tt in range(NT):
                out = pS1[0:128, 0:64]
                for dm in range(ND):
                    xv, xk = xs_views(P, A, xi, xe, dm, (tt * 128, 128))
                    P.mm(out, xv, wcur[:, (16 + dm) * 64:(16 + dm + 1) * 64],
                         start=(dm == 0), stop=(dm == ND - 1), reads=(xk, wck), writes=("pS1",))
                P.act(vaug[:, tt * 65:tt * 65 + 64], out, AF.Copy, reads=("pS1",), writes=("VAU",))
            for gi, raw, rk, fin, fk, sc, sck, cn, sn in (
                    (0, qraw, "QRAW", qfin, "QFIN", qsc, "qsc", "rot_cq", "rot_sq"),
                    (1, kraw, "KRAW", kfin, "KFIN", ksc, "ksc", "rot_ck", "rot_sk")):
                for tc in range(NCH):
                    sl = slice(tc * 512, (tc + 1) * 512)
                    out = pA[0:64, 0:512]
                    for dm in range(ND):
                        xv, xk = xs_views(P, A, xi, xe, dm, (tc * 512, 512))
                        P.mm(out, wcur[:, (gi * 8 + dm) * 64:(gi * 8 + dm + 1) * 64], xv,
                             start=(dm == 0), stop=(dm == ND - 1), reads=(wck, xk), writes=("pA",))
                    rsl = raw[0:64, 0:512]
                    P.act(rsl, out, AF.Copy, reads=("pA",), writes=(rk,))
                    P.act(sqb[0:64, :], rsl, AF.Square, reads=(rk,), writes=("SQB",))
                    P.mm(pB[0:1, 0:512], C["ones_col"][0:64, 0:1], sqb[0:64, :],
                         start=True, stop=True, reads=("SQB", "ones_col"), writes=("pB",))
                    r0 = nrm[0:1, 0:512]
                    P.act(r0, pB[0:1, 0:512], AF.Sqrt, reads=("pB",), writes=("NRM",))
                    P.ts("dve", r0, r0, 1e-12, None, ALU.max, reads=("NRM",), writes=("NRM",))
                    P.emit("dve", lambda E, r0=r0: E.reciprocal(r0, r0), reads=("NRM",), writes=("NRM",))
                    bc = pB[0:64, 0:512]
                    P.mm(bc, C["ones_row"][0:1, 0:64], r0, start=True, stop=True,
                         reads=("NRM", "ones_row"), writes=("pB",))
                    P.tt("dve", qtmp[0:64, :], rsl, bc, ALU.mult, reads=(rk, "pB"), writes=("QTMP",))
                    P.ts("dve", qtmp[0:64, :], qtmp[0:64, :], sc[:, l:l + 1], None, ALU.mult,
                         reads=("QTMP", sck), writes=("QTMP",))
                    rot = pA[0:64, 0:512]
                    P.mm(rot, rott["rotmatT"][:], qtmp[0:64, :], start=True, stop=True,
                         reads=("QTMP", "rotmatT"), writes=("pA",))
                    P.tt("dve", sqb[0:64, :], qtmp[0:64, :], rott[cn][0:64, sl], ALU.mult,
                         reads=("QTMP", cn), writes=("SQB",))
                    P.tt("dve", pmb[0:64, 0:512], rot, rott[sn][0:64, sl], ALU.mult,
                         reads=("pA", sn), writes=("PMB",))
                    P.tt("dve", fin[0:64, sl], sqb[0:64, :], pmb[0:64, 0:512], ALU.add,
                         reads=("SQB", "PMB"), writes=(fk,))
            iters = [(w, it) for w in range(8) for it in range(4)]
            def head_b(idx):
                w, it = iters[idx]
                par = idx % 2
                sps, spk = (pS0, "pS0") if par == 0 else (pS1, "pS1")
                jlo = it if w > 0 else max(it, 4)
                jhi = it + 4
                irng = slice(w * 512 + it * 128, w * 512 + (it + 1) * 128)
                for jc in range(jlo, jhi + 1):
                    jsl = slice((jc - jlo) * 128, (jc - jlo + 1) * 128)
                    gtok = 512 * (w - 1) + 128 * jc
                    P.mm(sps[:, jsl], qfin[0:64, irng], kfin[0:64, gtok:gtok + 128],
                         start=True, stop=True, reads=("QFIN", "KFIN"), writes=(spk,))
            def tail_b(idx):
                w, it = iters[idx]
                par = idx % 2
                sps, spk = (pS0, "pS0") if par == 0 else (pS1, "pS1")
                pmx, pmxk = (pmb, "PMB") if par == 0 else (pef, "PEF")
                pex, pexk = (pef2, "PEF2") if par == 0 else (pef3, "PEF3")
                ptp, ptpk = (pA, "pA") if par == 0 else (pB, "pB")
                jlo = it if w > 0 else max(it, 4)
                jhi = it + 4
                njc = jhi - jlo + 1
                wid = njc * 128
                for jc in range(jlo, jhi + 1):
                    jsl = slice((jc - jlo) * 128, (jc - jlo + 1) * 128)
                    if jc == it:
                        P.tt("dve", pmx[:, jsl], sps[:, jsl], C["maskA"][:], ALU.add,
                             reads=(spk, "maskA"), writes=(pmxk,))
                    elif jc == jhi:
                        P.tt("dve", pmx[:, jsl], sps[:, jsl], C["maskB"][:], ALU.add,
                             reads=(spk, "maskB"), writes=(pmxk,))
                    else:
                        P.copy("dve", pmx[:, jsl], sps[:, jsl], reads=(spk,), writes=(pmxk,))
                mrow = nrm[0:128, par:par + 1]
                P.emit("dve", lambda E, mrow=mrow, w_=wid, pmx=pmx: E.reduce_max(
                    mrow, pmx[:, 0:w_], axis=AX.X), reads=(pmxk,), writes=("NRM",))
                P.ts("dve", mrow, mrow, -8.0, None, ALU.mult, reads=("NRM",), writes=("NRM",))
                P.act(pex[:, 0:wid], pmx[:, 0:wid], AF.Exp, bias=mrow, scale=8.0,
                      reads=(pmxk, "NRM"), writes=(pexk,))
                for jc in range(jlo, jhi + 1):
                    jsl = slice((jc - jlo) * 128, (jc - jlo + 1) * 128)
                    pt = ptp[:, 0:128]
                    P.trans(pt, pex[:, jsl], C["ident"][:], reads=(pexk, "ident"), writes=(ptpk,))
                    slot = (jc % 2)
                    P.act(ptb[:, slot * 128:(slot + 1) * 128], pt, AF.Copy,
                          reads=(ptpk,), writes=(f"PTB{slot}",))
                    tglob = 4 * (w - 1) + jc
                    P.mm(pN0[0:65, 0:128], vaug[:, tglob * 65:tglob * 65 + 65],
                         ptb[:, slot * 128:(slot + 1) * 128],
                         start=(jc == jlo), stop=(jc == jhi),
                         reads=("VAU", f"PTB{slot}"), writes=("pN0",))
                t1 = ncomb[0:65, 0:128]
                P.copy("dve", t1, pN0[0:65, 0:128], reads=("pN0",), writes=("NC1",))
                rr = nrr[0:1, 0:128]
                P.emit("dve", lambda E, rr=rr, t1=t1: E.reciprocal(rr, t1[64:65, :]),
                       reads=("NC1",), writes=("NRR",))
                bc2 = ptp[0:64, 128:256]
                P.mm(bc2, C["ones_row"][0:1, 0:64], rr, start=True, stop=True,
                     reads=("NRR", "ones_row"), writes=(ptpk,))
                P.tt("dve", ohb[0:64, :], t1[0:64, :], bc2, ALU.mult,
                     reads=("NC1", ptpk), writes=("OHB",))
                P.dma("pool", scr["AOs"][h * 64:(h + 1) * 64,
                      w * 512 + it * 128:w * 512 + (it + 1) * 128],
                      ohb[0:64, :], reads=("OHB",), writes=("AOs",))
            for idx in range(32):
                head_b(idx)
                if idx > 0:
                    tail_b(idx - 1)
            tail_b(31)
        for nm in ["WHB0", "WHB1", "QRAW", "KRAW", "QFIN", "KFIN", "QTMP",
                   "VAU", "PTB", "OHB", "rot_cq", "rot_sq", "rot_ck", "rot_sk"]:
            A["ARB"].release(nm)
        P.barrier()
        wo = load_cast_bf(P, A, C, dram["L0_ffn_wo"][l], 512, D, "Wo")
        aostg = A["ARB"].alloc("AOC", 4 * 512, keys=[f"AOC{i}" for i in range(4)])
        for tc in range(NCH):
            for dm in range(4):
                P.dma("sp", aostg[:, dm * 512:(dm + 1) * 512],
                      scr["AOs"][dm * 128:(dm + 1) * 128, tc * 512:(tc + 1) * 512],
                      reads=("AOs",), writes=(f"AOC{dm}",))
            for of in range(ND):
                out = pS1[:, 0:512]
                for dm in range(4):
                    P.mm(out, wo[:, dm * D + of * 128:dm * D + of * 128 + 128],
                         aostg[:, dm * 512:(dm + 1) * 512],
                         start=(dm == 0), stop=(dm == 3), reads=("Wo", f"AOC{dm}"), writes=("pS1",))
                xsl = xi[:, of * T + tc * 512:of * T + (tc + 1) * 512]
                P.tt("dve", xsl, xsl, out, ALU.add, reads=("XI", "pS1"), writes=("XI",))
        A["ARB"].release("Wo")
        A["ARB"].release("AOC")
        P.barrier()
        build_xe(P, A, C, xi, xe)
        lgb = P.es.enter_context(nc.sbuf_tensor(f"b_lgb{l}", [128, 16], F32))
        P.dma("sp", lgb[:, 0:8], dram["L0_ffn_ln_g"][l].rearrange("(a p) -> p a", p=128),
              writes=("lgb",), slow=True)
        P.dma("sp", lgb[:, 8:16], dram["L0_ffn_ln_b"][l].rearrange("(a p) -> p a", p=128),
              writes=("lgb",), slow=True)
        b1a = A["ARF"].alloc("B1A", 22)
        b1g = A["ARF"].alloc("B1G", 22)
        P.dma("sp", b1a[:, 0:21], dram["L0_ffn_b1"][l][0:2688].rearrange("(a p) -> p a", p=128),
              writes=("B1A",), slow=True)
        P.dma("sp", b1a[0:42, 21:22], dram["L0_ffn_b1"][l][2688:2730].unsqueeze(1),
              writes=("B1A",), slow=True)
        P.dma("sp", b1g[:, 0:21], dram["L0_ffn_b1"][l][2730:5418].rearrange("(a p) -> p a", p=128),
              writes=("B1G",), slow=True)
        P.dma("sp", b1g[0:42, 21:22], dram["L0_ffn_b1"][l][5418:5460].unsqueeze(1),
              writes=("B1G",), slow=True)
        b2c = A["ARF"].alloc("B2C", 8)
        P.dma("sp", b2c[:], dram["L0_ffn_b2"][l].rearrange("(a p) -> p a", p=128),
              writes=("B2C",), slow=True)
        zb = A["ARB"].alloc("ZB", ND * 512)
        mch = A["ARB"].alloc("MCH", NJ * 512)
        w1r = [A["ARB"].alloc(f"W1R{i}", 16 * 128) for i in range(2)]
        w2r = [A["ARB"].alloc(f"W2R{i}", NJ * 128) for i in range(2)]
        for tc in range(NCH):
            tiles = []
            for dm in range(ND):
                xv, xk = xs_views(P, A, xi, xe, dm, (tc * 512, 512))
                tiles.append((xv, xk))
            a_row, b_row = ln_stats(P, C, tiles, D)
            ab = pS0[:, 0:512]
            bb = pS0[:, 512:1024]
            P.mm(ab, C["ones_row"][0:1, 0:128], a_row, start=True, stop=True,
                 reads=("st_a", "ones_row"), writes=("pS0",))
            P.mm(bb, C["ones_row"][0:1, 0:128], b_row, start=True, stop=True,
                 reads=("st_b", "ones_row"), writes=("pS0",))
            for dm in range(ND):
                xv, xk = tiles[dm]
                t0 = C["lnsq"]
                P.tt("dve", t0[:], xv, ab, ALU.mult, reads=(xk, "pS0"), writes=("lnsq",))
                P.tt("dve", t0[:], t0[:], bb, ALU.add, reads=("lnsq", "pS0"), writes=("lnsq",))
                P.ts("dve", zb[:, dm * 512:(dm + 1) * 512], t0[:],
                     lgb[:, dm:dm + 1], lgb[:, 8 + dm:9 + dm], ALU.mult, ALU.add,
                     reads=("lnsq", "lgb"), writes=("ZB",))
            for j in range(NJ):
                wj = min(128, HID - j * 128)
                wr, wrk = w1r[j % 2], f"W1R{j % 2}"
                P.dma("sp", wr.rearrange("p (h d c) -> p h d c", h=2, d=8)[:, :, :, 0:wj],
                      scr["w1bf"][l][j].rearrange("h d p c -> p h d c")[:, :, :, 0:wj],
                      reads=("w1bf",), writes=(wrk,))
                pa = pS1[:, 0:512]
                for dm in range(ND):
                    P.mm(pa[0:wj, :] if wj < 128 else pa,
                         wr[:, dm * 128:dm * 128 + wj],
                         zb[:, dm * 512:(dm + 1) * 512],
                         start=(dm == 0), stop=(dm == ND - 1), reads=(wrk, "ZB"), writes=("pS1",))
                pg = pS1[0:wj, 512:1024]
                for dm in range(ND):
                    P.mm(pg, wr[:, (8 + dm) * 128:(8 + dm) * 128 + wj],
                         zb[:, dm * 512:(dm + 1) * 512],
                         start=(dm == 0), stop=(dm == ND - 1), reads=(wrk, "ZB"), writes=("pS1",))
                gact = pmb[0:wj, 0:512]
                P.act(gact, pg, AF.Gelu, bias=b1g[0:wj, j:j + 1], scale=1.0,
                      reads=("pS1", "B1G"), writes=("PMB",))
                P.emit("dve", lambda E, o=mch[0:wj, j * 512:j * 512 + 512],
                       a=(pa[0:wj, :] if wj < 128 else pa),
                       s=b1a[0:wj, j:j + 1], g=gact: E.scalar_tensor_tensor(
                           o, a, s, g, ALU.add, ALU.mult),
                       reads=("pS1", "B1A", "PMB"), writes=("MCH",))
            for of in range(ND):
                wr2, wrk2 = w2r[of % 2], f"W2R{of % 2}"
                P.dma("sp", wr2.rearrange("p (j c) -> p j c", c=128)[:, 0:21, :],
                      scr["w2bf"][l][of][0:21].rearrange("j p c -> p j c"),
                      reads=("w2bf",), writes=(wrk2,))
                P.dma("sp", wr2[0:42, 21 * 128:22 * 128],
                      scr["w2bf"][l][of][21][0:42, :], reads=("w2bf",), writes=(wrk2,))
                out = pS0[:, 0:512] if of % 2 == 0 else pS0[:, 512:1024]
                ok = "pS0"
                for j in range(NJ):
                    wj = min(128, HID - j * 128)
                    P.mm(out, wr2[:, j * 128:j * 128 + 128][0:wj, :],
                         mch[0:wj, j * 512:(j + 1) * 512],
                         start=(j == 0), stop=(j == NJ - 1), reads=(wrk2, "MCH"), writes=(ok,))
                xsl = xi[:, of * T + tc * 512:of * T + (tc + 1) * 512]
                P.emit("dve", lambda E, o=xsl, a=out, s=b2c[:, of:of + 1], g=xsl:
                       E.scalar_tensor_tensor(o, a, s, g, ALU.add, ALU.add),
                       reads=(ok, "B2C", "XI"), writes=("XI",))
        A["ARF"].release("B1A")
        A["ARF"].release("B1G")
        A["ARF"].release("B2C")
        A["ARB"].release("ZB")
        A["ARB"].release("MCH")
        A["ARB"].release("W1R0")
        A["ARB"].release("W1R1")
        A["ARB"].release("W2R0")
        A["ARB"].release("W2R1")
        P.barrier()
    rc = A["ARF"].alloc("RCs", ND * 512, keys=[f"RC{i}" for i in range(ND)])
    for tc in range(NCH):
        ztiles = [(xi[:, dm * T + tc * 512:dm * T + (tc + 1) * 512], "XI") for dm in range(ND)]
        sandwich_update(P, A, C, ztiles, rin, rout, tc, rc)
    A["ARF"].release("RCs")
    for nm in ["XI", "XE"]:
        A["ARB"].release(nm)
    for nm in ["WST", "PMB", "PEF", "PEF2", "PEF3", "SQB", "NRM", "NCB", "NRR"]:
        A["ARF"].release(nm)
    P.barrier()

def emit_stage_c(P, A, C, dram, rin, rout, scr):
    nc = P.nc
    NH = 16
    gb = P.es.enter_context(nc.sbuf_tensor("c_gb", [128, NT], F32))
    P.dma("sp", gb[:], dram["L1_attn_pos_rev"], writes=("GB",))
    wb = P.es.enter_context(nc.sbuf_tensor("c_wb", [64, 1], F32))
    P.dma("sp", wb[:], dram["L1_attn_w_pos_rev"], writes=("WB",))
    bvr = P.es.enter_context(nc.sbuf_tensor("c_bvr", [1, D], F32))
    P.dma("sp", bvr[:], dram["L1_attn_b_v"].unsqueeze(0), writes=("BVR",))
    xn = A["ARB"].alloc("XN", ND * T)
    compute_xn(P, A, C, rin, xn, "XN")
    whb = A["ARB"].alloc("WHB", 40 * 64)
    bcolv = A["ARF"].alloc("BCOL", 4)
    wst = A["ARF"].alloc("WST", 64)
    qh = A["ARB"].alloc("WQH", T)
    kh = A["ARB"].alloc("WKH", T)
    qg = A["ARB"].alloc("GQH", T)
    kg = A["ARB"].alloc("GKH", T)
    vaug = A["ARB"].alloc("VAU", NT * 65)
    vaug64 = A["ARB"].alloc("VA64", NT * 65)
    owh = A["ARB"].alloc("OWH", T)
    pwb = A["ARB"].alloc("PWB", 512)
    pb = [A["ARB"].alloc(f"PB{i}", 1024) for i in range(2)]
    ohb = A["ARB"].alloc("OHB", 1024)
    ncomb = A["ARF"].alloc("NCOMB", 1024, keys=["NC1"])
    nrr = A["ARF"].alloc("NRR", 1024, keys=["NRR"])
    pS0, pS1, pN0 = C["pS0"], C["pS1"], C["pN0"]
    pA, pB = C["pA"], C["pB"]
    for h in range(NH):
        for gi, (wn, cb) in enumerate([("L1_attn_w_wqk", h * 64), ("L1_attn_w_wqk", D + h * 64),
                                       ("L1_attn_w_qk", h * 64), ("L1_attn_w_qk", D + h * 64),
                                       ("L1_attn_w_v", h * 64)]):
            for dm in range(ND):
                P.dma("sp", wst, dram[wn][dm * 128:(dm + 1) * 128, cb:cb + 64], writes=("WST",))
                P.copy("pool", whb[:, (gi * 8 + dm) * 64:(gi * 8 + dm + 1) * 64],
                       wst, reads=("WST",), writes=("WHB",))
        for gi, (bn, off) in enumerate([("L1_attn_b_wqk", h * 64), ("L1_attn_b_wqk", D + h * 64),
                                        ("L1_attn_b_qk", h * 64), ("L1_attn_b_qk", D + h * 64)]):
            P.dma("sp", bcolv[0:64, gi:gi + 1], dram[bn][off:off + 64].unsqueeze(1),
                  writes=("BCOL",), slow=True)
        for gi, dst, dk in ((0, qh, "WQH"), (1, kh, "WKH"), (2, qg, "GQH"), (3, kg, "GKH")):
            for tc in range(NCH):
                out = pA[0:64, 0:512]
                for dm in range(ND):
                    P.mm(out, whb[:, (gi * 8 + dm) * 64:(gi * 8 + dm + 1) * 64],
                         xn[:, dm * T + tc * 512:dm * T + (tc + 1) * 512],
                         start=(dm == 0), stop=(dm == ND - 1), reads=("WHB", "XN"), writes=("pA",))
                P.act(dst[0:64, tc * 512:(tc + 1) * 512], out, AF.Identity,
                      bias=bcolv[0:64, gi:gi + 1], reads=("pA", "BCOL"), writes=(dk,))
        P.memset("pool", vaug[:], 1.0, writes=("VAU",))
        P.memset("pool", vaug64[:], 1.0, writes=("VA64",))
        for tt in range(NT):
            out = pS1[0:128, 0:64]
            for dm in range(ND):
                P.mm(out, xn[:, dm * T + tt * 128:dm * T + (tt + 1) * 128],
                     whb[:, (32 + dm) * 64:(32 + dm + 1) * 64],
                     start=(dm == 0), stop=False, reads=("XN", "WHB"), writes=("pS1",))
            P.mm(out, C["ones_row"][0:1, 0:128], bvr[0:1, h * 64:(h + 1) * 64],
                 start=False, stop=True, reads=("ones_row", "BVR"), writes=("pS1",))
            P.act(vaug[:, tt * 65:tt * 65 + 64], out, AF.Copy, reads=("pS1",), writes=("VAU",))
        for tt in range(NT):
            P.dma("pool", vaug64[0:64, tt * 65:tt * 65 + 65],
                  vaug[:, tt * 65:tt * 65 + 65][64:128, :], reads=("VAU",), writes=("VA64",))
        for nb in range(8):
            sp = pS0[0:64, 0:512]
            for wloc in range(8):
                n = nb * 8 + wloc
                P.mm(sp[:, wloc * 64:(wloc + 1) * 64],
                     kh[0:64, n * 64:(n + 1) * 64], qh[0:64, n * 64:(n + 1) * 64],
                     start=True, stop=True, reads=("WKH", "WQH"), writes=("pS0",))
            P.act(pwb[0:64, :], sp, AF.Exp, bias=wb[:], scale=0.125,
                  reads=("pS0", "WB"), writes=("PWB",))
            nps = pN0[0:65, 0:512]
            for wloc in range(8):
                n = nb * 8 + wloc
                tt = n // 2
                if n % 2 == 0:
                    vsl, vk = vaug[:, tt * 65:tt * 65 + 65][0:64, :], "VAU"
                else:
                    vsl, vk = vaug64[0:64, tt * 65:tt * 65 + 65], "VA64"
                P.mm(nps[:, wloc * 64:(wloc + 1) * 64], vsl,
                     pwb[0:64, wloc * 64:(wloc + 1) * 64],
                     start=True, stop=True, reads=(vk, "PWB"), writes=("pN0",))
            t1 = ncomb[0:65, 0:512]
            P.copy("dve", t1, nps, reads=("pN0",), writes=("NC1",))
            rr = nrr[0:1, 0:512]
            P.emit("dve", lambda E, rr=rr, t1=t1: E.reciprocal(rr, t1[64:65, :]),
                   reads=("NC1",), writes=("NRR",))
            bc = pB[0:64, 0:512]
            P.mm(bc, C["ones_row"][0:1, 0:64], rr, start=True, stop=True,
                 reads=("NRR", "ones_row"), writes=("pB",))
            P.tt("dve", owh[0:64, nb * 512:(nb + 1) * 512], t1[0:64, :], bc, ALU.mult,
                 reads=("NC1", "pB"), writes=("OWH",))
        for qcb in range(4):
            def head_c(kt, qcb=qcb):
                sps = "pS0" if kt % 2 == 0 else "pS1"
                S = C[sps]
                for half in range(2):
                    P.mm(S[:, half * 512:(half + 1) * 512],
                         kg[0:64, kt * 128:(kt + 1) * 128],
                         qg[0:64, qcb * 1024 + half * 512:qcb * 1024 + (half + 1) * 512],
                         start=True, stop=True, reads=("GKH", "GQH"), writes=(sps,))
            def tail_c(kt, qcb=qcb):
                sps = "pS0" if kt % 2 == 0 else "pS1"
                S = C[sps]
                pP, pPk = pb[kt % 2], f"PB{kt % 2}"
                P.act(pP, S[:], AF.Exp, bias=gb[:, kt:kt + 1], scale=1.0,
                      reads=(sps, "GB"), writes=(pPk,))
                for half in range(2):
                    sl = slice(half * 512, (half + 1) * 512)
                    P.mm(pN0[:, sl], vaug[:, kt * 65:kt * 65 + 65], pP[:, sl],
                         start=(kt == 0), stop=(kt == NT - 1), reads=("VAU", pPk), writes=("pN0",))
            for kt in range(NT):
                head_c(kt)
                if kt > 0:
                    tail_c(kt - 1)
            tail_c(NT - 1)
            t1 = ncomb[0:65, 0:1024]
            P.copy("dve", t1, pN0[:], reads=("pN0",), writes=("NC1",))
            rr = nrr[0:1, 0:1024]
            P.emit("dve", lambda E, rr=rr, t1=t1: E.reciprocal(rr, t1[64:65, :]),
                   reads=("NC1",), writes=("NRR",))
            bc = pS0[0:64, 0:1024]
            for half in range(2):
                P.mm(bc[:, half * 512:(half + 1) * 512], C["ones_row"][0:1, 0:64],
                     rr[:, half * 512:(half + 1) * 512], start=True, stop=True,
                     reads=("NRR", "ones_row"), writes=("pS0",))
            P.tt("dve", ohb[0:64, :], t1[0:64, :], bc, ALU.mult, reads=("NC1", "pS0"), writes=("OHB",))
            qsl = slice(qcb * 1024, (qcb + 1) * 1024)
            P.tt("dve", ohb[0:64, :], ohb[0:64, :], owh[0:64, qsl], ALU.add,
                 reads=("OHB", "OWH"), writes=("OHB",))
            P.dma("pool", scr["AOs"][h * 64:(h + 1) * 64, qsl], ohb[0:64, :],
                  reads=("OHB",), writes=("AOs",))
    for nm in ["WHB", "WQH", "WKH", "GQH", "GKH", "VAU", "VA64", "OWH", "PWB",
               "PB0", "PB1", "OHB"]:
        A["ARB"].release(nm)
    A["ARF"].release("NCOMB")
    A["ARF"].release("NRR")
    A["ARF"].release("BCOL")
    A["ARF"].release("WST")
    A["ARB"].release("XN")
    P.barrier()
    wout = load_cast_bf(P, A, C, dram["L1_attn_w_out"], D, D, "WoutC")
    boc = A["ARF"].alloc("BOC", 8)
    P.dma("sp", boc[:], dram["L1_attn_b_out"].rearrange("(a p) -> p a", p=128),
          writes=("BOC",), slow=True)
    aostg = A["ARB"].alloc("AOC", ND * 512, keys=[f"AOC{i}" for i in range(ND)])
    zc = A["ARF"].alloc("ZC", ND * 512, keys=[f"ZC{i}" for i in range(ND)])
    rc = A["ARF"].alloc("RCs", ND * 512, keys=[f"RC{i}" for i in range(ND)])
    for tc in range(NCH):
        for dm in range(ND):
            P.dma("sp", aostg[:, dm * 512:(dm + 1) * 512],
                  scr["AOs"][dm * 128:(dm + 1) * 128, tc * 512:(tc + 1) * 512],
                  reads=("AOs",), writes=(f"AOC{dm}",))
        ztiles = []
        for of in range(ND):
            out = C["pS1"][:, 0:512]
            for dm in range(ND):
                P.mm(out, wout[:, dm * D + of * 128:dm * D + of * 128 + 128],
                     aostg[:, dm * 512:(dm + 1) * 512],
                     start=(dm == 0), stop=(dm == ND - 1), reads=("WoutC", f"AOC{dm}"),
                     writes=("pS1",))
            zv = zc[:, of * 512:(of + 1) * 512]
            P.act(zv, out, AF.Identity, bias=boc[:, of:of + 1], reads=("pS1", "BOC"),
                  writes=(f"ZC{of}",))
            ztiles.append((zv, f"ZC{of}"))
        sandwich_update(P, A, C, ztiles, rin, rout, tc, rc)
    A["ARB"].release("WoutC")
    A["ARB"].release("AOC")
    A["ARF"].release("ZC")
    A["ARF"].release("RCs")
    A["ARF"].release("BOC")
    P.barrier()

def emit_stage_d(P, A, C, dram, rin, rout, scr):
    nc = P.nc
    E, F = 8, 4096
    NF = F // 128
    cts = P.es.enter_context(nc.sbuf_tensor("d_cts", [128, 64], F32))
    P.dma("sp", cts[:], dram["L1_ffn_centT"],
          writes=("CTS",))
    gtc = P.es.enter_context(nc.sbuf_tensor("d_gtc", [128, 64], F32))
    P.dma("sp", gtc[:], dram["L1_ffn_gammaT"],
          writes=("GTC",))
    btc = P.es.enter_context(nc.sbuf_tensor("d_btc", [128, 64], F32))
    P.dma("sp", btc[:], dram["L1_ffn_betaT"],
          writes=("BTC",))
    for e in range(E):
        stg = A["ARF"].alloc("wts", 4096)
        stgb = A["ARB"].alloc("wtsb", 4096)
        for r in range(ND):
            P.dma("sp", stg, dram["L1_ffn_W1T"][e][r * 128:(r + 1) * 128, :], writes=("wts",))
            P.copy("pool", stgb, stg, reads=("wts",), writes=("wtsb",))
            P.dma("pool", scr["W1Tbf"][e][:, r].rearrange("f p c -> p f c"),
                  stgb.rearrange("p (f c) -> p f c", c=128), reads=("wtsb",), writes=("W1Tbf",))
        A["ARF"].release("wts")
        A["ARB"].release("wtsb")
        stg = A["ARF"].alloc("wts", 1024)
        stgb = A["ARB"].alloc("wtsb", 1024)
        for r in range(NF):
            P.dma("sp", stg, dram["L1_ffn_W2T"][e][r * 128:(r + 1) * 128, :], writes=("wts",))
            P.copy("pool", stgb, stg, reads=("wts",), writes=("wtsb",))
            P.dma("pool", scr["W2Tbf"][e][:, r].rearrange("o p c -> p o c"),
                  stgb.rearrange("p (o c) -> p o c", c=128), reads=("wtsb",), writes=("W2Tbf",))
        A["ARF"].release("wts")
        A["ARB"].release("wtsb")
    b1r = A["ARB"].alloc("B1R", 4096)
    b2r = A["ARB"].alloc("B2R", 1024)
    xhc = A["ARB"].alloc("XHC", ND * 512)
    fcb = A["ARB"].alloc("FCB", ND * 512)
    zaf = A["ARB"].alloc("ZAF", ND * 512)
    ych = A["ARB"].alloc("YCH", ND * 512)
    rlc = A["ARB"].alloc("RLC", NF * 512)
    mfm = A["ARB"].alloc("MFM", 512)
    mrw = A["ARB"].alloc("MRW", E * 512)
    w1r = [A["ARB"].alloc(f"W1R{i}", 8 * 128) for i in range(2)]
    w2r = [A["ARB"].alloc(f"W2R{i}", 32 * 128) for i in range(2)]
    alc = A["ARF"].alloc("ALC", 4)
    alt4 = A["ARF"].alloc("ALT4", 128)
    alr = A["ARF"].alloc("ALR", 512)
    s2r = A["ARF"].alloc("S2R", 512)
    mu2r = A["ARF"].alloc("MU2R", 512)
    mkb = A["ARF"].alloc("MKB", 512)
    mxv = A["ARF"].alloc("MXV", 1)
    mtm = A["ARF"].alloc("MTM", 8)
    rc = A["ARF"].alloc("RCs", ND * 512, keys=[f"RC{i}" for i in range(ND)])
    pS0, pS1, pN0 = C["pS0"], C["pS1"], C["pN0"]
    pA, pB = C["pA"], C["pB"]
    for tc in range(NCH):
        fc = A["ARF"].alloc("FC", ND * 512, keys=[f"FC{i}" for i in range(ND)])
        rtiles = stream_rin_chunk(P, A, C, rin, rc, tc)
        a_row, b_row = ln_stats(P, C, rtiles, D)
        ab, bb = pS0[:, 0:512], pS0[:, 512:1024]
        P.mm(ab, C["ones_row"][0:1, 0:128], a_row, start=True, stop=True,
             reads=("st_a", "ones_row"), writes=("pS0",))
        P.mm(bb, C["ones_row"][0:1, 0:128], b_row, start=True, stop=True,
             reads=("st_b", "ones_row"), writes=("pS0",))
        ftiles = []
        for dm in range(ND):
            rap, rkey = rtiles[dm]
            fv = fc[:, dm * 512:(dm + 1) * 512]
            P.tt("dve", C["lnsq"][:], rap, ab, ALU.mult, reads=(rkey, "pS0"), writes=("lnsq",))
            P.tt("dve", fv, C["lnsq"][:], bb, ALU.add, reads=("lnsq", "pS0"), writes=(f"FC{dm}",))
            P.copy("dve", fcb[:, dm * 512:(dm + 1) * 512], fv, reads=(f"FC{dm}",), writes=("FCB",))
            ftiles.append((fv, f"FC{dm}"))
        for t4 in range(4):
            sc = pS1[0:128, 0:8]
            for dm in range(ND):
                P.mm(sc, fc[:, dm * 512 + t4 * 128:dm * 512 + (t4 + 1) * 128],
                     cts[:, dm * 8:(dm + 1) * 8],
                     start=(dm == 0), stop=(dm == ND - 1), reads=(f"FC{dm}", "CTS"),
                     writes=("pS1",))
            mv = mxv[0:128, 0:1]
            P.emit("dve", lambda E_, mv=mv, sc=sc: E_.reduce_max(mv, sc, axis=AX.X),
                   reads=("pS1",), writes=("MXV",))
            P.emit("dve", lambda E_, sc=sc, mv=mv: E_.tensor_scalar(
                mtm[0:128, 0:8], sc, mv, None, ALU.is_ge),
                reads=("pS1", "MXV"), writes=("MTM",))
            P.act(alc[0:128, t4:t4 + 1], mv, AF.Sigmoid, reads=("MXV",), writes=("ALC",))
            pt = pS0[0:8, 0:128]
            P.trans(pt, mtm[0:128, 0:8], C["ident"][:], reads=("MTM", "ident"), writes=("pS0",))
            P.copy("dve", mfm[0:8, t4 * 128:(t4 + 1) * 128], pt, reads=("pS0",), writes=("MFM",))
        pt = pS0[0:4, 128:256]
        P.trans(pt, alc[0:128, 0:4], C["ident"][:], reads=("ALC", "ident"), writes=("pS0",))
        P.copy("dve", alt4[0:4, 0:128], pt, reads=("pS0",), writes=("ALT4",))
        for t4 in range(4):
            P.dma("pool", alr[0:1, t4 * 128:(t4 + 1) * 128], alt4[t4:t4 + 1, 0:128],
                  reads=("ALT4",), writes=("ALR",))
        for e in range(E):
            P.dma("pool", mrw[0:1, e * 512:(e + 1) * 512], mfm[e:e + 1, :],
                  reads=("MFM",), writes=("MRW",))
        a2, b2 = ln_stats(P, C, ftiles, D)
        P.emit("dve", lambda E_: E_.reciprocal(s2r[0:1, :], a2), reads=("st_a",), writes=("S2R",))
        P.tt("dve", mu2r[0:1, :], b2, s2r[0:1, :], ALU.mult, reads=("st_b", "S2R"), writes=("MU2R",))
        P.ts("dve", mu2r[0:1, :], mu2r[0:1, :], -1.0, None, ALU.mult, reads=("MU2R",), writes=("MU2R",))
        ab2, bb2 = pS0[:, 0:512], pS0[:, 512:1024]
        P.mm(ab2, C["ones_row"][0:1, 0:128], a2, start=True, stop=True,
             reads=("st_a", "ones_row"), writes=("pS0",))
        P.mm(bb2, C["ones_row"][0:1, 0:128], b2, start=True, stop=True,
             reads=("st_b", "ones_row"), writes=("pS0",))
        for dm in range(ND):
            fv = fc[:, dm * 512:(dm + 1) * 512]
            P.tt("dve", C["lnsq"][:], fv, ab2, ALU.mult, reads=(f"FC{dm}", "pS0"), writes=("lnsq",))
            P.tt("dve", xhc[:, dm * 512:(dm + 1) * 512], C["lnsq"][:], bb2, ALU.add,
                 reads=("lnsq", "pS0"), writes=("XHC",))
        A["ARF"].release("FC")
        P.memset("pool", ych[:], 0.0, writes=("YCH",))
        for e in range(E):
            b1s = A["ARF"].alloc("B1S", 4096)
            P.dma("sp", b1s[0:1, :], dram["L1_ffn_b1"][e].unsqueeze(0), writes=("B1S",))
            P.copy("pool", b1r[0:1, :], b1s[0:1, :], reads=("B1S",), writes=("B1R",))
            P.dma("sp", b1s[0:1, 0:1024], dram["L1_ffn_b2"][e].unsqueeze(0), writes=("B1S",))
            P.copy("pool", b2r[0:1, :], b1s[0:1, 0:1024], reads=("B1S",), writes=("B2R",))
            A["ARF"].release("B1S")
            mb = pS1[:, 0:512]
            P.mm(mb, C["ones_row_bf"][0:1, 0:128], mrw[0:1, e * 512:(e + 1) * 512],
                 start=True, stop=True, reads=("ones_row_bf", "MRW"), writes=("pS1",))
            P.copy("dve", mkb[:], mb, reads=("pS1",), writes=("MKB",))
            for dm in range(ND):
                P.emit("dve", lambda E_, o=C["lnsq"][:], i=xhc[:, dm * 512:(dm + 1) * 512],
                       g=gtc[:, dm * 8 + e:dm * 8 + e + 1], b=btc[:, dm * 8 + e:dm * 8 + e + 1]:
                       E_.tensor_scalar(o, i, g, b, ALU.mult, ALU.add),
                       reads=("XHC", "GTC", "BTC"), writes=("lnsq",))
                P.tt("dve", zaf[:, dm * 512:(dm + 1) * 512], C["lnsq"][:], mkb[:], ALU.mult,
                     reads=("lnsq", "MKB"), writes=("ZAF",))
            for ft in range(NF):
                wr, wrk = w1r[ft % 2], f"W1R{ft % 2}"
                P.dma("sp", wr.rearrange("p (d c) -> p d c", c=128),
                      scr["W1Tbf"][e][ft].rearrange("d p c -> p d c"),
                      reads=("W1Tbf",), writes=(wrk,))
                hp, hpk = (pA[:], "pA") if ft % 2 == 0 else (pB[:], "pB")
                for dm in range(ND):
                    P.mm(hp, wr[:, dm * 128:(dm + 1) * 128], zaf[:, dm * 512:(dm + 1) * 512],
                         start=(dm == 0), stop=False, reads=(wrk, "ZAF"), writes=(hpk,))
                P.mm(hp, b1r[0:1, ft * 128:(ft + 1) * 128], mrw[0:1, e * 512:(e + 1) * 512],
                     start=False, stop=True, reads=("B1R", "MRW"), writes=(hpk,))
                P.act(rlc[:, ft * 512:(ft + 1) * 512], hp, AF.Relu, reads=(hpk,), writes=("RLC",))
            for of in range(ND):
                wr2, wrk2 = w2r[of % 2], f"W2R{of % 2}"
                P.dma("sp", wr2.rearrange("p (f c) -> p f c", c=128),
                      scr["W2Tbf"][e][of].rearrange("f p c -> p f c"),
                      reads=("W2Tbf",), writes=(wrk2,))
                yp, ypk = (pS0[:, 0:512], "pS0") if of % 2 == 0 else (pS1[:, 0:512], "pS1")
                for ft in range(NF):
                    P.mm(yp, wr2[:, ft * 128:(ft + 1) * 128], rlc[:, ft * 512:(ft + 1) * 512],
                         start=(ft == 0), stop=False, reads=(wrk2, "RLC"), writes=(ypk,))
                P.mm(yp, b2r[0:1, of * 128:(of + 1) * 128], mrw[0:1, e * 512:(e + 1) * 512],
                     start=False, stop=True, reads=("B2R", "MRW"), writes=(ypk,))
                ysl = ych[:, of * 512:(of + 1) * 512]
                P.tt("dve", ysl, ysl, yp, ALU.add, reads=("YCH", ypk), writes=("YCH",))
        zc = A["ARF"].alloc("ZC", ND * 512, keys=[f"ZC{i}" for i in range(ND)])
        s2b, mu2b = pS0[:, 0:512], pS0[:, 512:1024]
        alb = pS1[:, 0:512]
        P.mm(s2b, C["ones_row"][0:1, 0:128], s2r[0:1, :], start=True, stop=True,
             reads=("S2R", "ones_row"), writes=("pS0",))
        P.mm(mu2b, C["ones_row"][0:1, 0:128], mu2r[0:1, :], start=True, stop=True,
             reads=("MU2R", "ones_row"), writes=("pS0",))
        P.mm(alb, C["ones_row"][0:1, 0:128], alr[0:1, :], start=True, stop=True,
             reads=("ALR", "ones_row"), writes=("pS1",))
        ztiles = []
        for dm in range(ND):
            zv = zc[:, dm * 512:(dm + 1) * 512]
            P.tt("dve", C["lnsq"][:], xhc[:, dm * 512:(dm + 1) * 512], s2b, ALU.mult,
                 reads=("XHC", "pS0"), writes=("lnsq",))
            P.tt("dve", C["lnsq"][:], C["lnsq"][:], mu2b, ALU.add, reads=("lnsq", "pS0"),
                 writes=("lnsq",))
            P.tt("dve", zv, ych[:, dm * 512:(dm + 1) * 512], alb, ALU.mult,
                 reads=("YCH", "pS1"), writes=(f"ZC{dm}",))
            P.tt("dve", zv, zv, C["lnsq"][:], ALU.add, reads=(f"ZC{dm}", "lnsq"),
                 writes=(f"ZC{dm}",))
            ztiles.append((zv, f"ZC{dm}"))
        sandwich_update(P, A, C, ztiles, rin, rout, tc, rc)
        A["ARF"].release("ZC")
    for nm in ["B1R", "B2R", "XHC", "FCB", "ZAF", "YCH", "RLC", "MFM", "MRW",
               "W1R0", "W1R1", "W2R0", "W2R1"]:
        A["ARB"].release(nm)
    for nm in ["ALC", "ALT4", "ALR", "S2R", "MU2R", "MKB", "MXV", "MTM", "RCs"]:
        A["ARF"].release(nm)
    P.barrier()


CONSTS = ["ident", "ones_row", "ones_row_bf", "ones_col", "ones_col_bf"]
A_CONSTS = ["decay"]
A_WEIGHTS = ["L0_attn_Wq", "L0_attn_Wk", "L0_attn_Wv", "L0_attn_Wo",
             "L0_attn_W_down_kv", "L0_attn_W_up_k", "L0_attn_W_up_v"]

WSHAPES = {
    "L0_attn_Wq": (D, D), "L0_attn_Wk": (D, D), "L0_attn_Wv": (D, D),
    "L0_attn_Wo": (D, D), "L0_attn_W_down_kv": (D, 256),
    "L0_attn_W_up_k": (256, D), "L0_attn_W_up_v": (256, D),
    "L0_ffn_wqkv": (2, D, 1536), "L0_ffn_q_scale": (2, 64), "L0_ffn_k_scale": (2, 64),
    "L0_ffn_wo": (2, 512, D), "L0_ffn_ln_g": (2, D), "L0_ffn_ln_b": (2, D),
    "L0_ffn_w1": (2, D, 5460), "L0_ffn_b1": (2, 5460),
    "L0_ffn_w2": (2, 2730, D), "L0_ffn_b2": (2, D),
    "L1_attn_w_v": (D, D), "L1_attn_b_v": (D,),
    "L1_attn_w_wqk": (D, 2048), "L1_attn_b_wqk": (2048,),
    "L1_attn_w_qk": (D, 2048), "L1_attn_b_qk": (2048,),
    "L1_attn_w_out": (D, D), "L1_attn_b_out": (D,),
    "L1_attn_w_pos_rev": (64, 1), "L1_attn_pos_rev": (128, 32),
    "L1_ffn_centT": (128, 64), "L1_ffn_gammaT": (128, 64), "L1_ffn_betaT": (128, 64),
    "L1_ffn_W1T": (8, D, 4096), "L1_ffn_b1": (8, 4096),
    "L1_ffn_W2T": (8, 4096, D), "L1_ffn_b2": (8, D),
    "L1_ffn_g8": (8, D), "L1_ffn_b8": (8, D),
    "L1_ffn_b1c": (128, 8 * 32), "L1_ffn_b2c": (128, 8 * 8),
    "iotacol": (128, 1),
}

CSHAPES = {
    "ident": (128, 128), "ones_row": (1, 512), "ones_row_bf": (1, 512),
    "ones_col": (128, 1), "ones_col_bf": (128, 1),
    "decay": (128, 8192),
    "rot_cq": (64, T), "rot_sq": (64, T), "rot_ck": (64, T), "rot_sk": (64, T),
    "rotmatT": (64, 64), "maskA": (128, 128), "maskB": (128, 128),
}
CBF16 = {"ones_row_bf", "ones_col_bf", "rot_cq", "rot_sq", "rot_ck", "rot_sk", "rotmatT"}


def build_stage_program(stage, with_out=False):
    nc = bass.Bass("TRN2", target_bir_lowering=False)
    es = contextlib.ExitStack()
    dram = {}
    needed_c = list(CONSTS)
    needed_w = []
    if "a" in stage:
        needed_c += A_CONSTS
        needed_w += A_WEIGHTS
    if "b" in stage:
        needed_c += ["rot_cq", "rot_sq", "rot_ck", "rot_sk", "rotmatT", "maskA", "maskB"]
        needed_w += [k for k in WSHAPES if k.startswith("L0_ffn")]
    if "c" in stage:
        needed_w += [k for k in WSHAPES if k.startswith("L1_attn")]
    if "d" in stage:
        needed_w += [k for k in WSHAPES if k.startswith("L1_ffn")] + ["iotacol"]
    for k in needed_c:
        dram[k] = nc.dram_tensor("K_" + k, list(CSHAPES[k]),
                                 BF16 if k in CBF16 else F32, kind="ExternalInput").ap()
    for k in needed_w:
        dram[k] = nc.dram_tensor(k, list(WSHAPES[k]), F32, kind="ExternalInput").ap()
    rin = nc.dram_tensor("rin", [D, T], F32, kind="ExternalInput").ap()
    last = stage[-1]
    scr = {}
    scr["Qs"] = nc.dram_tensor("sQs", [D, T], BF16, kind="Internal").ap()
    scr["AOs"] = nc.dram_tensor("sAOs", [D, T], BF16, kind="Internal").ap()
    scr["XH"] = nc.dram_tensor("sXH", [D, T], BF16, kind="Internal").ap()
    scr["w1bf"] = nc.dram_tensor("sw1bf", [2, 22, 2, 8, 128, 128], BF16, kind="Internal").ap()
    scr["w2bf"] = nc.dram_tensor("sw2bf", [2, 8, 22, 128, 128], BF16, kind="Internal").ap()
    scr["W1Tbf"] = nc.dram_tensor("sW1Tbf", [8, 32, 8, 128, 128], BF16, kind="Internal").ap()
    scr["W2Tbf"] = nc.dram_tensor("sW2Tbf", [8, 8, 32, 128, 128], BF16, kind="Internal").ap()
    scr["zperm"] = nc.dram_tensor("szperm", [6144, D], BF16, kind="Internal").ap()
    scr["yperm"] = nc.dram_tensor("syperm", [6144, D], BF16, kind="Internal").ap()
    rmids = {}
    for s in stage[:-1]:
        rmids[s] = nc.dram_tensor("rmid_" + s, [D, T], F32, kind="Internal").ap()
    rout = nc.dram_tensor("rout", [D, T], F32, kind="ExternalOutput").ap()

    P, A, C = prog_setup(nc, es, dram)
    cur = rin
    emitters = {"a": emit_stage_a, "b": emit_stage_b, "c": emit_stage_c, "d": emit_stage_d}
    for s in stage:
        dst = rout if s == last else rmids[s]
        emitters[s](P, A, C, dram, cur, dst, scr)
        cur = dst
        P.barrier()
        for ar in (A["ARB"], A["ARF"]):
            used = {t: (o, n) for t, (o, n, _) in ar.live.items()}
            assert not used, f"stage {s}: arena {ar.name} leaked {used}"
    import sys as _sys
    print("[build] stage", stage, "instrs:",
          {e: len(P.ops[e]) for e in P.ops}, file=_sys.stderr)
    P.finalize()
    es.close()
    names = needed_c + needed_w
    return nc, names


_PROG_CACHE = {}

def _get_prog(stage):
    if stage not in _PROG_CACHE:
        _PROG_CACHE[stage] = build_stage_program(stage)
    return _PROG_CACHE[stage]


def host_inputs(inputs):
    h = dict(host_constants())
    h.update(rotary_tables())
    for k in CBF16:
        if h.get(k) is not None and h[k].dtype != BF:
            h[k] = h[k].astype(BF)
    w = {}
    for k in ["L0_attn_Wq", "L0_attn_Wk", "L0_attn_Wv", "L0_attn_Wo",
              "L0_attn_W_down_kv", "L0_attn_W_up_k", "L0_attn_W_up_v",
              "L0_ffn_wqkv", "L0_ffn_q_scale", "L0_ffn_k_scale", "L0_ffn_wo",
              "L0_ffn_ln_g", "L0_ffn_ln_b", "L0_ffn_w1", "L0_ffn_b1",
              "L0_ffn_w2", "L0_ffn_b2",
              "L1_attn_w_v", "L1_attn_b_v", "L1_attn_w_wqk", "L1_attn_b_wqk",
              "L1_attn_w_qk", "L1_attn_b_qk", "L1_attn_w_out", "L1_attn_b_out",
              "L1_ffn_b1", "L1_ffn_b2"]:
        w[k] = np.ascontiguousarray(inputs[k])
    w["L1_attn_w_pos_rev"] = np.ascontiguousarray(
        inputs["L1_attn_w_pos_embedding"][63::-1][:, None])
    pr = inputs["L1_attn_pos_embedding"][::-1]
    gbv = np.concatenate([pr[:1], pr])[:4096]
    w["L1_attn_pos_rev"] = np.ascontiguousarray(gbv.reshape(32, 128).T)
    def _col64(a):
        return np.ascontiguousarray(a.T.reshape(8, 128, 8).transpose(1, 0, 2).reshape(128, 64))
    w["L1_ffn_centT"] = _col64(inputs["L1_ffn_centroids"])
    w["L1_ffn_gammaT"] = _col64(inputs["L1_ffn_gamma"])
    w["L1_ffn_betaT"] = _col64(inputs["L1_ffn_beta"])
    w["L1_ffn_W1T"] = np.ascontiguousarray(inputs["L1_ffn_W1"].transpose(0, 2, 1))
    w["L1_ffn_W2T"] = np.ascontiguousarray(inputs["L1_ffn_W2"].transpose(0, 2, 1))
    w["L1_ffn_g8"] = np.ascontiguousarray(inputs["L1_ffn_gamma"])
    w["L1_ffn_b8"] = np.ascontiguousarray(inputs["L1_ffn_beta"])
    w["L1_ffn_b1c"] = np.ascontiguousarray(
        inputs["L1_ffn_b1"].reshape(8, 32, 128).transpose(2, 0, 1).reshape(128, 256))
    w["L1_ffn_b2c"] = np.ascontiguousarray(
        inputs["L1_ffn_b2"].reshape(8, 8, 128).transpose(2, 0, 1).reshape(128, 64))
    w["iotacol"] = np.arange(128, dtype=np.float32)[:, None]
    h.update(w)
    return h


def run_stages(stages_list, inputs):
    hi = host_inputs(inputs)
    x = inputs["L0_attn_x"]
    B = x.shape[0]
    r_cur = [np.ascontiguousarray(x[b].T) for b in range(B)]
    for stage in stages_list:
        nc, names = _get_prog(stage)
        base = {("K_" + k if k in CSHAPES else k): hi[k] for k in names}
        in_maps = []
        for b in range(B):
            m = dict(base)
            m["rin"] = r_cur[b]
            in_maps.append(m)
        res = run_bass_kernel_spmd(nc, in_maps, core_ids=list(range(B)))
        r_cur = [np.asarray(res.results[b]["rout"]) for b in range(B)]
    return np.stack([r.T for r in r_cur], axis=0)


STAGES_MODE = ["abcd"]

def kernel(**inputs):
    out = run_stages(STAGES_MODE, inputs)
    return out.astype(np.float32)
